# version 10
# baseline (speedup 1.0000x reference)
import contextlib
import numpy as np

import concourse.bass as bass
import concourse.mybir as mybir
from concourse.bass_utils import run_bass_kernel_spmd

F32 = mybir.dt.float32
BF16 = mybir.dt.bfloat16
AF = mybir.ActivationFunctionType
ALU = mybir.AluOpType
AX = mybir.AxisListType

NCORES = 8
T = 2048
D = 1024
DIN = 2048
NST = 16
DTR = 64
HID = 4096
NH, DH = 16, 64
KWIN, DIL = 8, 4
HLO = KWIN * DIL
NEXP = 8
EPS_LN = 1e-5
EPS_ATT = 1e-6

ENGINES = ("sync", "gpsimd", "scalar", "vector", "tensor")
DBG_D1 = "all"
DBG_QK = 5


class Prog:

    def __init__(self, nc):
        self.nc = nc
        self.ctx = contextlib.ExitStack()
        self._scopes = [self.ctx]
        self._prefix = [""]
        self.q = {e: [] for e in ENGINES}
        self.pg = {}
        self.pgn = {e: 0 for e in ENGINES}
        for e in ENGINES:
            self.pg[e] = self.ctx.enter_context(nc.semaphore(f"pg_{e}"))
        self._cvals = {}
        self._cbuf = self.ctx.enter_context(
            nc.sbuf_tensor("consts", (128, 16), F32))
        self.pgn["vector"] += 16
        self.streams = []

    @contextlib.contextmanager
    def scope(self, name):
        es = contextlib.ExitStack()
        self._scopes.append(es)
        self._prefix.append(name + "_")
        try:
            yield
        finally:
            self._prefix.pop()
            self._scopes.pop()
            es.close()

    def barrier(self):
        targets = [(self.pg[e2], self.pgn[e2]) for e2 in ENGINES
                   if self.pgn[e2] > 0]
        targets += [(s.h, s.n) for s in self.streams if s.n > 0]
        for e in ENGINES:
            for h, v in targets:
                self.q[e].append(lambda en, s=h, v=v: en.wait_ge(s, v))

    def sbuf(self, name, shape, dtype):
        return self._scopes[-1].enter_context(
            self.nc.sbuf_tensor(self._prefix[-1] + name, list(shape), dtype))

    def psum(self, name, shape, dtype=F32):
        return self._scopes[-1].enter_context(
            self.nc.psum_tensor(self._prefix[-1] + name, list(shape), dtype))

    def cf(self, value, parts=128):
        if value not in self._cvals:
            i = len(self._cvals)
            assert i < 16
            self._cvals[value] = i
        return self._cbuf[0:parts, self._cvals[value]:self._cvals[value] + 1]

    def op(self, eng, fn, step=1):
        self.pgn[eng] += step
        h = self.pg[eng]
        def run(e, fn=fn, h=h, step=step):
            fn(e).then_inc(h, step)
        self.q[eng].append(run)
        return self.pgn[eng]

    class _Stream:
        __slots__ = ("h", "n")

    def stream(self, name):
        s = Prog._Stream()
        s.h = self.ctx.enter_context(self.nc.semaphore(name))
        s.n = 0
        self.streams.append(s)
        return s

    def dma(self, eng, out, in_, stream):
        stream.n += 16
        def run(e, o=out, i=in_, h=stream.h):
            e.dma_start(out=o, in_=i).then_inc(h, 16)
        self.q[eng].append(run)
        return stream.n

    def swait(self, eng, stream, val=None):
        v = stream.n if val is None else val
        if v > 0:
            self.q[eng].append(lambda e, s=stream.h, v=v: e.wait_ge(s, v))

    def wait(self, eng, src_eng, val):
        if val <= 0:
            return
        self.q[eng].append(lambda e, s=self.pg[src_eng], v=val: e.wait_ge(s, v))

    def finish(self):
        if True:
            items = dict(self._cvals)
            cbuf = self._cbuf
            h = self.pg["vector"]
            def memsets(e, items=items, cbuf=cbuf, h=h):
                for v, i in items.items():
                    e.memset(cbuf[:, i:i + 1], float(v)).then_inc(h, 1)
                for _ in range(16 - len(items)):
                    e.nop().then_inc(h, 1)
            self.q["vector"].insert(0, memsets)
            for eng in ENGINES:
                if eng != "vector":
                    self.q[eng].insert(0, lambda e, s=h: e.wait_ge(s, 16))
        finals = [(self.pg[e], self.pgn[e]) for e in ENGINES if self.pgn[e] > 0]
        finals += [(s.h, s.n) for s in self.streams if s.n > 0]
        def final_waits(e, finals=finals):
            for s, v in finals:
                e.wait_ge(s, v)
        self.q["sync"].append(final_waits)
        with self.ctx:
            with self.nc.Block() as block:
                for e in ENGINES:
                    fns = self.q[e]
                    def runner(eng, fns=fns):
                        for fn in fns:
                            fn(eng)
                    getattr(block, e)(runner)
        return self.nc


class PsumPool:

    def __init__(self, P, name, n, shape=(128, 1024)):
        self.P = P
        self.n = n
        self.tiles = [P.psum(f"{name}{i}", shape) for i in range(n)]
        self.rel = [None] * n
        self.uses = 0

    def acquire(self):
        i = self.uses % self.n
        self.uses += 1
        if self.rel[i] is not None:
            eng, cnt = self.rel[i]
            self.P.wait("tensor", eng, cnt)
            self.rel[i] = None
        return i, self.tiles[i]

    def release(self, slot, eng, cnt):
        self.rel[slot] = (eng, cnt)


def cdiv(a, b):
    return (a + b - 1) // b


def blocks_of(total, w):
    return [(i, min(w, total - i)) for i in range(0, total, w)]


def ln_stream(P, *, xdram, nd, tcols, out_bf, xblk, sq, stats, ones1, onesc,
              pool, ld_eng="sync", guard0=0, lds=None):
    ndim = nd * 128
    apply_cnt = guard0
    for bi, (t0, nw) in enumerate(blocks_of(tcols, 512)):
        P.wait(ld_eng, "vector", apply_cnt)
        ldv = P.dma(ld_eng, xblk[:, :, 0:nw], xdram[:, :, t0:t0 + nw], lds)
        s1, tile = pool.acquire()
        P.swait("scalar", lds, ldv)
        for d in range(nd):
            sq_c = P.op("scalar",
                        lambda e, o=sq[:, d, 0:nw], i=xblk[:, d, 0:nw]:
                        e.activation(o, i, AF.Square))
        P.swait("tensor", lds, ldv)
        for d in range(nd):
            P.op("tensor",
                 lambda e, o=tile[0:1, 0:nw], l=ones1, r=xblk[:, d, 0:nw],
                        s=(d == 0), p=(d == nd - 1):
                 e.matmul(o, l, r, start=s, stop=p))
        P.wait("tensor", "scalar", sq_c)
        for d in range(nd):
            mm_c = P.op("tensor",
                        lambda e, o=tile[0:1, 512:512 + nw], l=ones1,
                               r=sq[:, d, 0:nw], s=(d == 0), p=(d == nd - 1):
                        e.matmul(o, l, r, start=s, stop=p))
        P.wait("vector", "tensor", mm_c)
        c1 = P.op("vector", lambda e, o=stats(3, nw), i=tile[0:1, 0:nw]:
                  e.tensor_scalar_mul(o, i, 1.0 / ndim))
        c2 = P.op("vector", lambda e, o=stats(1, nw), i=tile[0:1, 512:512 + nw]:
                  e.tensor_scalar_mul(o, i, 1.0 / ndim))
        pool.release(s1, "vector", c2)
        P.wait("vector", "vector", c1)
        c3 = P.op("vector", lambda e, o=stats(0, nw), a=stats(3, nw):
                  e.tensor_mul(o, a, a))
        P.wait("vector", "vector", c3)
        var_c = P.op("vector", lambda e, o=stats(1, nw), a=stats(1, nw),
                     b=stats(0, nw): e.tensor_sub(o, a, b))
        P.wait("scalar", "vector", var_c)
        c4 = P.op("scalar", lambda e, o=stats(2, nw), i=stats(1, nw),
                  b=P.cf(EPS_LN, 1): e.activation(o, i, AF.Ln, bias=b))
        P.wait("scalar", "scalar", c4)
        rstd_c = P.op("scalar", lambda e, o=stats(2, nw), i=stats(2, nw),
                      sc=P.cf(-0.5, 1): e.activation(o, i, AF.Exp, scale=sc))
        s2, tile2 = pool.acquire()
        P.wait("tensor", "scalar", rstd_c)
        P.wait("tensor", "vector", c1)
        P.op("tensor", lambda e, o=tile2[:, 0:nw], l=onesc, r=stats(3, nw):
             e.matmul(o, l, r, start=True, stop=True))
        bc_c = P.op("tensor", lambda e, o=tile2[:, 512:512 + nw], l=onesc,
                    r=stats(2, nw): e.matmul(o, l, r, start=True, stop=True))
        P.wait("vector", "tensor", bc_c)
        for d in range(nd):
            cs = P.op("vector", lambda e, o=sq[:, d, 0:nw], a=xblk[:, d, 0:nw],
                      b=tile2[:, 0:nw]: e.tensor_sub(o, a, b))
            P.wait("vector", "vector", cs)
            apply_cnt = P.op("vector", lambda e, o=out_bf[:, d, t0:t0 + nw],
                             a=sq[:, d, 0:nw], b=tile2[:, 512:512 + nw]:
                             e.tensor_mul(o, a, b))
        pool.release(s2, "vector", apply_cnt)
    return apply_cnt


def emit_A(P, S, t):
    nc = P.nc
    TE = T + 3
    TS = 512
    NSUB = T // TS
    xfm = t["xfm"]; win = t["win"]; convw = t["convw"]; convb = t["convb"]
    wx = t["wx"]; wdt = t["wdt"]; bdt = t["bdt"]; alog = t["alog"]
    xn_out = t["xn_out"]; res_out = t["res_out"]; xconv_out = t["xconv_out"]
    delta_out = t["delta_out"]; bc_out = t["bc_out"]; fq_out = t["fq_out"]

    xnb = P.sbuf("xnb", (128, 8, T), BF16)
    xnh = P.sbuf("xnh", (128, 8, 4), BF16)
    big1 = P.sbuf("big1", (128, 8192), F32)
    xr = P.sbuf("xr", (128, 16, TS + 3), F32)
    xcdu = P.sbuf("xcdu", (128, 16, TS), F32)
    xcb = P.sbuf("xcb", (128, 16, TS), BF16)
    wstrip = P.sbuf("wstrip", (128, 2, 8, 128), BF16)
    res_st = P.sbuf("res_st", (128, 2, TS), F32)
    da_sb = P.sbuf("da", (128, 2, TS), F32)
    acc_sb = P.sbuf("acc", (128, 2, TS), F32)
    scan_sb = P.sbuf("scan", (128, TS), F32)
    state = P.sbuf("state", (128, 256), F32)
    fq_sb = P.sbuf("fq", (128, 1024), F32)
    stats1 = P.sbuf("stats1", (1, 2048), F32)
    ones1 = P.sbuf("ones1", (128, 1), F32)
    onesb = P.sbuf("onesb", (1, 128), F32)
    cw_sb = P.sbuf("cw", (128, 64), F32)
    cb_sb = P.sbuf("cb", (128, 16), F32)
    wx_sb = P.sbuf("wxb", (128, 16, 96), BF16)
    wdt_sb = P.sbuf("wdtb", (64, DIN), BF16)
    bdt_sb = P.sbuf("bdtb", (128, 16), F32)
    alog_sb = P.sbuf("alogb", (128, 256), F32)
    a_sb = P.sbuf("a", (128, 256), F32)
    drw_sb = P.sbuf("drw", (96, TS), F32)
    drwb = P.sbuf("drwb", (64, TS), BF16)
    sd_sb = P.sbuf("sd", (128, 16), F32)
    sdt_sb = P.sbuf("sdt", (128, 16), F32)

    def stat_row(r, nw):
        return stats1[0:1, r * 512:r * 512 + nw]

    dbua_sb = xcb[:].rearrange("p d w -> p (d w)").bitcast(F32) \
        .rearrange("p (d w) -> p d w", d=8)
    xblk = big1[:].rearrange("p (a d w) -> p a d w", a=2, d=8)[:, 0]
    sqb = big1[:].rearrange("p (a d w) -> p a d w", a=2, d=8)[:, 1]
    bb_sb = big1[:].rearrange("p (d w) -> p d w", d=16)
    dl_sb = xr[:, :, 3:]

    pool = S["pool"]
    sSm = S["SM"]; sGm = S["GM"]; sLn = S["LN"]
    sWinP = [S["G0"], S["G1"]]; sSt = S["ST"]; sBb = S["BB"]

    P.dma("sync", cw_sb[:], convw, sSm)
    P.dma("sync", cb_sb[:], convb, sSm)
    P.dma("sync", bdt_sb[:], bdt, sSm)
    P.dma("sync", alog_sb[:], alog, sSm)
    ld_small = P.dma("sync", alog_sb[:], alog, sSm) if False else sSm.n
    P.dma("gpsimd", wx_sb[:], wx.rearrange("(kt p) o -> p kt o", p=128), sGm)
    gld_small = P.dma("gpsimd", wdt_sb[:], wdt, sGm)

    P.op("vector", lambda e: e.memset(ones1[:], 1.0))
    P.op("vector", lambda e: e.memset(onesb[:], 1.0))
    P.op("vector", lambda e: e.memset(state[:], 0.0))
    init_v = P.op("vector", lambda e: e.memset(sd_sb[:], 0.0))
    P.wait("tensor", "vector", init_v)

    P.swait("scalar", sSm, sSm.n)
    aexp_c = P.op("scalar", lambda e: e.activation(a_sb[:], alog_sb[:], AF.Exp))
    P.wait("vector", "scalar", aexp_c)
    a_ready = P.op("vector", lambda e: e.tensor_scalar_mul(a_sb[:], a_sb[:], -1.0))

    lnm_c = ln_stream(P, xdram=xfm[:, :, 3:], nd=8, tcols=T, out_bf=xnb[:],
              xblk=xblk, sq=sqb, stats=stat_row, ones1=ones1[:], onesc=onesb[:],
              pool=pool, lds=sLn)
    ln_stream(P, xdram=xfm[:, :, 0:3], nd=8, tcols=3, out_bf=xnh[:, :, 0:3],
              xblk=xblk, sq=sqb, stats=stat_row, ones1=ones1[:],
              onesc=onesb[:], pool=pool, guard0=lnm_c, lds=sLn)
    xn_v = P.pgn["vector"]
    P.wait("sync", "vector", xn_v)
    P.dma("sync", xn_out, xnb[:], sSt)

    win_r = win.rearrange("(kt p) o -> p kt o", p=128)

    strip_users = []
    res_sts = []
    scan_cs = []
    nwin = 0
    P.wait("tensor", "vector", xn_v)
    for s in range(NSUB):
        t0 = s * TS
        if s > 0:
            P.swait("vector", sSt)
            P.swait("scalar", sSt)
            P.wait("vector", "scalar", P.pgn["scalar"])
        for ot in range(32):
            is_x = ot < 16
            sbx = nwin % 2
            if nwin >= 2:
                P.wait("gpsimd", "tensor", strip_users[nwin - 2])
            wld = P.dma("gpsimd", wstrip[:, sbx],
                        win_r[:, :, ot * 128:(ot + 1) * 128], sWinP[sbx])
            nwin += 1
            s1, tile = pool.acquire()
            P.swait("tensor", sWinP[sbx], wld)
            for kt in range(8):
                mm_c = P.op("tensor",
                            lambda e, o=tile[:, 0:TS], l=wstrip[:, sbx, kt, :],
                                   r=xnb[:, kt, t0:t0 + TS], s_=(kt == 0), p=(kt == 7):
                            e.matmul(o, l, r, start=s_, stop=p))
            if is_x:
                hsrc = xnh if s == 0 else xnb
                hof = 0 if s == 0 else t0 - 3
                for kt in range(8):
                    mm_c = P.op("tensor",
                                lambda e, o=tile[:, 512:515], l=wstrip[:, sbx, kt, :],
                                       r=hsrc[:, kt, hof:hof + 3], s_=(kt == 0), p=(kt == 7):
                                e.matmul(o, l, r, start=s_, stop=p, skip_group_check=True))
            strip_users.append(mm_c)
            P.wait("vector", "tensor", mm_c)
            if is_x:
                P.op("vector", lambda e, o=xr[:, ot, 3:3 + TS], i=tile[:, 0:TS]:
                     e.tensor_copy(o, i))
                ev_c = P.op("vector", lambda e, o=xr[:, ot, 0:3], i=tile[:, 512:515]:
                            e.tensor_copy(o, i))
            else:
                r = ot - 16
                if len(res_sts) >= 2:
                    P.swait("vector", sSt)
                ev_c = P.op("vector", lambda e, o=res_st[:, r % 2, :], i=tile[:, 0:TS]:
                            e.tensor_copy(o, i))
                P.wait("sync", "vector", ev_c)
                res_sts.append(P.dma("sync", res_out[:, r, s * TS:(s + 1) * TS],
                                     res_st[:, r % 2, :], sSt))
            pool.release(s1, "vector", ev_c)

        P.wait("vector", "vector", P.pgn["vector"])
        if s == 0:
            P.swait("vector", sSm, sSm.n)
        for d in range(16):
            c0 = P.op("vector", lambda e, o=acc_sb[:, 0, :], a=xr[:, d, 0:TS],
                      s1=cw_sb[:, d * 4:d * 4 + 1]: e.tensor_scalar_mul(o, a, s1))
            for j in (1, 2, 3):
                P.wait("vector", "vector", c0)
                c0 = P.op("vector",
                          lambda e, o=acc_sb[:, j % 2, :], a=xr[:, d, j:j + TS],
                                 s1=cw_sb[:, d * 4 + j:d * 4 + j + 1],
                                 b=acc_sb[:, (j - 1) % 2, :]:
                          e.scalar_tensor_tensor(o, a, s1, b, ALU.mult, ALU.add))
            P.wait("vector", "vector", c0)
            conv_c = P.op("vector", lambda e, o=xcdu[:, d, :], i=acc_sb[:, 1, :]:
                          e.tensor_copy(o, i))
        P.wait("scalar", "vector", conv_c)
        for d in range(16):
            silu_c = P.op("scalar", lambda e, o=xcdu[:, d, :], i=xcdu[:, d, :],
                          b=cb_sb[:, d:d + 1]: e.activation(o, i, AF.Silu, bias=b))
        P.wait("vector", "scalar", silu_c)
        for d in range(16):
            xcb_c = P.op("vector", lambda e, o=xcb[:, d, :], i=xcdu[:, d, :]:
                         e.tensor_copy(o, i))
        P.wait("sync", "scalar", silu_c)
        xc_st = P.dma("sync", xconv_out[:, :, s * TS:(s + 1) * TS], xcdu[:], sSt)

        P.wait("tensor", "vector", xcb_c)
        if s == 0:
            P.swait("tensor", sGm, gld_small)
        s1, tile = pool.acquire()
        for kt in range(16):
            mm_c = P.op("tensor", lambda e, o=tile[0:96, 0:TS], l=wx_sb[:, kt, :],
                        r=xcb[:, kt, :], s_=(kt == 0), p=(kt == 15):
                        e.matmul(o, l, r, start=s_, stop=p))
        P.wait("vector", "tensor", mm_c)
        drw_c = P.op("vector", lambda e, o=drw_sb[:, :], i=tile[0:96, 0:TS]:
                     e.tensor_copy(o, i))
        pool.release(s1, "vector", drw_c)
        P.wait("vector", "vector", drw_c)
        drb_c = P.op("vector", lambda e, o=drwb[:], i=drw_sb[0:64, :]:
                     e.tensor_copy(o, i))
        P.wait("sync", "vector", drw_c)
        bc_st = P.dma("sync", bc_out[:, s * TS:(s + 1) * TS], drw_sb[64:96, :], sSt)

        P.wait("tensor", "vector", drb_c)
        dl_c = 0
        for ot in range(16):
            s1, tile = pool.acquire()
            mm_c = P.op("tensor", lambda e, o=tile[:, 0:TS],
                        l=wdt_sb[:, ot * 128:(ot + 1) * 128], r=drwb[:]:
                        e.matmul(o, l, r, start=True, stop=True))
            P.wait("scalar", "tensor", mm_c)
            if dl_c:
                P.wait("scalar", "vector", dl_c)
            c1 = P.op("scalar", lambda e, o=acc_sb[:, 0, :], i=tile[:, 0:TS],
                      b=bdt_sb[:, ot:ot + 1]: e.activation(o, i, AF.Abs, bias=b))
            P.wait("scalar", "scalar", c1)
            c2 = P.op("scalar", lambda e, o=acc_sb[:, 0, :], i=acc_sb[:, 0, :],
                      sc=P.cf(-1.0, 128): e.activation(o, i, AF.Exp, scale=sc))
            P.wait("scalar", "scalar", c2)
            c3 = P.op("scalar", lambda e, o=acc_sb[:, 0, :], i=acc_sb[:, 0, :],
                      b=P.cf(1.0, 128): e.activation(o, i, AF.Ln, bias=b))
            sp_c = P.op("scalar", lambda e, o=acc_sb[:, 1, :], i=tile[:, 0:TS],
                        b=bdt_sb[:, ot:ot + 1]: e.activation(o, i, AF.Relu, bias=b))
            pool.release(s1, "scalar", sp_c)
            P.wait("vector", "scalar", sp_c)
            P.wait("vector", "scalar", c3)
            dl_c = P.op("vector", lambda e, o=dl_sb[:, ot, :], a=acc_sb[:, 0, :],
                        b=acc_sb[:, 1, :]: e.tensor_add(o, a, b))
        P.wait("sync", "vector", dl_c)
        P.dma("sync", delta_out[:, :, s * TS:(s + 1) * TS], dl_sb, sSt)

        P.swait("vector", sSt)
        for d in range(16):
            du_c = P.op("vector", lambda e, o=xcdu[:, d, :], a=dl_sb[:, d, :],
                        b=xcdu[:, d, :]: e.tensor_mul(o, a, b))
        sdt_c = P.op("vector", lambda e, o=sdt_sb[:], i=dl_sb:
                     e.reduce_sum(o, i, axis=AX.X))
        P.wait("vector", "vector", sdt_c)
        sd_c = P.op("vector", lambda e: e.tensor_add(sd_sb[:], sd_sb[:], sdt_sb[:]))

        P.swait("sync", sSt)
        bb_src = bass.AP(bc_out.tensor, s * TS, [[0, 128], [T, 16], [1, TS]])
        bb_ld = P.dma("sync", bb_sb, bb_src, sBb)

        P.wait("scalar", "vector", dl_c)
        if s == 0:
            P.wait("scalar", "vector", a_ready)
        P.swait("vector", sBb, bb_ld)
        P.wait("vector", "vector", du_c)
        P.wait("vector", "tensor", P.pgn["tensor"])
        for d in range(16):
            if d >= 1:
                P.wait("vector", "vector", scan_cs[-1])
            dba_c = P.op("vector", lambda e, o=dbua_sb[:],
                         a=bass.AP(xcdu, d * TS, [[16 * TS, 128], [0, 8], [1, TS]]),
                         b=bb_sb[:, 0:8, :]: e.tensor_mul(o, a, b))
            for n in range(NST):
                ci = d * NST + n
                cg = s * 256 + ci
                buf = cg % 2
                if len(scan_cs) >= 2:
                    P.wait("scalar", "vector", scan_cs[-2])
                da_c = P.op("scalar", lambda e, o=da_sb[:, buf, :], i=dl_sb[:, d, :],
                            sc=a_sb[:, ci:ci + 1]:
                            e.activation(o, i, AF.Exp, scale=sc))
                if n == 8:
                    P.wait("vector", "vector", scan_cs[-1])
                    dba_c = P.op("vector", lambda e, o=dbua_sb[:], d=d,
                                 a=bass.AP(xcdu, d * TS, [[16 * TS, 128], [0, 8], [1, TS]]),
                                 b=bb_sb[:, 8:16, :]: e.tensor_mul(o, a, b))
                P.wait("vector", "scalar", da_c)
                if n in (0, 8):
                    P.wait("vector", "vector", dba_c)
                sc_c = P.op("vector", lambda e, o=scan_sb[:], a=da_sb[:, buf, :],
                            b=dbua_sb[:, n % 8, :], ini=state[:, ci:ci + 1]:
                            e.tensor_tensor_scan(o, a, b, ini, ALU.mult, ALU.add))
                P.wait("vector", "vector", sc_c)
                if s == NSUB - 1:
                    P.op("vector", lambda e, o=fq_sb[:, 512 + ci:513 + ci],
                         i=scan_sb[:, 479:480]: e.tensor_copy(o, i))
                scan_cs.append(P.op("vector", lambda e, o=state[:, ci:ci + 1],
                                    i=scan_sb[:, TS - 1:TS]: e.tensor_copy(o, i)))

    P.wait("vector", "vector", scan_cs[-1])
    P.op("vector", lambda e: e.tensor_copy(fq_sb[:, 0:256], state[:]))
    P.wait("vector", "vector", sd_c)
    t_c = P.op("vector", lambda e: e.reduce_sum(sdt_sb[:], dl_sb[:, :, 480:512],
                                                axis=AX.X))
    P.wait("vector", "vector", t_c)
    s2_c = P.op("vector", lambda e: e.tensor_sub(sdt_sb[:], sd_sb[:], sdt_sb[:]))
    for d in range(16):
        q_c = P.op("vector", lambda e, o=fq_sb[:, 256 + d * 16:256 + (d + 1) * 16],
                   a=a_sb[:, d * 16:(d + 1) * 16], s1=sd_sb[:, d:d + 1]:
                   e.tensor_scalar_mul(o, a, s1))
    P.wait("vector", "vector", s2_c)
    for d in range(16):
        q_c = P.op("vector", lambda e, o=fq_sb[:, 768 + d * 16:768 + (d + 1) * 16],
                   a=a_sb[:, d * 16:(d + 1) * 16], s1=sdt_sb[:, d:d + 1]:
                   e.tensor_scalar_mul(o, a, s1))
    P.wait("scalar", "vector", q_c)
    qe_c = P.op("scalar", lambda e: e.activation(fq_sb[:, 256:512],
                fq_sb[:, 256:512], AF.Exp))
    qe_c = P.op("scalar", lambda e: e.activation(fq_sb[:, 768:1024],
                fq_sb[:, 768:1024], AF.Exp))
    P.wait("sync", "scalar", qe_c)
    P.wait("sync", "vector", P.pgn["vector"])
    P.dma("sync", fq_out, fq_sb[:], sSt)
    return



def mk_streams(P):
    return {"pool": PsumPool(P, "ps", 4),
            "SM": P.stream("sSM"), "LN": P.stream("sLN"), "IO": P.stream("sIO"),
            "BB": P.stream("sBB"), "RES": P.stream("sRES"), "ST": P.stream("sST"),
            "GM": P.stream("sGM"), "G0": P.stream("sG0"), "G1": P.stream("sG1"),
            "BB2": P.stream("sBB2"),
            "CC": P.stream("sCC")}


def _dram_io_A(nc):
    TE = T + 3
    t = {}
    t["xfm"] = nc.dram_tensor("xfm", [128, 8, TE], F32, kind="ExternalInput").ap()
    t["win"] = nc.dram_tensor("win", [1024, 2 * DIN], F32, kind="ExternalInput").ap()
    t["convw"] = nc.dram_tensor("convw", [128, 64], F32, kind="ExternalInput").ap()
    t["convb"] = nc.dram_tensor("convb", [128, 16], F32, kind="ExternalInput").ap()
    t["wx"] = nc.dram_tensor("wx", [DIN, 96], F32, kind="ExternalInput").ap()
    t["wdt"] = nc.dram_tensor("wdt", [DTR, DIN], F32, kind="ExternalInput").ap()
    t["bdt"] = nc.dram_tensor("bdt", [128, 16], F32, kind="ExternalInput").ap()
    t["alog"] = nc.dram_tensor("alog", [128, 256], F32, kind="ExternalInput").ap()
    return t


def build_launchA():
    nc = bass.Bass("TRN2", target_bir_lowering=False)
    P = Prog(nc)
    t = _dram_io_A(nc)
    t["xn_out"] = nc.dram_tensor("xn_out", [128, 8, T], BF16, kind="ExternalOutput").ap()
    t["res_out"] = nc.dram_tensor("res_out", [128, 16, T], F32, kind="ExternalOutput").ap()
    t["xconv_out"] = nc.dram_tensor("xconv_out", [128, 16, T], F32, kind="ExternalOutput").ap()
    t["delta_out"] = nc.dram_tensor("delta_out", [128, 16, T], F32, kind="ExternalOutput").ap()
    t["bc_out"] = nc.dram_tensor("bc_out", [32, T], F32, kind="ExternalOutput").ap()
    t["fq_out"] = nc.dram_tensor("fq_out", [128, 1024], F32, kind="ExternalOutput").ap()
    S = mk_streams(P)
    emit_A(P, S, t)
    return P.finish()

def emit_B2(P, S, t):
    nc = P.nc
    W = [544, 512, 512, 544]
    E = [0, 544, 1056, 1568]
    MB = 544
    u_in = t["u_in"]; dl_in = t["dl_in"]; bc_in = t["bc_in"]; res_in = t["res_in"]
    uex = t["uex"]; dlex = t["dlex"]; resex = t["resex"]; bcex = t["bcex"]
    alog = t["alog"]; dvec = t["dvec"]; wout = t["wout"]; fqa = t["fqa"]
    cmask = t["cmask"]; mam_out = t["mam_out"]

    u_sb = P.sbuf("u", (128, 16, MB), F32)
    dl_sb = P.sbuf("dl", (128, 16, MB), F32)
    du_sb = P.sbuf("du", (128, 16, MB), F32)
    bcb = P.sbuf("bcb", (128, 2, 2, MB), F32)
    y_sb = P.sbuf("y", (128, 16, MB), F32)
    da_sb = P.sbuf("da", (128, 2, MB), F32)
    dbu_sb = P.sbuf("dbu", (128, 16, MB), F32)
    state = P.sbuf("state", (128, 256), F32)
    BCPW = 4 * MB

    def bcb_off(pr, bc):
        return pr * 2 * MB + bc * MB
    alog_sb = P.sbuf("alogb", (128, 256), F32)
    a_sb = P.sbuf("a", (128, 256), F32)
    dv_sb = P.sbuf("dv", (128, 16), F32)
    cm_sb = P.sbuf("cm", (128, 40), F32)
    wo_strip = P.sbuf("wo", (128, 1, 16, 128), BF16)
    _duf = du_sb[:].rearrange("p d w -> p (d w)")
    fq_sb = _duf[:, 0:1024]
    qe_sb = _duf[:, 1024:1280]
    fe_sb = _duf[:, 1280:1536]
    pool = S["pool"]
    sSm = S["SM"]; sFq = S["IO"]; sLd = S["LN"]; sRes = S["RES"]
    sBcP = [S["BB"], S["BB2"]]; sWoP = [S["G0"], S["G0"]]; sSt = S["ST"]

    yb = du_sb[:].rearrange("p d w -> p (d w)")[:, 0:8 * MB].bitcast(BF16) \
        .rearrange("p (d w) -> p d w", d=16)
    mo = du_sb[:, 8:16]

    P.dma("sync", alog_sb[:], alog, sSm)
    P.dma("sync", dv_sb[:], dvec, sSm)
    sm_c = P.dma("sync", cm_sb[:], cmask, sSm)

    P.op("vector", lambda e: e.memset(state[:], 0.0))
    P.swait("scalar", sSm, sm_c)
    aexp_c = P.op("scalar", lambda e: e.activation(a_sb[:], alog_sb[:], AF.Exp))
    P.wait("vector", "scalar", aexp_c)
    a_ready = P.op("vector", lambda e: e.tensor_scalar_mul(a_sb[:], a_sb[:], -1.0))

    P.swait("vector", sSm, sm_c)
    fqv_c = 0
    for j in range(NCORES):
        if j > 0:
            P.wait("sync", "vector", fqv_c)
        fq_c = P.dma("sync", fq_sb, fqa[j], sFq)
        P.swait("vector", sFq, fq_c)
        c1 = P.op("vector", lambda e, j=j: e.tensor_scalar_mul(
            qe_sb, fq_sb[:, 256:512], cm_sb[:, 16 + j:17 + j]))
        P.wait("vector", "vector", c1)
        c2 = P.op("vector", lambda e, j=j: e.scalar_tensor_tensor(
            qe_sb, fq_sb[:, 768:1024], cm_sb[:, 24 + j:25 + j], qe_sb,
            ALU.mult, ALU.add))
        P.wait("vector", "vector", c2)
        c3 = P.op("vector", lambda e, j=j: e.tensor_scalar_add(
            qe_sb, qe_sb, cm_sb[:, 32 + j:33 + j]))
        c4 = P.op("vector", lambda e, j=j: e.tensor_scalar_mul(
            fe_sb, fq_sb[:, 0:256], cm_sb[:, 16 + j:17 + j]))
        P.wait("vector", "vector", c4)
        c5 = P.op("vector", lambda e, j=j: e.scalar_tensor_tensor(
            fe_sb, fq_sb[:, 512:768], cm_sb[:, 24 + j:25 + j], fe_sb,
            ALU.mult, ALU.add))
        P.wait("vector", "vector", c5)
        c6 = P.op("vector", lambda e: e.tensor_mul(state[:], state[:], qe_sb))
        P.wait("vector", "vector", c6)
        fqv_c = P.op("vector", lambda e: e.tensor_add(state[:], state[:], fe_sb))

    scan_cs = []
    y_c = 0
    wo_users = [0, 0]
    stp_l = None
    for s in range(4):
        w = W[s]
        m0 = E[s] - 32
        if s > 0:
            P.swait("sync", sSt)
            P.wait("sync", "vector", P.pgn["vector"])
            P.wait("sync", "scalar", P.pgn["scalar"])
            P.wait("sync", "gpsimd", P.pgn["gpsimd"])
            P.wait("gpsimd", "vector", P.pgn["vector"])
            P.wait("vector", "gpsimd", P.pgn["gpsimd"])
        if s == 0:
            P.dma("sync", u_sb[:, :, 0:32], uex[:, :, 0:32], sLd)
            P.dma("sync", u_sb[:, :, 32:544], u_in[:, :, 0:512], sLd)
            P.dma("sync", dl_sb[:, :, 0:32], dlex[:, :, 0:32], sLd)
            ld_c = P.dma("sync", dl_sb[:, :, 32:544], dl_in[:, :, 0:512], sLd)
        elif s == 3:
            P.dma("sync", u_sb[:, :, 0:512], u_in[:, :, 1536:2048], sLd)
            P.dma("sync", u_sb[:, :, 512:544], uex[:, :, 32:64], sLd)
            P.dma("sync", dl_sb[:, :, 0:512], dl_in[:, :, 1536:2048], sLd)
            ld_c = P.dma("sync", dl_sb[:, :, 512:544], dlex[:, :, 32:64], sLd)
        else:
            P.dma("sync", u_sb[:, :, 0:512], u_in[:, :, m0:m0 + 512], sLd)
            ld_c = P.dma("sync", dl_sb[:, :, 0:512], dl_in[:, :, m0:m0 + 512], sLd)
        def bc_load(n, pr, s=s, m0=m0):
            sBc = sBcP[pr]
            if s == 0:
                P.dma("sync", bcb[:, pr, :, 0:32],
                      bass.AP(bcex.tensor, n * 64, [[0, 128], [16 * 64, 2], [1, 32]]), sBc)
                return P.dma("sync", bcb[:, pr, :, 32:544],
                             bass.AP(bc_in.tensor, n * T, [[0, 128], [16 * T, 2], [1, 512]]), sBc)
            elif s == 3:
                P.dma("sync", bcb[:, pr, :, 0:512],
                      bass.AP(bc_in.tensor, n * T + 1536, [[0, 128], [16 * T, 2], [1, 512]]), sBc)
                return P.dma("sync", bcb[:, pr, :, 512:544],
                             bass.AP(bcex.tensor, n * 64 + 32, [[0, 128], [16 * 64, 2], [1, 32]]), sBc)
            else:
                return P.dma("sync", bcb[:, pr, :, 0:512],
                             bass.AP(bc_in.tensor, n * T + m0, [[0, 128], [16 * T, 2], [1, 512]]), sBc)

        P.swait("vector", sLd, ld_c)
        if s == 0:
            P.wait("vector", "vector", fqv_c)
            stp_l = [fqv_c] * NST
        du_c = P.op("vector", lambda e, o=du_sb[:, :, 0:w], a=dl_sb[:, :, 0:w],
                    b=u_sb[:, :, 0:w]: e.tensor_mul(o, a, b))
        P.wait("vector", "vector", du_c)
        for d in range(16):
            ud_c = P.op("vector", lambda e, o=u_sb[:, d, 0:w], a=u_sb[:, d, 0:w],
                        s1=dv_sb[:, d:d + 1]: e.tensor_scalar_mul(o, a, s1))
        P.swait("scalar", sLd, ld_c)
        if s == 0:
            P.wait("scalar", "vector", fqv_c)

        NIT = 16 * NST
        da_l = [0] * (NIT + 1)
        glob0 = s * NIT

        def issue_da(i):
            n, d = divmod(i, 16)
            ci = d * NST + n
            b = (glob0 + i) % 2
            if glob0 + i >= 2:
                P.wait("scalar", "vector", scan_cs[glob0 + i - 2])
            da_l[i] = P.op("scalar", lambda e, o=da_sb[:, b, 0:w],
                           i_=dl_sb[:, d, 0:w], sc=a_sb[:, ci:ci + 1]:
                           e.activation(o, i_, AF.Exp, scale=sc))

        bcl = [0] * (NST + 1)
        bcl[0] = bc_load(0, 0)
        issue_da(0)
        GD = 7
        zt_l = [0] * NST
        ztv_l = [0] * NST
        y_l = [0] * NST
        yv_l = [0] * NST
        stc_l = [0] * NST
        DPW = 16 * MB
        for n in range(NST):
            pr = n % 2
            if n + 1 < NST:
                if n >= 1:
                    P.wait("sync", "gpsimd", zt_l[n - 1])
                    P.wait("sync", "vector", ztv_l[n - 1])
                bcl[n + 1] = bc_load(n + 1, (n + 1) % 2)
            P.swait("gpsimd", sBcP[pr], bcl[n])
            P.wait("gpsimd", "vector", du_c)
            if n >= 1:
                P.wait("gpsimd", "gpsimd", y_l[n - 1])
                P.wait("gpsimd", "vector", yv_l[n - 1])
            dball = P.op("gpsimd", lambda e, o=dbu_sb[:, :, 0:w],
                         a=du_sb[:, :, 0:w],
                         b_=bass.AP(bcb, bcb_off(pr, 0),
                                    [[BCPW, 128], [0, 16], [1, w]]):
                         e.tensor_mul(o, a, b_))
            P.wait("vector", "gpsimd", dball)
            P.wait("vector", "vector", stp_l[n])
            for d in range(16):
                i = n * 16 + d
                ci = d * NST + n
                if i + 1 < NIT:
                    issue_da(i + 1)
                P.wait("vector", "scalar", da_l[i])
                sc_c = P.op("vector", lambda e, o=dbu_sb[:, d, 0:w],
                            a=da_sb[:, (glob0 + i) % 2, 0:w],
                            b_=dbu_sb[:, d, 0:w], ini=state[:, ci:ci + 1]:
                            e.tensor_tensor_scan(o, a, b_, ini, ALU.mult, ALU.add))
                scan_cs.append(sc_c)
            P.wait("vector", "vector", sc_c)
            stc_l[n] = P.op("vector", lambda e,
                            o=bass.AP(state, n, [[256, 128], [NST, 16]]),
                            i_=bass.AP(dbu_sb, w - 1, [[DPW, 128], [MB, 16]]):
                            e.tensor_copy(o, i_))
            P.wait("gpsimd", "vector", sc_c)
            P.wait("gpsimd", "vector", stc_l[n])
            zt_l[n] = P.op("gpsimd", lambda e, o=dbu_sb[:, 0:GD, 0:w],
                           a=dbu_sb[:, 0:GD, 0:w],
                           b_=bass.AP(bcb, bcb_off(pr, 1),
                                      [[BCPW, 128], [0, GD], [1, w]]):
                           e.tensor_mul(o, a, b_))
            P.wait("vector", "vector", stc_l[n])
            ztv_l[n] = P.op("vector", lambda e, o=dbu_sb[:, GD:16, 0:w],
                            a=dbu_sb[:, GD:16, 0:w],
                            b_=bass.AP(bcb, bcb_off(pr, 1),
                                       [[BCPW, 128], [0, 16 - GD], [1, w]]):
                            e.tensor_mul(o, a, b_))
            P.wait("gpsimd", "gpsimd", zt_l[n])
            P.wait("vector", "vector", ztv_l[n])
            if n == 0:
                P.wait("gpsimd", "vector", ud_c)
                P.wait("vector", "vector", ud_c)
                y_l[0] = P.op("gpsimd", lambda e, o=y_sb[:, 0:GD, 0:w],
                              a=dbu_sb[:, 0:GD, 0:w], b_=u_sb[:, 0:GD, 0:w]:
                              e.tensor_add(o, a, b_))
                yv_l[0] = P.op("vector", lambda e, o=y_sb[:, GD:16, 0:w],
                               a=dbu_sb[:, GD:16, 0:w], b_=u_sb[:, GD:16, 0:w]:
                               e.tensor_add(o, a, b_))
            else:
                y_l[n] = P.op("gpsimd", lambda e, o=y_sb[:, 0:GD, 0:w],
                              a=y_sb[:, 0:GD, 0:w], b_=dbu_sb[:, 0:GD, 0:w]:
                              e.tensor_add(o, a, b_))
                yv_l[n] = P.op("vector", lambda e, o=y_sb[:, GD:16, 0:w],
                               a=y_sb[:, GD:16, 0:w], b_=dbu_sb[:, GD:16, 0:w]:
                               e.tensor_add(o, a, b_))
        y_c = y_l[NST - 1]
        yv_c = yv_l[NST - 1]
        stp_l = stc_l

        P.wait("sync", "gpsimd", y_c)
        P.wait("sync", "vector", yv_c)
        if s == 0:
            P.dma("sync", u_sb[:, :, 0:32], resex[:, :, 0:32], sRes)
            res_c = P.dma("sync", u_sb[:, :, 32:544], res_in[:, :, 0:512], sRes)
        elif s == 3:
            P.dma("sync", u_sb[:, :, 0:512], res_in[:, :, 1536:2048], sRes)
            res_c = P.dma("sync", u_sb[:, :, 512:544], resex[:, :, 32:64], sRes)
        else:
            res_c = P.dma("sync", u_sb[:, :, 0:512], res_in[:, :, m0:m0 + 512], sRes)
        P.swait("scalar", sRes, res_c)
        for d in range(16):
            sil_c = P.op("scalar", lambda e, o=u_sb[:, d, 0:w], i=u_sb[:, d, 0:w]:
                         e.activation(o, i, AF.Silu))
        P.wait("vector", "scalar", sil_c)
        P.wait("vector", "gpsimd", y_c)
        P.wait("vector", "vector", yv_c)
        for d in range(16):
            yb_c = P.op("vector", lambda e, o=yb[:, d, 0:w], a=y_sb[:, d, 0:w],
                        b=u_sb[:, d, 0:w]: e.tensor_mul(o, a, b))

        for ot in range(8):
            sbx = 0
            if wo_users[sbx]:
                P.wait("sync", "tensor", wo_users[sbx])
            wo_c = P.dma("sync", wo_strip[:, sbx], wout[ot], sWoP[sbx])
            s1, tile = pool.acquire()
            P.swait("tensor", sWoP[sbx], wo_c)
            if ot == 0:
                P.wait("tensor", "vector", yb_c)
            for bi, (n0, nw) in enumerate(blocks_of(w, 512)):
                for kt in range(16):
                    mm_c = P.op("tensor", lambda e, o=tile[:, bi * 512:bi * 512 + nw],
                                l=wo_strip[:, sbx, kt, :], r=yb[:, kt, n0:n0 + nw],
                                s_=(kt == 0), p=(kt == 15):
                                e.matmul(o, l, r, start=s_, stop=p))
            wo_users[sbx] = mm_c
            P.wait("scalar", "tensor", mm_c)
            for bi, (n0, nw) in enumerate(blocks_of(w, 512)):
                ev_c = P.op("scalar", lambda e, o=mo[:, ot, n0:n0 + nw],
                            i=tile[:, bi * 512:bi * 512 + nw]: e.copy(o, i))
            pool.release(s1, "scalar", ev_c)
        P.wait("sync", "scalar", ev_c)
        P.dma("sync", mam_out[:, :, E[s]:E[s] + w], mo[:, :, 0:w], sSt)
    return


def build_launchB():
    nc = bass.Bass("TRN2", target_bir_lowering=False)
    P = Prog(nc)
    t = {}
    t["u_in"] = nc.dram_tensor("u_in", [128, 16, T], F32, kind="ExternalInput").ap()
    t["dl_in"] = nc.dram_tensor("dl_in", [128, 16, T], F32, kind="ExternalInput").ap()
    t["bc_in"] = nc.dram_tensor("bc_in", [32, T], F32, kind="ExternalInput").ap()
    t["res_in"] = nc.dram_tensor("res_in", [128, 16, T], F32, kind="ExternalInput").ap()
    t["alog"] = nc.dram_tensor("alog", [128, 256], F32, kind="ExternalInput").ap()
    t["dvec"] = nc.dram_tensor("dvec", [128, 16], F32, kind="ExternalInput").ap()
    t["wout"] = nc.dram_tensor("wout", [DIN, D], F32, kind="ExternalInput").ap()
    t["fqa"] = nc.dram_tensor("fqa", [8, 128, 512], F32, kind="ExternalInput").ap()
    t["cmask"] = nc.dram_tensor("cmask", [128, 16], F32, kind="ExternalInput").ap()
    t["mam_out"] = nc.dram_tensor("mam_out", [128, 8, T], F32, kind="ExternalOutput").ap()
    S = mk_streams(P)
    emit_B(P, S, t)
    return P.finish()

def emit_C(P, S, t):
    nc = P.nc
    X = t["xn_in"].shape[2]
    if X == 2112:
        SUB = [(0, 1056), (1056, 1056)]
    else:
        SUB = [(0, 1024), (1024, 1024)]
    MB = max(w for _, w in SUB)
    xn_in = t["xn_in"]; x_in = t["x_in"]; mam_in = t["mam_in"]
    w1T = t["w1T"]; w2T = t["w2T"]; h1T = t["h1T"]; h2T = t["h2T"]; w3T = t["w3T"]
    w1s = t["w1s"]; w2s = t["w2s"]; h1s = t["h1s"]; h2s = t["h2s"]; w3s = t["w3s"]
    x1_out = t["x1_out"]

    xnb = P.sbuf("xnb", (128, 8, MB), BF16)
    ha = P.sbuf("ha", (128, 32, MB), BF16)
    hb = P.sbuf("hb", (128, 32, MB), BF16)
    wst = P.sbuf("wst", (128, 2, 32, 128), BF16)
    fx = P.sbuf("fx", (128, 8, 544), F32)
    fm = P.sbuf("fm", (128, 8, 544), F32)
    t2_sb = P.sbuf("t2s", (128, 544), F32)
    s1_sb = P.sbuf("s1s", (128, 32), F32)
    s2_sb = P.sbuf("s2s", (128, 32), F32)
    s3_sb = P.sbuf("s3s", (128, 32), F32)
    s4_sb = P.sbuf("s4s", (128, 32), F32)
    s5_sb = P.sbuf("s5s", (128, 8), F32)
    pool = S["pool"]
    sSm = S["SM"]; sXn = S["LN"]; sIo = S["IO"]
    sWP = [S["G0"], S["G1"]]; sSt = S["ST"]

    P.dma("sync", s1_sb[:], w1s, sSm)
    P.dma("sync", s2_sb[:], w2s, sSm)
    P.dma("sync", s3_sb[:], h1s, sSm)
    P.dma("sync", s4_sb[:], h2s, sSm)
    sm_c = P.dma("sync", s5_sb[:], w3s, sSm)

    wu = [0, 0]
    h_c = 0
    ev_c = 0
    for (t0, w) in SUB:
        NT = blocks_of(w, 512)
        if ev_c:
            P.wait("sync", "tensor", P.pgn["tensor"])
        xn_c = P.dma("sync", xnb[:, :, 0:w], xn_in[:, :, t0:t0 + w], sXn)
        first = True
        for ot in range(32):
            pr = ot % 2
            if wu[pr]:
                P.wait("sync", "tensor", wu[pr])
            P.dma("sync", wst[:, pr, 0:8, :], w1T[ot], sWP[pr])
            wl_c = P.dma("sync", wst[:, pr, 8:16, :], w2T[ot], sWP[pr])
            for (n0, nw) in NT:
                s1t, tile = pool.acquire()
                P.swait("tensor", sWP[pr], wl_c)
                if first:
                    P.swait("tensor", sXn, xn_c)
                    first = False
                for kt in range(8):
                    P.op("tensor", lambda e, o=tile[:, 0:nw], l=wst[:, pr, kt, :],
                         r=xnb[:, kt, n0:n0 + nw], s_=(kt == 0), p=(kt == 7):
                         e.matmul(o, l, r, start=s_, stop=p))
                for kt in range(8):
                    mm_c = P.op("tensor", lambda e, o=tile[:, 512:512 + nw],
                                l=wst[:, pr, 8 + kt, :],
                                r=xnb[:, kt, n0:n0 + nw], s_=(kt == 0), p=(kt == 7):
                                e.matmul(o, l, r, start=s_, stop=p))
                wu[pr] = mm_c
                P.wait("scalar", "tensor", mm_c)
                if h_c:
                    P.wait("scalar", "vector", h_c)
                sl_c = P.op("scalar", lambda e, o=t2_sb[:, 0:nw], i=tile[:, 0:nw],
                            sc=s1_sb[:, ot:ot + 1]:
                            e.activation(o, i, AF.Silu, scale=sc))
                P.wait("vector", "tensor", mm_c)
                P.wait("vector", "scalar", sl_c)
                h_c = P.op("vector", lambda e, o=ha[:, ot, n0:n0 + nw], a=t2_sb[:, 0:nw],
                           b=tile[:, 512:512 + nw], s2=s2_sb[:, ot:ot + 1]:
                           e.scalar_tensor_tensor(o, b, s2, a, ALU.mult, ALU.mult))
                pool.release(s1t, "vector", h_c)
        h_was_vec = True

        for stage, (wr, ssc, hin, hout) in enumerate(
                [(h1T, s3_sb, ha, hb), (h2T, s4_sb, hb, ha)]):
            for ot in range(32):
                pr = ot % 2
                if wu[pr]:
                    P.wait("sync", "tensor", wu[pr])
                wl_c = P.dma("sync", wst[:, pr], wr[ot], sWP[pr])
                for (n0, nw) in NT:
                    s1t, tile = pool.acquire()
                    P.swait("tensor", sWP[pr], wl_c)
                    if ot == 0 and n0 == 0:
                        if h_was_vec:
                            P.wait("tensor", "vector", h_c)
                        else:
                            P.wait("tensor", "scalar", h_c)
                    for kt in range(32):
                        mm_c = P.op("tensor", lambda e, o=tile[:, 0:nw],
                                    l=wst[:, pr, kt, :], r=hin[:, kt, n0:n0 + nw],
                                    s_=(kt == 0), p=(kt == 31):
                                    e.matmul(o, l, r, start=s_, stop=p))
                    wu[pr] = mm_c
                    P.wait("scalar", "tensor", mm_c)
                    h_c = P.op("scalar", lambda e, o=hout[:, ot, n0:n0 + nw],
                               i=tile[:, 0:nw], sc=ssc[:, ot:ot + 1]:
                               e.activation(o, i, AF.Gelu, scale=sc))
                    pool.release(s1t, "scalar", h_c)
            h_was_vec = False

        for (n0, nw) in NT:
            P.swait("sync", sSt)
            P.wait("sync", "vector", P.pgn["vector"])
            P.dma("sync", fx[:, :, 0:nw], x_in[:, :, t0 + n0:t0 + n0 + nw], sIo)
            io_c = P.dma("sync", fm[:, :, 0:nw], mam_in[:, :, t0 + n0:t0 + n0 + nw], sIo)
            for ot in range(8):
                pr = ot % 2
                if wu[pr]:
                    P.wait("sync", "tensor", wu[pr])
                wl_c = P.dma("sync", wst[:, pr], w3T[ot], sWP[pr])
                s1t, tile = pool.acquire()
                P.swait("tensor", sWP[pr], wl_c)
                if ot == 0 and n0 == 0:
                    P.wait("tensor", "scalar", h_c)
                for kt in range(32):
                    mm_c = P.op("tensor", lambda e, o=tile[:, 0:nw],
                                l=wst[:, pr, kt, :], r=ha[:, kt, n0:n0 + nw],
                                s_=(kt == 0), p=(kt == 31):
                                e.matmul(o, l, r, start=s_, stop=p))
                wu[pr] = mm_c
                P.wait("vector", "tensor", mm_c)
                if ot == 0:
                    P.swait("vector", sIo, io_c)
                    P.swait("vector", sSm, sm_c)
                c1 = P.op("vector", lambda e, o=fm[:, ot, 0:nw],
                          a=fx[:, ot, 0:nw], b=fm[:, ot, 0:nw]:
                          e.tensor_add(o, a, b))
                P.wait("vector", "vector", c1)
                ev_c = P.op("vector", lambda e, o=fx[:, ot, 0:nw],
                            a=tile[:, 0:nw], s1=s5_sb[:, ot:ot + 1],
                            b=fm[:, ot, 0:nw]:
                            e.scalar_tensor_tensor(o, a, s1, b, ALU.mult, ALU.add))
                pool.release(s1t, "vector", ev_c)
            P.wait("sync", "vector", ev_c)
            P.dma("sync", x1_out[:, :, t0 + n0:t0 + n0 + nw], fx[:, :, 0:nw], sSt)
        h_c = 0
    return


def build_launchC():
    nc = bass.Bass("TRN2", target_bir_lowering=False)
    P = Prog(nc)
    t = {}
    t["xn_in"] = nc.dram_tensor("xn_in", [128, 8, T], BF16, kind="ExternalInput").ap()
    t["x_in"] = nc.dram_tensor("x_in", [128, 8, T], F32, kind="ExternalInput").ap()
    t["mam_in"] = nc.dram_tensor("mam_in", [128, 8, T], F32, kind="ExternalInput").ap()
    t["w1T"] = nc.dram_tensor("w1T", [D, HID], F32, kind="ExternalInput").ap()
    t["w2T"] = nc.dram_tensor("w2T", [D, HID], F32, kind="ExternalInput").ap()
    t["h1T"] = nc.dram_tensor("h1T", [HID, HID], F32, kind="ExternalInput").ap()
    t["h2T"] = nc.dram_tensor("h2T", [HID, HID], F32, kind="ExternalInput").ap()
    t["w3T"] = nc.dram_tensor("w3T", [HID, D], F32, kind="ExternalInput").ap()
    t["w1s"] = nc.dram_tensor("w1s", [128, 32], F32, kind="ExternalInput").ap()
    t["w2s"] = nc.dram_tensor("w2s", [128, 32], F32, kind="ExternalInput").ap()
    t["h1s"] = nc.dram_tensor("h1s", [128, 32], F32, kind="ExternalInput").ap()
    t["h2s"] = nc.dram_tensor("h2s", [128, 32], F32, kind="ExternalInput").ap()
    t["w3s"] = nc.dram_tensor("w3s", [128, 8], F32, kind="ExternalInput").ap()
    t["x1_out"] = nc.dram_tensor("x1_out", [128, 8, T], F32, kind="ExternalOutput").ap()
    S = mk_streams(P)
    emit_C(P, S, t)
    return P.finish()

def emit_D1(P, S, t):
    _dl = ["ln", "qk", "v", "attn", "scale", "all"].index(DBG_D1)
    nc = P.nc
    TEX = T + 2 * HLO
    x1e = t["x1e"]; wqT = t["wqT"]; wkT = t["wkT"]; wvT = t["wvT"]; woT = t["woT"]
    bq = t["bq"]; bk = t["bk"]; bvt = t["bvt"]; bo_ = t["bo_"]
    mask1 = t["mask1"]; mask2 = t["mask2"]; nrm_d = t["nrm_d"]; srow_d = t["srow_d"]
    xn1_out = t["xn1_out"]; at_out = t["at_out"]
    xn1b = P.sbuf("xn1b", (128, 8, TEX), BF16)
    arena = P.sbuf("arena", (128, 8192), F32)
    qb = P.sbuf("qb", (128, 8, T), BF16)
    kb = P.sbuf("kb", (128, 8, TEX), BF16)
    v_tm = P.sbuf("vtm", (128, 17, D), BF16)
    aob = P.sbuf("aob", (128, 8, T), BF16)
    m1_sb = P.sbuf("m1s", (128, 128), F32)
    m2_sb = P.sbuf("m2s", (64, 128), F32)
    e1_sb = P.sbuf("e1s", (128, 128), F32)
    e2_sb = P.sbuf("e2s", (64, 128), F32)
    e1b = P.sbuf("e1b", (128, 128), BF16)
    e2b = P.sbuf("e2b", (64, 128), BF16)
    wst = P.sbuf("wst", (128, 2, 8, 128), BF16)
    bq_sb = P.sbuf("bqs", (128, 8), F32)
    bk_sb = P.sbuf("bks", (128, 8), F32)
    bv_sb = P.sbuf("bvs", (128, D), F32)
    bo_sb = P.sbuf("bos", (128, 8), F32)
    ones1 = P.sbuf("ones1", (128, 1), F32)
    onesb = P.sbuf("onesb", (1, 128), F32)
    ones2 = P.sbuf("ones2", (128, 2), F32)
    onesb1 = P.sbuf("onesb1", (128, 1), BF16)
    pool = S["pool"]
    stats1 = qb[0:1].rearrange("p d w -> p (d w)")[:, 0:4096].bitcast(F32)

    xblk = arena[:].rearrange("p (a d w) -> p a d w", a=2, d=8)[:, 0]
    sqb = arena[:].rearrange("p (a d w) -> p a d w", a=2, d=8)[:, 1]
    rnb = arena[:].rearrange("p (d w) -> p d w", d=16)[:, 0:8]
    att_st = arena[:].rearrange("p (d w) -> p d w", d=16)[:, 8:16]
    nqb = arena[0:2].rearrange("p (x d w) -> p x d w", x=2, d=8)[:, 0]
    sums_st = arena[0:1, 4096:6144]
    q2_sb = arena[:, 6144:6656]

    def stat_row(r, nw):
        return stats1[0:1, r * 512:r * 512 + nw]

    sSm = S["SM"]; sLn = S["LN"]; sW = [S["G0"], S["G1"]]
    sMk = S["IO"]; sNr = S["BB"]; sSt = S["ST"]

    P.dma("sync", bq_sb[:], bq, sSm)
    P.dma("sync", bk_sb[:], bk, sSm)
    P.dma("sync", bv_sb[:], bvt, sSm)
    sm_c = P.dma("sync", bo_sb[:], bo_, sSm)
    iv = P.op("vector", lambda e: e.memset(ones2[:], 0.0))
    P.wait("vector", "vector", iv)
    P.op("vector", lambda e: e.memset(ones2[0:64, 0:1], 1.0))
    P.op("vector", lambda e: e.memset(ones2[64:128, 1:2], 1.0))
    P.op("vector", lambda e: e.memset(ones1[:], 1.0))
    P.op("vector", lambda e: e.memset(onesb[:], 1.0))
    on_c = P.op("vector", lambda e: e.memset(onesb1[:], 1.0))
    P.wait("tensor", "vector", on_c)

    ln_c = ln_stream(P, xdram=x1e, nd=8, tcols=TEX, out_bf=xn1b[:], xblk=xblk,
                     sq=sqb, stats=stat_row, ones1=ones1[:], onesc=onesb[:],
                     pool=pool, lds=sLn)
    P.wait("sync", "vector", ln_c)
    P.dma("sync", xn1_out, xn1b[:, :, HLO:HLO + T], sSt)

    wu = [0, 0]
    P.wait("tensor", "vector", ln_c)
    P.swait("scalar", sSm, sm_c)
    P.swait("vector", sSm, sm_c)

    def qk_phase(nrow, wr, bias_sb, out_bf, tcols, coff):
        nonlocal wu
        nv_c = nm_c = 0
        nz_c = P.pgn["vector"]
        P.wait("vector", "vector", P.pgn["vector"])
        for blk, (t0, nw) in enumerate(blocks_of(tcols, 512)):
            P.swait("vector", sNr)
            for ot in range(8):
                pr = ot % 2
                if wu[pr]:
                    P.wait("sync", "tensor", wu[pr])
                wl_c = P.dma("sync", wst[:, pr], wr[ot], sW[pr])
                s1t, tile = pool.acquire()
                P.swait("tensor", sW[pr], wl_c)
                for kt in range(8):
                    mm_c = P.op("tensor", lambda e, o=tile[:, 0:nw],
                                l=wst[:, pr, kt, :],
                                r=xn1b[:, kt, coff + t0:coff + t0 + nw],
                                s_=(kt == 0), p=(kt == 7):
                                e.matmul(o, l, r, start=s_, stop=p))
                wu[pr] = mm_c
                P.wait("scalar", "tensor", mm_c)
                if DBG_QK >= 1:
                    if nm_c:
                        P.wait("scalar", "tensor", nm_c)
                    sq_c = P.op("scalar", lambda e, o=q2_sb[:, 0:nw], i=tile[:, 0:nw],
                                b=bias_sb[:, ot:ot + 1]:
                                e.activation(o, i, AF.Square, bias=b))
                P.wait("vector", "tensor", mm_c)
                if DBG_QK >= 1:
                    P.wait("vector", "scalar", sq_c)
                ev_c = P.op("vector", lambda e, o=out_bf[:, ot, t0:t0 + nw],
                            i=tile[:, 0:nw], s1=bias_sb[:, ot:ot + 1]:
                            e.tensor_scalar_add(o, i, s1))
                pool.release(s1t, "vector", ev_c)
                if DBG_QK >= 1:
                    s2t, tile2 = pool.acquire()
                    P.wait("tensor", "scalar", sq_c)
                    nm_c = P.op("tensor", lambda e, o=tile2[0:2, 0:nw], l=ones2[:],
                                r=q2_sb[:, 0:nw]: e.matmul(o, l, r, start=True, stop=True))
                    P.wait("vector", "tensor", nm_c)
                    nv_c = P.op("vector", lambda e, o=nqb[:, ot, 0:nw],
                                i=tile2[0:2, 0:nw]: e.tensor_copy(o, i))
                    pool.release(s2t, "vector", nv_c)
            if DBG_QK < 2:
                continue
            P.wait("scalar", "vector", nv_c)
            c1 = P.op("scalar", lambda e, o=nqb[:, :, 0:nw], i=nqb[:, :, 0:nw],
                      b=P.cf(1e-12, 2): e.activation(o, i, AF.Ln, bias=b))
            P.wait("scalar", "scalar", c1)
            c2 = P.op("scalar", lambda e, o=nqb[:, :, 0:nw], i=nqb[:, :, 0:nw],
                      sc=P.cf(0.5, 2): e.activation(o, i, AF.Exp, scale=sc))
            P.wait("vector", "scalar", c2)
            c3 = P.op("vector", lambda e, o=nqb[:, :, 0:nw], i=nqb[:, :, 0:nw]:
                      e.tensor_scalar_add(o, i, float(EPS_ATT)))
            P.wait("vector", "vector", c3)
            c4 = P.op("vector", lambda e, o=nqb[:, :, 0:nw], i=nqb[:, :, 0:nw]:
                      e.reciprocal(o, i))
            if DBG_QK < 3:
                continue
            P.wait("sync", "vector", c4)
            dst_e = bass.AP(nrm_d, t0, [[16 * TEX, 1], [TEX, 8], [1, nw]])
            dst_o = bass.AP(nrm_d, 16 * TEX + t0, [[16 * TEX, 1], [TEX, 8], [1, nw]])
            P.dma("sync", dst_e, nqb[0:1, :, 0:nw], sNr)
            P.dma("sync", dst_o, nqb[1:2, :, 0:nw], sNr)
        for blk, (t0, nw) in enumerate(blocks_of(tcols, 512) if DBG_QK >= 4 else []):
            P.swait("sync", sNr)
            P.wait("sync", "vector", P.pgn["vector"])
            src_e = bass.AP(nrm_d, t0, [[0, 64], [TEX, 8], [1, nw]])
            src_o = bass.AP(nrm_d, 16 * TEX + t0, [[0, 64], [TEX, 8], [1, nw]])
            P.dma("sync", rnb[0:64, :, 0:nw], src_e, sNr)
            rb_c = P.dma("sync", rnb[64:128, :, 0:nw], src_o, sNr)
            P.swait("vector", sNr, rb_c)
            for ot in range(8):
                nz_c = P.op("vector", lambda e, o=out_bf[:, ot, t0:t0 + nw],
                            a=out_bf[:, ot, t0:t0 + nw], b=rnb[:, ot, 0:nw]:
                            e.tensor_mul(o, a, b))
        return nz_c

    if _dl >= 1:
        qn_c = qk_phase(0, wqT, bq_sb, qb[:], T, HLO)
        kn_c = qk_phase(1, wkT, bk_sb, kb[:], TEX, 0)

    vt_c = 0
    for tt in range(17 if _dl >= 2 else 0):
        rows = 128 if tt < 16 else 64
        for oc in range(8):
            pr = oc % 2
            o0 = oc * 128
            if wu[pr]:
                P.wait("sync", "tensor", wu[pr])
            wl_c = P.dma("sync", wst[:, pr], wvT[oc], sW[pr])
            s1t, tile = pool.acquire()
            P.swait("tensor", sW[pr], wl_c)
            for kt in range(8):
                mm_c = P.op("tensor", lambda e, o=tile[0:rows, 0:128],
                            l=xn1b[:, kt, tt * 128:tt * 128 + rows],
                            r=wst[:, pr, kt, :], s_=(kt == 0), p=(kt == 7):
                            e.matmul(o, l, r, start=s_, stop=p))
            wu[pr] = mm_c
            P.wait("vector", "tensor", mm_c)
            vt_c = P.op("vector", lambda e, o=v_tm[0:rows, tt, o0:o0 + 128],
                        a=tile[0:rows, 0:128], b=bv_sb[0:rows, o0:o0 + 128]:
                        e.tensor_add(o, a, b))
            pool.release(s1t, "vector", vt_c)

    P.wait("tensor", "vector", vt_c)
    ao_c = mk_c = sums_c = 0
    sums_ev = [0, 0]
    for qt in range(16 if _dl >= 3 else 0):
        if mk_c:
            P.wait("sync", "gpsimd", mk_c)
        P.dma("sync", m1_sb[:], mask1[qt], sMk)
        mk_ld = P.dma("sync", m2_sb[:], mask2[qt], sMk)
        P.swait("vector", sNr)
        s_sums, tsum = pool.acquire()
        for h in range(16):
            hb = (h % 2) * 64
            ho = h // 2
            sb_half = (h % 2) * 512
            s1t, tile = pool.acquire()
            P.op("tensor", lambda e, o=tile[:, 0:128],
                 l=kb[hb:hb + 64, ho, qt * 128:qt * 128 + 128],
                 r=qb[hb:hb + 64, ho, qt * 128:qt * 128 + 128]:
                 e.matmul(o, l, r, start=True, stop=True))
            sc2 = P.op("tensor", lambda e, o=tile[0:64, 512:640],
                       l=kb[hb:hb + 64, ho, qt * 128 + 128:qt * 128 + 192],
                       r=qb[hb:hb + 64, ho, qt * 128:qt * 128 + 128]:
                       e.matmul(o, l, r, start=True, stop=True))
            P.wait("scalar", "tensor", sc2)
            if mk_c:
                P.wait("scalar", "gpsimd", mk_c)
            P.op("scalar", lambda e, o=e1_sb[:], i=tile[:, 0:128]:
                 e.activation(o, i, AF.Exp))
            x2c = P.op("scalar", lambda e, o=e2_sb[:], i=tile[0:64, 512:640]:
                       e.activation(o, i, AF.Exp))
            P.wait("gpsimd", "scalar", x2c)
            if h == 0:
                P.swait("gpsimd", sMk, mk_ld)
            P.op("gpsimd", lambda e: e.tensor_mul(e1b[:], e1_sb[:], m1_sb[:]))
            m2c = P.op("gpsimd", lambda e: e.tensor_mul(e2b[:], e2_sb[:], m2_sb[:]))
            pool.release(s1t, "scalar", x2c)
            mk_c = m2c
            P.wait("tensor", "gpsimd", m2c)
            if sums_ev[h % 2]:
                P.wait("tensor", "vector", sums_ev[h % 2])
            P.op("tensor", lambda e, o=tsum[0:1, sb_half:sb_half + 128],
                 l=onesb1[:], r=e1b[:]: e.matmul(o, l, r, start=True, stop=False))
            sm_mm = P.op("tensor", lambda e, o=tsum[0:1, sb_half:sb_half + 128],
                         l=onesb1[0:64, :], r=e2b[:]:
                         e.matmul(o, l, r, start=False, stop=True))
            P.wait("vector", "tensor", sm_mm)
            sums_ev[h % 2] = P.op("vector",
                lambda e, o=sums_st[:, h * 128:(h + 1) * 128],
                       i=tsum[0:1, sb_half:sb_half + 128]: e.tensor_copy(o, i))
            s2t, tile2 = pool.acquire()
            P.op("tensor", lambda e, o=tile2[hb:hb + 64, 0:128],
                 l=v_tm[:, qt, h * 64:(h + 1) * 64], r=e1b[:]:
                 e.matmul(o, l, r, start=True, stop=False))
            av2 = P.op("tensor", lambda e, o=tile2[hb:hb + 64, 0:128],
                       l=v_tm[0:64, qt + 1, h * 64:(h + 1) * 64], r=e2b[:]:
                       e.matmul(o, l, r, start=False, stop=True))
            P.wait("vector", "tensor", av2)
            ao_c = P.op("vector",
                        lambda e, o=aob[hb:hb + 64, ho, qt * 128:(qt + 1) * 128],
                        i=tile2[hb:hb + 64, 0:128]: e.tensor_copy(o, i))
            pool.release(s2t, "vector", ao_c)
        pool.release(s_sums, "vector", sums_ev[1])
        P.wait("sync", "vector", max(sums_ev))
        dst = bass.AP(srow_d, qt * 128, [[16 * T, 1], [T, 16], [1, 128]])
        P.dma("sync", dst, sums_st.rearrange("p (h w) -> p h w", h=16), sNr)

    as_c = 0
    P.wait("vector", "vector", P.pgn["vector"])
    for blk in range(4 if _dl >= 4 else 0):
        t0 = blk * 512
        P.swait("sync", sNr)
        P.wait("sync", "vector", P.pgn["vector"])
        src_e = bass.AP(srow_d, t0, [[0, 64], [2 * T, 8], [1, 512]])
        src_o = bass.AP(srow_d, T + t0, [[0, 64], [2 * T, 8], [1, 512]])
        P.dma("sync", rnb[0:64, :, 0:512], src_e, sNr)
        rb_c = P.dma("sync", rnb[64:128, :, 0:512], src_o, sNr)
        P.swait("vector", sNr, rb_c)
        rc_c = P.op("vector", lambda e, o=rnb[:, :, 0:512], i=rnb[:, :, 0:512]:
                    e.reciprocal(o, i))
        P.wait("vector", "vector", rc_c)
        for ot in range(8):
            as_c = P.op("vector", lambda e, o=aob[:, ot, t0:t0 + 512],
                        a=aob[:, ot, t0:t0 + 512], b=rnb[:, ot, 0:512]:
                        e.tensor_mul(o, a, b))

    P.wait("tensor", "vector", as_c)
    ev_c = 0
    for blk in range(4 if _dl >= 5 else 0):
        t0 = blk * 512
        if blk > 0:
            P.swait("scalar", sSt)
        for ot in range(8):
            pr = ot % 2
            if wu[pr]:
                P.wait("sync", "tensor", wu[pr])
            wl_c = P.dma("sync", wst[:, pr], woT[ot], sW[pr])
            s1t, tile = pool.acquire()
            P.swait("tensor", sW[pr], wl_c)
            for kt in range(8):
                mm_c = P.op("tensor", lambda e, o=tile[:, 0:512],
                            l=wst[:, pr, kt, :], r=aob[:, kt, t0:t0 + 512],
                            s_=(kt == 0), p=(kt == 7):
                            e.matmul(o, l, r, start=s_, stop=p))
            wu[pr] = mm_c
            P.wait("scalar", "tensor", mm_c)
            ev_c = P.op("scalar", lambda e, o=att_st[:, ot, :], i=tile[:, 0:512],
                        b=bo_sb[:, ot:ot + 1]:
                        e.activation(o, i, AF.Identity, bias=b))
            pool.release(s1t, "scalar", ev_c)
        P.wait("sync", "scalar", ev_c)
        P.dma("sync", at_out[:, :, t0:t0 + 512], att_st[:], sSt)
    return


def build_launchD1():
    nc = bass.Bass("TRN2", target_bir_lowering=False)
    P = Prog(nc)
    TEX = T + 2 * HLO
    t = {}
    t["x1e"] = nc.dram_tensor("x1e", [128, 8, TEX], F32, kind="ExternalInput").ap()
    t["wqT"] = nc.dram_tensor("wqT", [D, D], F32, kind="ExternalInput").ap()
    t["wkT"] = nc.dram_tensor("wkT", [D, D], F32, kind="ExternalInput").ap()
    t["wvT"] = nc.dram_tensor("wvT", [D, D], F32, kind="ExternalInput").ap()
    t["woT"] = nc.dram_tensor("woT", [D, D], F32, kind="ExternalInput").ap()
    t["bq"] = nc.dram_tensor("bq", [128, 8], F32, kind="ExternalInput").ap()
    t["bk"] = nc.dram_tensor("bk", [128, 8], F32, kind="ExternalInput").ap()
    t["bvt"] = nc.dram_tensor("bvt", [128, D], F32, kind="ExternalInput").ap()
    t["bo_"] = nc.dram_tensor("bo_", [128, 8], F32, kind="ExternalInput").ap()
    t["mask1"] = nc.dram_tensor("mask1", [16, 128, 128], F32, kind="ExternalInput").ap()
    t["mask2"] = nc.dram_tensor("mask2", [16, 64, 128], F32, kind="ExternalInput").ap()
    t["nrm_d"] = nc.dram_tensor("nrm_d", [2, 16, TEX], F32)
    t["srow_d"] = nc.dram_tensor("srow_d", [16, T], F32)
    t["xn1_out"] = nc.dram_tensor("xn1_out", [128, 8, T], BF16, kind="ExternalOutput").ap()
    t["at_out"] = nc.dram_tensor("at_out", [128, 8, T], F32, kind="ExternalOutput").ap()
    S = mk_streams(P)
    emit_D1(P, S, t)
    return P.finish()

def emit_D2(P, S, t):
    nc = P.nc
    TH = 1024
    NSUB = T // TH
    xn1_in = t["xn1_in"]; x1_in = t["x1_in"]; at_in = t["at_in"]
    gate_w = t["gate_w"]; gbt = t["gbt"]; w1e = t["w1e"]; b1e = t["b1e"]
    w2e = t["w2e"]; b2e = t["b2e"]; ws1 = t["ws1"]; bs1 = t["bs1"]
    ws2 = t["ws2"]; bs2 = t["bs2"]; ident = t["ident"]; comb_d = t["comb_d"]
    x2_out = t["x2_out"]

    xn1b = P.sbuf("xn1b", (128, 8, T), BF16)
    hm = P.sbuf("hm", (128, 32, TH), BF16)
    acc = P.sbuf("acc", (128, 8, TH), F32)
    cbb = P.sbuf("cbb", (128, 8, TH), F32)
    stage = P.sbuf("stage", (128, 8, 512), F32)
    wst = P.sbuf("wst", (128, 2, 32, 128), BF16)
    lg_sb = P.sbuf("lg", (128, 16, 8), F32)
    m1_sb = P.sbuf("m1", (128, 16), F32)
    m2_sb = P.sbuf("m2", (128, 16), F32)
    eq1_sb = P.sbuf("eq1", (128, 16, 8), F32)
    eq2_sb = P.sbuf("eq2", (128, 16, 8), F32)
    tm_sb = P.sbuf("tm", (128, 8), F32)
    gw_sb = P.sbuf("gw", (128, 16, 2), F32)
    cmb_sb = P.sbuf("cmb", (128, 16, 8), F32)
    cfm_sb = P.sbuf("cfm", (8, T), F32)
    id_sb = P.sbuf("idn", (128, 128), F32)
    gr_sb = P.sbuf("gr", (128, 8, 8), BF16)
    gbt_sb = P.sbuf("gbt_s", (128, 8), F32)
    b1_sb = P.sbuf("b1s", (128, 8, 32), F32)
    b2_sb = P.sbuf("b2s", (128, 8, 8), F32)
    bs1_sb = P.sbuf("bs1s", (128, 32), F32)
    bs2_sb = P.sbuf("bs2s", (128, 8), F32)
    pool = S["pool"]

    sSm = S["SM"]; sXn = S["LN"]; sW = [S["G0"], S["G1"]]
    sCb = S["BB"]; sIo = S["IO"]; sSt = S["ST"]; sGm = S["GM"]

    P.dma("sync", gr_sb[:], gate_w, sGm)
    P.dma("sync", gbt_sb[:], gbt, sSm)
    P.dma("sync", b1_sb[:], b1e, sSm)
    P.dma("sync", b2_sb[:], b2e, sSm)
    P.dma("sync", bs1_sb[:], bs1, sSm)
    P.dma("sync", bs2_sb[:], bs2, sSm)
    P.dma("sync", id_sb[:], ident, sSm)
    sm_c = P.dma("sync", xn1b[:], xn1_in, sXn)

    P.swait("tensor", sXn, sm_c)
    P.swait("tensor", sGm, sGm.n)
    P.swait("vector", sSm, sSm.n)
    lg_c = 0
    for tt in range(16):
        s1t, tile = pool.acquire()
        for kt in range(8):
            mm_c = P.op("tensor", lambda e, o=tile[0:128, 0:8],
                        l=xn1b[:, kt, tt * 128:(tt + 1) * 128], r=gr_sb[:, kt, :],
                        s_=(kt == 0), p=(kt == 7):
                        e.matmul(o, l, r, start=s_, stop=p))
        P.wait("vector", "tensor", mm_c)
        lg_c = P.op("vector", lambda e, o=lg_sb[:, tt, :], a=tile[0:128, 0:8],
                    b=gbt_sb[:]: e.tensor_add(o, a, b))
        pool.release(s1t, "vector", lg_c)
    P.wait("vector", "vector", lg_c)
    m1_c = P.op("vector", lambda e: e.reduce_max(m1_sb[:], lg_sb[:], axis=AX.X))
    eq_c = 0
    for tt in range(16):
        P.wait("vector", "vector", m1_c if tt == 0 else eq_c)
        e1c = P.op("vector", lambda e, o=eq1_sb[:, tt, :], a=lg_sb[:, tt, :],
                   s1=m1_sb[:, tt:tt + 1]:
                   e.tensor_single_scalar(o, a, s1, ALU.is_equal))
        P.wait("vector", "vector", e1c)
        eq_c = P.op("vector", lambda e, o=tm_sb[:], a=eq1_sb[:, tt, :],
                    b=lg_sb[:, tt, :]:
                    e.scalar_tensor_tensor(o, a, -1e30, b, ALU.mult, ALU.add))
        P.wait("vector", "vector", eq_c)
        m2c = P.op("vector", lambda e, o=m2_sb[:, tt:tt + 1], i=tm_sb[:]:
                   e.reduce_max(o, i, axis=AX.X))
        P.wait("vector", "vector", m2c)
        eq_c = P.op("vector", lambda e, o=eq2_sb[:, tt, :], a=tm_sb[:],
                    s1=m2_sb[:, tt:tt + 1]:
                    e.tensor_single_scalar(o, a, s1, ALU.is_equal))
    P.wait("vector", "vector", eq_c)
    d_c = P.op("vector", lambda e: e.tensor_sub(m1_sb[:], m1_sb[:], m2_sb[:]))
    P.wait("scalar", "vector", d_c)
    sg_c = P.op("scalar", lambda e, o=gw_sb[:, :, 0], i=m1_sb[:]:
                e.activation(o, i, AF.Sigmoid))
    P.wait("vector", "scalar", sg_c)
    g2_c = P.op("vector", lambda e, o=gw_sb[:, :, 1], i=gw_sb[:, :, 0]:
                e.tensor_scalar(o, i, -1.0, 1.0, ALU.mult, ALU.add))
    cm_c = 0
    for tt in range(16):
        P.wait("vector", "vector", g2_c if tt == 0 else cm_c)
        c1 = P.op("vector", lambda e, o=cmb_sb[:, tt, :], a=eq1_sb[:, tt, :],
                  s1=gw_sb[:, tt:tt + 1, 0]: e.tensor_scalar_mul(o, a, s1))
        P.wait("vector", "vector", c1)
        cm_c = P.op("vector", lambda e, o=cmb_sb[:, tt, :], a=eq2_sb[:, tt, :],
                    s1=gw_sb[:, tt:tt + 1, 1], b=cmb_sb[:, tt, :]:
                    e.scalar_tensor_tensor(o, a, s1, b, ALU.mult, ALU.add))
    P.wait("tensor", "vector", cm_c)
    P.swait("tensor", sSm, sSm.n)
    tr_c = 0
    for tt in range(16):
        s1t, tile = pool.acquire()
        t_c = P.op("tensor", lambda e, o=tile[0:8, 0:128], i=cmb_sb[:, tt, :],
                   idn=id_sb[:]: e.transpose(o, i, idn))
        P.wait("vector", "tensor", t_c)
        tr_c = P.op("vector", lambda e, o=cfm_sb[:, tt * 128:(tt + 1) * 128],
                    i=tile[0:8, 0:128]: e.tensor_copy(o, i))
        pool.release(s1t, "vector", tr_c)
    P.wait("sync", "vector", tr_c)
    cb_st = P.dma("sync", bass.AP(comb_d, 0, [[T, 8], [1, T]]), cfm_sb[:], sCb)

    wu = [0, 0]
    ac_c = 0
    for s in range(NSUB):
        t0 = s * TH
        P.swait("sync", sCb)
        if ac_c:
            P.wait("sync", "vector", P.pgn["vector"])
        cb_c = P.dma("sync", cbb[:],
                     bass.AP(comb_d, t0, [[0, 128], [T, 8], [1, TH]]), sCb)
        P.swait("vector", sCb, cb_c)
        P.wait("vector", "vector", P.pgn["vector"])
        z_c = P.op("vector", lambda e: e.memset(acc[:], 0.0))

        for ex in range(NEXP + 1):
            shared = ex == NEXP
            w1r = ws1 if shared else w1e[ex]
            w2r = ws2 if shared else w2e[ex]
            bcol1 = bs1_sb if shared else b1_sb[:, ex, :]
            bcol2 = bs2_sb if shared else b2_sb[:, ex, :]
            for ot in range(32):
                pr = ot % 2
                if wu[pr]:
                    P.wait("sync", "tensor", wu[pr])
                wl_c = P.dma("sync", wst[:, pr, 0:8, :], w1r[ot], sW[pr])
                s1t, tile = pool.acquire()
                P.swait("tensor", sW[pr], wl_c)
                for nt in range(2):
                    for kt in range(8):
                        mm_c = P.op("tensor", lambda e, o=tile[:, nt * 512:nt * 512 + 512],
                                    l=wst[:, pr, kt, :],
                                    r=xn1b[:, kt, t0 + nt * 512:t0 + nt * 512 + 512],
                                    s_=(kt == 0), p=(kt == 7):
                                    e.matmul(o, l, r, start=s_, stop=p))
                wu[pr] = mm_c
                P.wait("scalar", "tensor", mm_c)
                h_c = P.op("scalar", lambda e, o=hm[:, ot, :], i=tile[:, 0:1024],
                           b=bcol1[:, ot:ot + 1]:
                           e.activation(o, i, AF.Gelu, bias=b))
                pool.release(s1t, "scalar", h_c)
            for ot in range(8):
                pr = ot % 2
                if wu[pr]:
                    P.wait("sync", "tensor", wu[pr])
                wl_c = P.dma("sync", wst[:, pr], w2r[ot], sW[pr])
                s1t, tile = pool.acquire()
                P.swait("tensor", sW[pr], wl_c)
                if ot == 0:
                    P.wait("tensor", "scalar", h_c)
                for nt in range(2):
                    for kt in range(32):
                        mm_c = P.op("tensor", lambda e, o=tile[:, nt * 512:nt * 512 + 512],
                                    l=wst[:, pr, kt, :], r=hm[:, kt, nt * 512:nt * 512 + 512],
                                    s_=(kt == 0), p=(kt == 31):
                                    e.matmul(o, l, r, start=s_, stop=p))
                wu[pr] = mm_c
                P.wait("vector", "tensor", mm_c)
                if ex == 0 and ot == 0:
                    P.wait("vector", "vector", z_c)
                for nt in range(2):
                    if ac_c:
                        P.wait("vector", "vector", ac_c)
                    if shared:
                        c1 = P.op("vector", lambda e, o=stage[:, ot, :],
                                  a=tile[:, nt * 512:nt * 512 + 512],
                                  s1=bcol2[:, ot:ot + 1]:
                                  e.tensor_scalar_add(o, a, s1))
                    else:
                        c1 = P.op("vector", lambda e, o=stage[:, ot, :],
                                  a=tile[:, nt * 512:nt * 512 + 512],
                                  s1=bcol2[:, ot:ot + 1],
                                  b=cbb[:, ex, nt * 512:nt * 512 + 512]:
                                  e.scalar_tensor_tensor(o, a, s1, b, ALU.add, ALU.mult))
                    P.wait("vector", "vector", c1)
                    ac_c = P.op("vector", lambda e, o=acc[:, ot, nt * 512:nt * 512 + 512],
                                a=acc[:, ot, nt * 512:nt * 512 + 512],
                                b=stage[:, ot, :]: e.tensor_add(o, a, b))
                pool.release(s1t, "vector", ac_c)

        for half in range(2):
            hs = t0 + half * 512
            P.wait("sync", "vector", P.pgn["vector"])
            io_c = P.dma("sync", stage[:], x1_in[:, :, hs:hs + 512], sIo)
            P.swait("vector", sIo, io_c)
            for ot in range(8):
                a1_c = P.op("vector", lambda e, o=acc[:, ot, half * 512:half * 512 + 512],
                            a=acc[:, ot, half * 512:half * 512 + 512],
                            b=stage[:, ot, :]: e.tensor_add(o, a, b))
            P.wait("sync", "vector", a1_c)
            io_c = P.dma("sync", stage[:], at_in[:, :, hs:hs + 512], sIo)
            P.swait("vector", sIo, io_c)
            for ot in range(8):
                a2_c = P.op("vector", lambda e, o=acc[:, ot, half * 512:half * 512 + 512],
                            a=acc[:, ot, half * 512:half * 512 + 512],
                            b=stage[:, ot, :]: e.tensor_add(o, a, b))
            P.wait("sync", "vector", a2_c)
            P.dma("sync", x2_out[:, :, hs:hs + 512],
                  acc[:, :, half * 512:half * 512 + 512], sSt)
        P.swait("vector", sSt)
    return


def build_launchD2():
    nc = bass.Bass("TRN2", target_bir_lowering=False)
    P = Prog(nc)
    t = {}
    t["xn1_in"] = nc.dram_tensor("xn1_in", [128, 8, T], BF16, kind="ExternalInput").ap()
    t["x1_in"] = nc.dram_tensor("x1_in", [128, 8, T], F32, kind="ExternalInput").ap()
    t["at_in"] = nc.dram_tensor("at_in", [128, 8, T], F32, kind="ExternalInput").ap()
    t["gate_w"] = nc.dram_tensor("gate_w", [D, NEXP], F32, kind="ExternalInput").ap()
    t["gbt"] = nc.dram_tensor("gbt", [128, NEXP], F32, kind="ExternalInput").ap()
    t["w1e"] = nc.dram_tensor("w1e", [NEXP, D, HID], F32, kind="ExternalInput").ap()
    t["b1e"] = nc.dram_tensor("b1e", [128, NEXP, 32], F32, kind="ExternalInput").ap()
    t["w2e"] = nc.dram_tensor("w2e", [NEXP, HID, D], F32, kind="ExternalInput").ap()
    t["b2e"] = nc.dram_tensor("b2e", [128, NEXP, 8], F32, kind="ExternalInput").ap()
    t["ws1"] = nc.dram_tensor("ws1", [D, HID], F32, kind="ExternalInput").ap()
    t["bs1"] = nc.dram_tensor("bs1", [128, 32], F32, kind="ExternalInput").ap()
    t["ws2"] = nc.dram_tensor("ws2", [HID, D], F32, kind="ExternalInput").ap()
    t["bs2"] = nc.dram_tensor("bs2", [128, 8], F32, kind="ExternalInput").ap()
    t["ident"] = nc.dram_tensor("ident", [128, 128], F32, kind="ExternalInput").ap()
    t["comb_d"] = nc.dram_tensor("comb_d", [NEXP, T], F32)
    t["x2_out"] = nc.dram_tensor("x2_out", [128, 8, T], F32, kind="ExternalOutput").ap()
    S = mk_streams(P)
    emit_D2(P, S, t)
    return P.finish()

def _fm(aT):
    Dd, Tt = aT.shape
    return np.ascontiguousarray(aT.reshape(Dd // 128, 128, Tt).transpose(1, 0, 2))


def _cols(v):
    return np.ascontiguousarray(np.asarray(v).reshape(-1, 128).T)




def build_launch2():
    nc = bass.Bass("TRN2", target_bir_lowering=False)
    P = Prog(nc)
    TEX = T + 2 * HLO
    EI = lambda nm, sh, dt=F32: nc.dram_tensor(nm, sh, dt, kind="ExternalInput").ap()
    t = {}
    for nm, sh, *dt in [("u_in", [128, 16, T]), ("dl_in", [128, 16, T]),
                   ("res_in", [128, 16, T]), ("bc_in", [32, T]),
                   ("uex", [128, 16, 64]), ("dlex", [128, 16, 64]),
                   ("resex", [128, 16, 64]), ("bcex", [32, 64]),
                   ("alog", [128, 256]), ("dvec", [128, 16]),
                   ("wout", [8, 128, 16, 128], BF16), ("fqa", [8, 128, 1024]),
                   ("cmask", [128, 40]),
                   ("x2fm", [128, 8, TEX]),
                   ("w1T", [32, 128, 8, 128], BF16), ("w2T", [32, 128, 8, 128], BF16),
                   ("h1T", [32, 128, 32, 128], BF16),
                   ("h2T", [32, 128, 32, 128], BF16),
                   ("w3T", [8, 128, 32, 128], BF16), ("w1s", [128, 32]),
                   ("w2s", [128, 32]), ("h1s", [128, 32]), ("h2s", [128, 32]),
                   ("w3s", [128, 8]),
                   ("wqT", [8, 128, 8, 128], BF16), ("wkT", [8, 128, 8, 128], BF16),
                   ("wvT", [8, 128, 8, 128], BF16), ("woT", [8, 128, 8, 128], BF16),
                   ("bq", [128, 8]),
                   ("bk", [128, 8]), ("bvt", [128, D]), ("bo_", [128, 8]),
                   ("mask1", [16, 128, 128]), ("mask2", [16, 64, 128]),
                   ("gate_w", [128, 8, NEXP], BF16), ("gbt", [128, NEXP]),
                   ("w1e", [NEXP, 32, 128, 8, 128], BF16), ("b1e", [128, NEXP, 32]),
                   ("w2e", [NEXP, 8, 128, 32, 128], BF16), ("b2e", [128, NEXP, 8]),
                   ("ws1", [32, 128, 8, 128], BF16), ("bs1", [128, 32]),
                   ("ws2", [8, 128, 32, 128], BF16),
                   ("bs2", [128, 8]), ("ident", [128, 128])]:
        t[nm] = EI(nm, sh, dt[0] if dt else F32)
    t["xn_ext"] = nc.dram_tensor("xn_ext", [128, 8, TEX], BF16,
                                 kind="ExternalInput").ap()
    ID = lambda nm, sh, dt=F32: nc.dram_tensor(nm, sh, dt)
    mam_d = ID("mam_d", [128, 8, TEX])
    x1e_d = ID("x1e_d", [128, 8, TEX])
    xn1_d = ID("xn1_d", [128, 8, T], BF16)
    at_d = ID("at_d", [128, 8, T])
    nrm_d = ID("nrm_d", [2, 16, TEX])
    srow_d = ID("srow_d", [16, T])
    comb_d = ID("comb_d", [NEXP, T])
    x2_out = nc.dram_tensor("x2_out", [128, 8, T], F32, kind="ExternalOutput").ap()

    S = mk_streams(P)
    tB = dict(t, mam_out=mam_d.ap())
    with P.scope("B"):
        emit_B2(P, S, tB)
    P.barrier()
    tC = dict(t, xn_in=t["xn_ext"], x_in=t["x2fm"], mam_in=mam_d.ap(),
              x1_out=x1e_d.ap())
    with P.scope("C"):
        emit_C(P, S, tC)
    P.barrier()
    tD1 = dict(t, x1e=x1e_d.ap(), nrm_d=nrm_d, srow_d=srow_d,
               xn1_out=xn1_d.ap(), at_out=at_d.ap())
    with P.scope("E"):
        emit_D1(P, S, tD1)
    P.barrier()
    tD2 = dict(t, xn1_in=xn1_d.ap(), x1_in=x1e_d.ap()[:, :, HLO:HLO + T],
               at_in=at_d.ap(), comb_d=comb_d, x2_out=x2_out)
    with P.scope("F"):
        emit_D2(P, S, tD2)
    return P.finish()


def _maps_A(inp):
    x = inp["L0_attn_x"]
    alog_l = np.ascontiguousarray(
        inp["L0_attn_A_log"].reshape(16, 128, 16).transpose(1, 0, 2).reshape(128, 256))
    cw = np.ascontiguousarray(
        inp["L0_attn_conv_w"][:, 0, :].reshape(4, 16, 128).transpose(2, 1, 0).reshape(128, 64))
    common = dict(win=inp["L0_attn_W_in"], convw=cw,
                  convb=_cols(inp["L0_attn_conv_b"]), wx=inp["L0_attn_W_x"],
                  wdt=inp["L0_attn_W_dt"], bdt=_cols(inp["L0_attn_b_dt"]), alog=alog_l)
    maps = []
    for c in range(NCORES):
        b, q = c // 4, c % 4
        t0 = q * T
        xe = np.zeros((T + 3, D), np.float32)
        xe[3:] = x[b, t0:t0 + T]
        if q > 0:
            xe[0:3] = x[b, t0 - 3:t0]
        maps.append(dict(common, xfm=_fm(np.ascontiguousarray(xe.T))))
    return maps, alog_l


def _maps_B(inp, a_res, alog_l):
    fqa = np.stack([a_res[c]["fq_out"] for c in range(NCORES)])
    maps = []
    for c in range(NCORES):
        b = c // 4
        m = np.zeros(8, np.float32)
        m[4 * b: c] = 1.0
        cm = np.zeros((128, 16), np.float32)
        cm[:, 0:8] = m
        cm[:, 8:16] = 1.0 - m
        maps.append(dict(u_in=a_res[c]["xconv_out"], dl_in=a_res[c]["delta_out"],
                         bc_in=a_res[c]["bc_out"], res_in=a_res[c]["res_out"],
                         alog=alog_l, dvec=_cols(inp["L0_attn_D"]),
                         wout=inp["L0_attn_W_out"], fqa=fqa, cmask=cm))
    return maps


def _maps_C(inp, x, a_res, b_res):
    tr = lambda a: np.ascontiguousarray(a.T)
    maps = []
    for c in range(NCORES):
        b, q = c // 4, c % 4
        maps.append(dict(
            xn_in=a_res[c]["xn_out"], mam_in=b_res[c]["mam_out"],
            x_in=_fm(np.ascontiguousarray(x[b, q * T:(q + 1) * T].T)),
            w1T=tr(inp["L0_ffn_w1_w"]), w2T=tr(inp["L0_ffn_w2_w"]),
            h1T=tr(inp["L0_ffn_h1_w"]), h2T=tr(inp["L0_ffn_h2_w"]),
            w3T=tr(inp["L0_ffn_w3_w"]),
            w1s=_cols(inp["L0_ffn_w1_s"]), w2s=_cols(inp["L0_ffn_w2_s"]),
            h1s=_cols(inp["L0_ffn_h1_s"]), h2s=_cols(inp["L0_ffn_h2_s"]),
            w3s=_cols(inp["L0_ffn_w3_s"])))
    return maps


def _unfm(a):
    return np.ascontiguousarray(np.asarray(a).transpose(1, 0, 2).reshape(-1, a.shape[2]))


def _maps_D1(inp, c_res):
    x1 = {c: _unfm(c_res[c]["x1_out"]) for c in range(NCORES)}
    tr = lambda a: np.ascontiguousarray(a.T)
    maps = []
    tk = np.arange(192)[:, None]
    tq = np.arange(128)[None, :]
    dd = tk - tq - 32
    band = (np.abs(dd) <= 32) & (dd % 4 == 0)
    for c in range(NCORES):
        b, q = c // 4, c % 4
        xe = np.zeros((D, T + 2 * HLO), np.float32)
        xe[:, HLO:HLO + T] = x1[c]
        if q > 0:
            xe[:, 0:HLO] = x1[c - 1][:, T - HLO:]
        if q < 3:
            xe[:, HLO + T:] = x1[c + 1][:, 0:HLO]
        m = np.zeros((16, 192, 128), np.float32)
        for qt in range(16):
            xext = qt * 128 + np.arange(192)
            valid = np.ones(192, bool)
            if q == 0:
                valid &= xext >= HLO
            if q == 3:
                valid &= xext < HLO + T
            m[qt] = band & valid[:, None]
        maps.append(dict(
            x1e=_fm(xe),
            wqT=st(tr(inp["L1_attn_Wq"]), 8, 8), wkT=st(tr(inp["L1_attn_Wk"]), 8, 8),
            wvT=st(tr(inp["L1_attn_Wv"]), 8, 8), woT=st(tr(inp["L1_attn_Wo"]), 8, 8),
            bq=_cols(inp["L1_attn_bq"]), bk=_cols(inp["L1_attn_bk"]),
            bvt=np.ascontiguousarray(np.tile(inp["L1_attn_bv"], (128, 1))),
            bo_=_cols(inp["L1_attn_bo"]),
            mask1=np.ascontiguousarray(m[:, 0:128, :]),
            mask2=np.ascontiguousarray(m[:, 128:192, :])))
    return maps


def _maps_D2(inp, c_res, d1_res):
    b1c = np.ascontiguousarray(inp["L1_ffn_b1"].reshape(8, 32, 128).transpose(2, 0, 1))
    b2c = np.ascontiguousarray(inp["L1_ffn_b2"].reshape(8, 8, 128).transpose(2, 0, 1))
    maps = []
    for c in range(NCORES):
        maps.append(dict(
            xn1_in=d1_res[c]["xn1_out"], x1_in=c_res[c]["x1_out"],
            at_in=d1_res[c]["at_out"],
            gate_w=np.ascontiguousarray(
                inp["L1_ffn_gate_w"].reshape(8, 128, NEXP).transpose(1, 0, 2)).astype(BFH),
            gbt=np.ascontiguousarray(np.tile(inp["L1_ffn_gate_b"], (128, 1))),
            w1e=np.ascontiguousarray(
                inp["L1_ffn_w1"].reshape(NEXP, 8, 128, 32, 128).transpose(0, 3, 2, 1, 4)).astype(BFH),
            b1e=b1c,
            w2e=np.ascontiguousarray(
                inp["L1_ffn_w2"].reshape(NEXP, 32, 128, 8, 128).transpose(0, 3, 2, 1, 4)).astype(BFH),
            b2e=b2c,
            ws1=st(inp["L1_ffn_ws1"], 8, 32), bs1=_cols(inp["L1_ffn_bs1"]),
            ws2=st(inp["L1_ffn_ws2"], 32, 8), bs2=_cols(inp["L1_ffn_bs2"]),
            ident=np.eye(128, dtype=np.float32)))
    return maps


def _maps_L2(inp, x, a_res, alog_l):
    import ml_dtypes
    BFH = ml_dtypes.bfloat16
    tr = lambda a: np.ascontiguousarray(a.T)
    st = lambda wT, KT, NO: np.ascontiguousarray(
        wT.reshape(KT, 128, NO, 128).transpose(2, 1, 0, 3)).astype(BFH)
    fqa = np.stack([a_res[c]["fq_out"] for c in range(NCORES)])
    b1c = np.ascontiguousarray(inp["L1_ffn_b1"].reshape(8, 32, 128).transpose(2, 0, 1))
    b2c = np.ascontiguousarray(inp["L1_ffn_b2"].reshape(8, 8, 128).transpose(2, 0, 1))
    tk = np.arange(192)[:, None]
    tq = np.arange(128)[None, :]
    dd = tk - tq - 32
    band = (np.abs(dd) <= 32) & (dd % 4 == 0)
    TEX = T + 2 * HLO
    maps = []
    for c in range(NCORES):
        b, q = c // 4, c % 4
        h1 = np.zeros(8, np.float32)
        h2 = np.zeros(8, np.float32)
        if q > 0:
            h1[4 * b: c - 1] = 1.0
            h2[c - 1] = 1.0
        cm = np.zeros((128, 40), np.float32)
        cm[:, 16:24] = h1
        cm[:, 24:32] = h2
        cm[:, 32:40] = 1.0 - h1 - h2
        def ext_of(key, P_, F_=16):
            e = np.zeros((128, F_, 64), np.float32)
            if q > 0:
                e[:, :, 0:32] = a_res[c - 1][key][:, :, T - 32:]
            if q < 3:
                e[:, :, 32:64] = a_res[c + 1][key][:, :, 0:32]
            return e
        uex = ext_of("xconv_out", None)
        dlex = ext_of("delta_out", None)
        resex = ext_of("res_out", None)
        bcex = np.zeros((32, 64), np.float32)
        if q > 0:
            bcex[:, 0:32] = a_res[c - 1]["bc_out"][:, T - 32:]
        if q < 3:
            bcex[:, 32:64] = a_res[c + 1]["bc_out"][:, 0:32]
        t0 = q * T
        xe = np.zeros((D, TEX), np.float32)
        xe[:, HLO:HLO + T] = x[b, t0:t0 + T].T
        if q > 0:
            xe[:, 0:HLO] = x[b, t0 - HLO:t0].T
        if q < 3:
            xe[:, HLO + T:] = x[b, t0 + T:t0 + T + HLO].T
        import ml_dtypes
        xne = np.zeros((128, 8, TEX), dtype=np.asarray(a_res[c]["xn_out"]).dtype)
        xne[:, :, HLO:HLO + T] = a_res[c]["xn_out"]
        if q > 0:
            xne[:, :, 0:HLO] = a_res[c - 1]["xn_out"][:, :, T - HLO:]
        if q < 3:
            xne[:, :, HLO + T:] = a_res[c + 1]["xn_out"][:, :, 0:HLO]
        m = np.zeros((16, 192, 128), np.float32)
        for qt in range(16):
            xext = qt * 128 + np.arange(192)
            valid = np.ones(192, bool)
            if q == 0:
                valid &= xext >= HLO
            if q == 3:
                valid &= xext < HLO + T
            m[qt] = band & valid[:, None]
        maps.append(dict(
            u_in=a_res[c]["xconv_out"], dl_in=a_res[c]["delta_out"],
            res_in=a_res[c]["res_out"], bc_in=a_res[c]["bc_out"],
            uex=uex, dlex=dlex, resex=resex, bcex=bcex,
            alog=alog_l, dvec=_cols(inp["L0_attn_D"]),
            wout=st(inp["L0_attn_W_out"], 16, 8),
            fqa=fqa, cmask=cm, x2fm=_fm(xe), xn_ext=xne,
            w1T=st(tr(inp["L0_ffn_w1_w"]), 8, 32), w2T=st(tr(inp["L0_ffn_w2_w"]), 8, 32),
            h1T=st(tr(inp["L0_ffn_h1_w"]), 32, 32), h2T=st(tr(inp["L0_ffn_h2_w"]), 32, 32),
            w3T=st(tr(inp["L0_ffn_w3_w"]), 32, 8),
            w1s=_cols(inp["L0_ffn_w1_s"]), w2s=_cols(inp["L0_ffn_w2_s"]),
            h1s=_cols(inp["L0_ffn_h1_s"]), h2s=_cols(inp["L0_ffn_h2_s"]),
            w3s=_cols(inp["L0_ffn_w3_s"]),
            wqT=st(tr(inp["L1_attn_Wq"]), 8, 8), wkT=st(tr(inp["L1_attn_Wk"]), 8, 8),
            wvT=st(tr(inp["L1_attn_Wv"]), 8, 8), woT=st(tr(inp["L1_attn_Wo"]), 8, 8),
            bq=_cols(inp["L1_attn_bq"]), bk=_cols(inp["L1_attn_bk"]),
            bvt=np.ascontiguousarray(np.tile(inp["L1_attn_bv"], (128, 1))),
            bo_=_cols(inp["L1_attn_bo"]),
            mask1=np.ascontiguousarray(m[:, 0:128, :]),
            mask2=np.ascontiguousarray(m[:, 128:192, :]),
            gate_w=np.ascontiguousarray(
                inp["L1_ffn_gate_w"].reshape(8, 128, NEXP).transpose(1, 0, 2)).astype(BFH),
            gbt=np.ascontiguousarray(np.tile(inp["L1_ffn_gate_b"], (128, 1))),
            w1e=np.ascontiguousarray(
                inp["L1_ffn_w1"].reshape(NEXP, 8, 128, 32, 128).transpose(0, 3, 2, 1, 4)).astype(BFH),
            b1e=b1c,
            w2e=np.ascontiguousarray(
                inp["L1_ffn_w2"].reshape(NEXP, 32, 128, 8, 128).transpose(0, 3, 2, 1, 4)).astype(BFH),
            b2e=b2c,
            ws1=st(inp["L1_ffn_ws1"], 8, 32), bs1=_cols(inp["L1_ffn_bs1"]),
            ws2=st(inp["L1_ffn_ws2"], 32, 8), bs2=_cols(inp["L1_ffn_bs2"]),
            ident=np.eye(128, dtype=np.float32)))
    return maps


def kernel(**inp):
    inp = {k: np.asarray(v) for k, v in inp.items()}
    x = inp["L0_attn_x"]
    cores = list(range(NCORES))

    maps, alog_l = _maps_A(inp)
    a_res = run_bass_kernel_spmd(build_launchA(), maps, core_ids=cores).results
    l2_res = run_bass_kernel_spmd(build_launch2(), _maps_L2(inp, x, a_res, alog_l),
                                  core_ids=cores).results
    out = np.zeros((2, 4 * T, D), np.float32)
    for c in range(NCORES):
        b, q = c // 4, c % 4
        out[b, q * T:(q + 1) * T] = _unfm(l2_res[c]["x2_out"]).T
    return out


# version 11
# speedup vs baseline: 1.0278x; 1.0278x over previous
import contextlib
import numpy as np

import concourse.bass as bass
import concourse.mybir as mybir
from concourse.bass_utils import run_bass_kernel_spmd

F32 = mybir.dt.float32
BF16 = mybir.dt.bfloat16
AF = mybir.ActivationFunctionType
ALU = mybir.AluOpType
AX = mybir.AxisListType

NCORES = 8
T = 2048
D = 1024
DIN = 2048
NST = 16
DTR = 64
HID = 4096
NH, DH = 16, 64
KWIN, DIL = 8, 4
HLO = KWIN * DIL
NEXP = 8
EPS_LN = 1e-5
EPS_ATT = 1e-6

ENGINES = ("sync", "gpsimd", "scalar", "vector", "tensor")
DBG_D1 = "all"
DBG_QK = 5


class Prog:

    def __init__(self, nc):
        self.nc = nc
        self.ctx = contextlib.ExitStack()
        self._scopes = [self.ctx]
        self._prefix = [""]
        self.q = {e: [] for e in ENGINES}
        self.pg = {}
        self.pgn = {e: 0 for e in ENGINES}
        for e in ENGINES:
            self.pg[e] = self.ctx.enter_context(nc.semaphore(f"pg_{e}"))
        self._cvals = {}
        self._cbuf = self.ctx.enter_context(
            nc.sbuf_tensor("consts", (128, 16), F32))
        self.pgn["vector"] += 16
        self.streams = []

    @contextlib.contextmanager
    def scope(self, name):
        es = contextlib.ExitStack()
        self._scopes.append(es)
        self._prefix.append(name + "_")
        try:
            yield
        finally:
            self._prefix.pop()
            self._scopes.pop()
            es.close()

    def barrier(self):
        targets = [(self.pg[e2], self.pgn[e2]) for e2 in ENGINES
                   if self.pgn[e2] > 0]
        targets += [(s.h, s.n) for s in self.streams if s.n > 0]
        for e in ENGINES:
            for h, v in targets:
                self.q[e].append(lambda en, s=h, v=v: en.wait_ge(s, v))

    def sbuf(self, name, shape, dtype):
        return self._scopes[-1].enter_context(
            self.nc.sbuf_tensor(self._prefix[-1] + name, list(shape), dtype))

    def psum(self, name, shape, dtype=F32):
        return self._scopes[-1].enter_context(
            self.nc.psum_tensor(self._prefix[-1] + name, list(shape), dtype))

    def cf(self, value, parts=128):
        if value not in self._cvals:
            i = len(self._cvals)
            assert i < 16
            self._cvals[value] = i
        return self._cbuf[0:parts, self._cvals[value]:self._cvals[value] + 1]

    def op(self, eng, fn, step=1):
        self.pgn[eng] += step
        h = self.pg[eng]
        def run(e, fn=fn, h=h, step=step):
            fn(e).then_inc(h, step)
        self.q[eng].append(run)
        return self.pgn[eng]

    class _Stream:
        __slots__ = ("h", "n")

    def stream(self, name):
        s = Prog._Stream()
        s.h = self.ctx.enter_context(self.nc.semaphore(name))
        s.n = 0
        self.streams.append(s)
        return s

    def dma(self, eng, out, in_, stream):
        stream.n += 16
        def run(e, o=out, i=in_, h=stream.h):
            e.dma_start(out=o, in_=i).then_inc(h, 16)
        self.q[eng].append(run)
        return stream.n

    def swait(self, eng, stream, val=None):
        v = stream.n if val is None else val
        if v > 0:
            self.q[eng].append(lambda e, s=stream.h, v=v: e.wait_ge(s, v))

    def wait(self, eng, src_eng, val):
        if val <= 0:
            return
        self.q[eng].append(lambda e, s=self.pg[src_eng], v=val: e.wait_ge(s, v))

    def finish(self):
        if True:
            items = dict(self._cvals)
            cbuf = self._cbuf
            h = self.pg["vector"]
            def memsets(e, items=items, cbuf=cbuf, h=h):
                for v, i in items.items():
                    e.memset(cbuf[:, i:i + 1], float(v)).then_inc(h, 1)
                for _ in range(16 - len(items)):
                    e.nop().then_inc(h, 1)
            self.q["vector"].insert(0, memsets)
            for eng in ENGINES:
                if eng != "vector":
                    self.q[eng].insert(0, lambda e, s=h: e.wait_ge(s, 16))
        finals = [(self.pg[e], self.pgn[e]) for e in ENGINES if self.pgn[e] > 0]
        finals += [(s.h, s.n) for s in self.streams if s.n > 0]
        def final_waits(e, finals=finals):
            for s, v in finals:
                e.wait_ge(s, v)
        self.q["sync"].append(final_waits)
        with self.ctx:
            with self.nc.Block() as block:
                for e in ENGINES:
                    fns = self.q[e]
                    def runner(eng, fns=fns):
                        for fn in fns:
                            fn(eng)
                    getattr(block, e)(runner)
        return self.nc


class PsumPool:

    def __init__(self, P, name, n, shape=(128, 1024)):
        self.P = P
        self.n = n
        self.tiles = [P.psum(f"{name}{i}", shape) for i in range(n)]
        self.rel = [None] * n
        self.uses = 0

    def acquire(self):
        i = self.uses % self.n
        self.uses += 1
        if self.rel[i] is not None:
            eng, cnt = self.rel[i]
            self.P.wait("tensor", eng, cnt)
            self.rel[i] = None
        return i, self.tiles[i]

    def release(self, slot, eng, cnt):
        self.rel[slot] = (eng, cnt)


def cdiv(a, b):
    return (a + b - 1) // b


def blocks_of(total, w):
    return [(i, min(w, total - i)) for i in range(0, total, w)]


def ln_stream(P, *, xdram, nd, tcols, out_bf, xblk, sq, stats, ones1, onesc,
              pool, ld_eng="sync", guard0=0, lds=None):
    ndim = nd * 128
    apply_cnt = guard0
    for bi, (t0, nw) in enumerate(blocks_of(tcols, 512)):
        P.wait(ld_eng, "vector", apply_cnt)
        ldv = P.dma(ld_eng, xblk[:, :, 0:nw], xdram[:, :, t0:t0 + nw], lds)
        s1, tile = pool.acquire()
        P.swait("scalar", lds, ldv)
        for d in range(nd):
            sq_c = P.op("scalar",
                        lambda e, o=sq[:, d, 0:nw], i=xblk[:, d, 0:nw]:
                        e.activation(o, i, AF.Square))
        P.swait("tensor", lds, ldv)
        for d in range(nd):
            P.op("tensor",
                 lambda e, o=tile[0:1, 0:nw], l=ones1, r=xblk[:, d, 0:nw],
                        s=(d == 0), p=(d == nd - 1):
                 e.matmul(o, l, r, start=s, stop=p))
        P.wait("tensor", "scalar", sq_c)
        for d in range(nd):
            mm_c = P.op("tensor",
                        lambda e, o=tile[0:1, 512:512 + nw], l=ones1,
                               r=sq[:, d, 0:nw], s=(d == 0), p=(d == nd - 1):
                        e.matmul(o, l, r, start=s, stop=p))
        P.wait("vector", "tensor", mm_c)
        c1 = P.op("vector", lambda e, o=stats(3, nw), i=tile[0:1, 0:nw]:
                  e.tensor_scalar_mul(o, i, 1.0 / ndim))
        c2 = P.op("vector", lambda e, o=stats(1, nw), i=tile[0:1, 512:512 + nw]:
                  e.tensor_scalar_mul(o, i, 1.0 / ndim))
        pool.release(s1, "vector", c2)
        P.wait("vector", "vector", c1)
        c3 = P.op("vector", lambda e, o=stats(0, nw), a=stats(3, nw):
                  e.tensor_mul(o, a, a))
        P.wait("vector", "vector", c3)
        var_c = P.op("vector", lambda e, o=stats(1, nw), a=stats(1, nw),
                     b=stats(0, nw): e.tensor_sub(o, a, b))
        P.wait("scalar", "vector", var_c)
        c4 = P.op("scalar", lambda e, o=stats(2, nw), i=stats(1, nw),
                  b=P.cf(EPS_LN, 1): e.activation(o, i, AF.Ln, bias=b))
        P.wait("scalar", "scalar", c4)
        rstd_c = P.op("scalar", lambda e, o=stats(2, nw), i=stats(2, nw),
                      sc=P.cf(-0.5, 1): e.activation(o, i, AF.Exp, scale=sc))
        s2, tile2 = pool.acquire()
        P.wait("tensor", "scalar", rstd_c)
        P.wait("tensor", "vector", c1)
        P.op("tensor", lambda e, o=tile2[:, 0:nw], l=onesc, r=stats(3, nw):
             e.matmul(o, l, r, start=True, stop=True))
        bc_c = P.op("tensor", lambda e, o=tile2[:, 512:512 + nw], l=onesc,
                    r=stats(2, nw): e.matmul(o, l, r, start=True, stop=True))
        P.wait("vector", "tensor", bc_c)
        for d in range(nd):
            cs = P.op("vector", lambda e, o=sq[:, d, 0:nw], a=xblk[:, d, 0:nw],
                      b=tile2[:, 0:nw]: e.tensor_sub(o, a, b))
            P.wait("vector", "vector", cs)
            apply_cnt = P.op("vector", lambda e, o=out_bf[:, d, t0:t0 + nw],
                             a=sq[:, d, 0:nw], b=tile2[:, 512:512 + nw]:
                             e.tensor_mul(o, a, b))
        pool.release(s2, "vector", apply_cnt)
    return apply_cnt


def emit_A(P, S, t):
    nc = P.nc
    TE = T + 3
    TS = 512
    NSUB = T // TS
    xfm = t["xfm"]; win = t["win"]; convw = t["convw"]; convb = t["convb"]
    wx = t["wx"]; wdt = t["wdt"]; bdt = t["bdt"]; alog = t["alog"]
    xn_out = t["xn_out"]; res_out = t["res_out"]; xconv_out = t["xconv_out"]
    delta_out = t["delta_out"]; bc_out = t["bc_out"]; fq_out = t["fq_out"]

    xnb = P.sbuf("xnb", (128, 8, T), BF16)
    xnh = P.sbuf("xnh", (128, 8, 4), BF16)
    big1 = P.sbuf("big1", (128, 8192), F32)
    xr = P.sbuf("xr", (128, 16, TS + 3), F32)
    xcdu = P.sbuf("xcdu", (128, 16, TS), F32)
    xcb = P.sbuf("xcb", (128, 16, TS), BF16)
    wstrip = P.sbuf("wstrip", (128, 2, 8, 128), BF16)
    res_st = P.sbuf("res_st", (128, 2, TS), F32)
    da_sb = P.sbuf("da", (128, 2, TS), F32)
    acc_sb = P.sbuf("acc", (128, 2, TS), F32)
    scan_sb = P.sbuf("scan", (128, TS), F32)
    state = P.sbuf("state", (128, 256), F32)
    fq_sb = P.sbuf("fq", (128, 1024), F32)
    stats1 = P.sbuf("stats1", (1, 2048), F32)
    ones1 = P.sbuf("ones1", (128, 1), F32)
    onesb = P.sbuf("onesb", (1, 128), F32)
    cw_sb = P.sbuf("cw", (128, 64), F32)
    cb_sb = P.sbuf("cb", (128, 16), F32)
    wx_sb = P.sbuf("wxb", (128, 16, 96), BF16)
    wdt_sb = P.sbuf("wdtb", (64, DIN), BF16)
    bdt_sb = P.sbuf("bdtb", (128, 16), F32)
    alog_sb = P.sbuf("alogb", (128, 256), F32)
    a_sb = P.sbuf("a", (128, 256), F32)
    drw_sb = P.sbuf("drw", (96, TS), F32)
    drwb = P.sbuf("drwb", (64, TS), BF16)
    sd_sb = P.sbuf("sd", (128, 16), F32)
    sdt_sb = P.sbuf("sdt", (128, 16), F32)

    def stat_row(r, nw):
        return stats1[0:1, r * 512:r * 512 + nw]

    dbua_sb = xcb[:].rearrange("p d w -> p (d w)").bitcast(F32) \
        .rearrange("p (d w) -> p d w", d=8)
    xblk = big1[:].rearrange("p (a d w) -> p a d w", a=2, d=8)[:, 0]
    sqb = big1[:].rearrange("p (a d w) -> p a d w", a=2, d=8)[:, 1]
    bb_sb = big1[:].rearrange("p (d w) -> p d w", d=16)
    dl_sb = xr[:, :, 3:]

    pool = S["pool"]
    sSm = S["SM"]; sGm = S["GM"]; sLn = S["LN"]
    sWinP = [S["G0"], S["G1"]]; sSt = S["ST"]; sBb = S["BB"]

    P.dma("sync", cw_sb[:], convw, sSm)
    P.dma("sync", cb_sb[:], convb, sSm)
    P.dma("sync", bdt_sb[:], bdt, sSm)
    P.dma("sync", alog_sb[:], alog, sSm)
    ld_small = P.dma("sync", alog_sb[:], alog, sSm) if False else sSm.n
    P.dma("gpsimd", wx_sb[:], wx.rearrange("(kt p) o -> p kt o", p=128), sGm)
    gld_small = P.dma("gpsimd", wdt_sb[:], wdt, sGm)

    P.op("vector", lambda e: e.memset(ones1[:], 1.0))
    P.op("vector", lambda e: e.memset(onesb[:], 1.0))
    P.op("vector", lambda e: e.memset(state[:], 0.0))
    init_v = P.op("vector", lambda e: e.memset(sd_sb[:], 0.0))
    P.wait("tensor", "vector", init_v)

    P.swait("scalar", sSm, sSm.n)
    aexp_c = P.op("scalar", lambda e: e.activation(a_sb[:], alog_sb[:], AF.Exp))
    P.wait("vector", "scalar", aexp_c)
    a_ready = P.op("vector", lambda e: e.tensor_scalar_mul(a_sb[:], a_sb[:], -1.0))

    lnm_c = ln_stream(P, xdram=xfm[:, :, 3:], nd=8, tcols=T, out_bf=xnb[:],
              xblk=xblk, sq=sqb, stats=stat_row, ones1=ones1[:], onesc=onesb[:],
              pool=pool, lds=sLn)
    ln_stream(P, xdram=xfm[:, :, 0:3], nd=8, tcols=3, out_bf=xnh[:, :, 0:3],
              xblk=xblk, sq=sqb, stats=stat_row, ones1=ones1[:],
              onesc=onesb[:], pool=pool, guard0=lnm_c, lds=sLn)
    xn_v = P.pgn["vector"]
    P.wait("sync", "vector", xn_v)
    P.dma("sync", xn_out, xnb[:], sSt)

    win_r = win.rearrange("(kt p) o -> p kt o", p=128)

    strip_users = []
    res_sts = []
    scan_cs = []
    nwin = 0
    P.wait("tensor", "vector", xn_v)
    for s in range(NSUB):
        t0 = s * TS
        if s > 0:
            P.swait("vector", sSt)
            P.swait("scalar", sSt)
            P.wait("vector", "scalar", P.pgn["scalar"])
        for ot in range(32):
            is_x = ot < 16
            sbx = nwin % 2
            if nwin >= 2:
                P.wait("gpsimd", "tensor", strip_users[nwin - 2])
            wld = P.dma("gpsimd", wstrip[:, sbx],
                        win_r[:, :, ot * 128:(ot + 1) * 128], sWinP[sbx])
            nwin += 1
            s1, tile = pool.acquire()
            P.swait("tensor", sWinP[sbx], wld)
            for kt in range(8):
                mm_c = P.op("tensor",
                            lambda e, o=tile[:, 0:TS], l=wstrip[:, sbx, kt, :],
                                   r=xnb[:, kt, t0:t0 + TS], s_=(kt == 0), p=(kt == 7):
                            e.matmul(o, l, r, start=s_, stop=p))
            if is_x:
                hsrc = xnh if s == 0 else xnb
                hof = 0 if s == 0 else t0 - 3
                for kt in range(8):
                    mm_c = P.op("tensor",
                                lambda e, o=tile[:, 512:515], l=wstrip[:, sbx, kt, :],
                                       r=hsrc[:, kt, hof:hof + 3], s_=(kt == 0), p=(kt == 7):
                                e.matmul(o, l, r, start=s_, stop=p, skip_group_check=True))
            strip_users.append(mm_c)
            P.wait("vector", "tensor", mm_c)
            if is_x:
                P.op("vector", lambda e, o=xr[:, ot, 3:3 + TS], i=tile[:, 0:TS]:
                     e.tensor_copy(o, i))
                ev_c = P.op("vector", lambda e, o=xr[:, ot, 0:3], i=tile[:, 512:515]:
                            e.tensor_copy(o, i))
            else:
                r = ot - 16
                if len(res_sts) >= 2:
                    P.swait("vector", sSt)
                ev_c = P.op("vector", lambda e, o=res_st[:, r % 2, :], i=tile[:, 0:TS]:
                            e.tensor_copy(o, i))
                P.wait("sync", "vector", ev_c)
                res_sts.append(P.dma("sync", res_out[:, r, s * TS:(s + 1) * TS],
                                     res_st[:, r % 2, :], sSt))
            pool.release(s1, "vector", ev_c)

        P.wait("vector", "vector", P.pgn["vector"])
        if s == 0:
            P.swait("vector", sSm, sSm.n)
        for d in range(16):
            c0 = P.op("vector", lambda e, o=acc_sb[:, 0, :], a=xr[:, d, 0:TS],
                      s1=cw_sb[:, d * 4:d * 4 + 1]: e.tensor_scalar_mul(o, a, s1))
            for j in (1, 2, 3):
                P.wait("vector", "vector", c0)
                c0 = P.op("vector",
                          lambda e, o=acc_sb[:, j % 2, :], a=xr[:, d, j:j + TS],
                                 s1=cw_sb[:, d * 4 + j:d * 4 + j + 1],
                                 b=acc_sb[:, (j - 1) % 2, :]:
                          e.scalar_tensor_tensor(o, a, s1, b, ALU.mult, ALU.add))
            P.wait("vector", "vector", c0)
            conv_c = P.op("vector", lambda e, o=xcdu[:, d, :], i=acc_sb[:, 1, :]:
                          e.tensor_copy(o, i))
        P.wait("scalar", "vector", conv_c)
        for d in range(16):
            silu_c = P.op("scalar", lambda e, o=xcdu[:, d, :], i=xcdu[:, d, :],
                          b=cb_sb[:, d:d + 1]: e.activation(o, i, AF.Silu, bias=b))
        P.wait("vector", "scalar", silu_c)
        for d in range(16):
            xcb_c = P.op("vector", lambda e, o=xcb[:, d, :], i=xcdu[:, d, :]:
                         e.tensor_copy(o, i))
        P.wait("sync", "scalar", silu_c)
        xc_st = P.dma("sync", xconv_out[:, :, s * TS:(s + 1) * TS], xcdu[:], sSt)

        P.wait("tensor", "vector", xcb_c)
        if s == 0:
            P.swait("tensor", sGm, gld_small)
        s1, tile = pool.acquire()
        for kt in range(16):
            mm_c = P.op("tensor", lambda e, o=tile[0:96, 0:TS], l=wx_sb[:, kt, :],
                        r=xcb[:, kt, :], s_=(kt == 0), p=(kt == 15):
                        e.matmul(o, l, r, start=s_, stop=p))
        P.wait("vector", "tensor", mm_c)
        drw_c = P.op("vector", lambda e, o=drw_sb[:, :], i=tile[0:96, 0:TS]:
                     e.tensor_copy(o, i))
        pool.release(s1, "vector", drw_c)
        P.wait("vector", "vector", drw_c)
        drb_c = P.op("vector", lambda e, o=drwb[:], i=drw_sb[0:64, :]:
                     e.tensor_copy(o, i))
        P.wait("sync", "vector", drw_c)
        bc_st = P.dma("sync", bc_out[:, s * TS:(s + 1) * TS], drw_sb[64:96, :], sSt)

        P.wait("tensor", "vector", drb_c)
        dl_c = 0
        for ot in range(16):
            s1, tile = pool.acquire()
            mm_c = P.op("tensor", lambda e, o=tile[:, 0:TS],
                        l=wdt_sb[:, ot * 128:(ot + 1) * 128], r=drwb[:]:
                        e.matmul(o, l, r, start=True, stop=True))
            P.wait("scalar", "tensor", mm_c)
            if dl_c:
                P.wait("scalar", "vector", dl_c)
            c1 = P.op("scalar", lambda e, o=acc_sb[:, 0, :], i=tile[:, 0:TS],
                      b=bdt_sb[:, ot:ot + 1]: e.activation(o, i, AF.Abs, bias=b))
            P.wait("scalar", "scalar", c1)
            c2 = P.op("scalar", lambda e, o=acc_sb[:, 0, :], i=acc_sb[:, 0, :],
                      sc=P.cf(-1.0, 128): e.activation(o, i, AF.Exp, scale=sc))
            P.wait("scalar", "scalar", c2)
            c3 = P.op("scalar", lambda e, o=acc_sb[:, 0, :], i=acc_sb[:, 0, :],
                      b=P.cf(1.0, 128): e.activation(o, i, AF.Ln, bias=b))
            sp_c = P.op("scalar", lambda e, o=acc_sb[:, 1, :], i=tile[:, 0:TS],
                        b=bdt_sb[:, ot:ot + 1]: e.activation(o, i, AF.Relu, bias=b))
            pool.release(s1, "scalar", sp_c)
            P.wait("vector", "scalar", sp_c)
            P.wait("vector", "scalar", c3)
            dl_c = P.op("vector", lambda e, o=dl_sb[:, ot, :], a=acc_sb[:, 0, :],
                        b=acc_sb[:, 1, :]: e.tensor_add(o, a, b))
        P.wait("sync", "vector", dl_c)
        P.dma("sync", delta_out[:, :, s * TS:(s + 1) * TS], dl_sb, sSt)

        P.swait("vector", sSt)
        for d in range(16):
            du_c = P.op("vector", lambda e, o=xcdu[:, d, :], a=dl_sb[:, d, :],
                        b=xcdu[:, d, :]: e.tensor_mul(o, a, b))
        sdt_c = P.op("vector", lambda e, o=sdt_sb[:], i=dl_sb:
                     e.reduce_sum(o, i, axis=AX.X))
        P.wait("vector", "vector", sdt_c)
        sd_c = P.op("vector", lambda e: e.tensor_add(sd_sb[:], sd_sb[:], sdt_sb[:]))

        P.swait("sync", sSt)
        bb_src = bass.AP(bc_out.tensor, s * TS, [[0, 128], [T, 16], [1, TS]])
        bb_ld = P.dma("sync", bb_sb, bb_src, sBb)

        P.wait("scalar", "vector", dl_c)
        if s == 0:
            P.wait("scalar", "vector", a_ready)
        P.swait("vector", sBb, bb_ld)
        P.wait("vector", "vector", du_c)
        P.wait("vector", "tensor", P.pgn["tensor"])
        for d in range(16):
            if d >= 1:
                P.wait("vector", "vector", scan_cs[-1])
            dba_c = P.op("vector", lambda e, o=dbua_sb[:],
                         a=bass.AP(xcdu, d * TS, [[16 * TS, 128], [0, 8], [1, TS]]),
                         b=bb_sb[:, 0:8, :]: e.tensor_mul(o, a, b))
            for n in range(NST):
                ci = d * NST + n
                cg = s * 256 + ci
                buf = cg % 2
                if len(scan_cs) >= 2:
                    P.wait("scalar", "vector", scan_cs[-2])
                da_c = P.op("scalar", lambda e, o=da_sb[:, buf, :], i=dl_sb[:, d, :],
                            sc=a_sb[:, ci:ci + 1]:
                            e.activation(o, i, AF.Exp, scale=sc))
                if n == 8:
                    P.wait("vector", "vector", scan_cs[-1])
                    dba_c = P.op("vector", lambda e, o=dbua_sb[:], d=d,
                                 a=bass.AP(xcdu, d * TS, [[16 * TS, 128], [0, 8], [1, TS]]),
                                 b=bb_sb[:, 8:16, :]: e.tensor_mul(o, a, b))
                P.wait("vector", "scalar", da_c)
                if n in (0, 8):
                    P.wait("vector", "vector", dba_c)
                sc_c = P.op("vector", lambda e, o=scan_sb[:], a=da_sb[:, buf, :],
                            b=dbua_sb[:, n % 8, :], ini=state[:, ci:ci + 1]:
                            e.tensor_tensor_scan(o, a, b, ini, ALU.mult, ALU.add))
                P.wait("vector", "vector", sc_c)
                if s == NSUB - 1:
                    P.op("vector", lambda e, o=fq_sb[:, 512 + ci:513 + ci],
                         i=scan_sb[:, 479:480]: e.tensor_copy(o, i))
                scan_cs.append(P.op("vector", lambda e, o=state[:, ci:ci + 1],
                                    i=scan_sb[:, TS - 1:TS]: e.tensor_copy(o, i)))

    P.wait("vector", "vector", scan_cs[-1])
    P.op("vector", lambda e: e.tensor_copy(fq_sb[:, 0:256], state[:]))
    P.wait("vector", "vector", sd_c)
    t_c = P.op("vector", lambda e: e.reduce_sum(sdt_sb[:], dl_sb[:, :, 480:512],
                                                axis=AX.X))
    P.wait("vector", "vector", t_c)
    s2_c = P.op("vector", lambda e: e.tensor_sub(sdt_sb[:], sd_sb[:], sdt_sb[:]))
    for d in range(16):
        q_c = P.op("vector", lambda e, o=fq_sb[:, 256 + d * 16:256 + (d + 1) * 16],
                   a=a_sb[:, d * 16:(d + 1) * 16], s1=sd_sb[:, d:d + 1]:
                   e.tensor_scalar_mul(o, a, s1))
    P.wait("vector", "vector", s2_c)
    for d in range(16):
        q_c = P.op("vector", lambda e, o=fq_sb[:, 768 + d * 16:768 + (d + 1) * 16],
                   a=a_sb[:, d * 16:(d + 1) * 16], s1=sdt_sb[:, d:d + 1]:
                   e.tensor_scalar_mul(o, a, s1))
    P.wait("scalar", "vector", q_c)
    qe_c = P.op("scalar", lambda e: e.activation(fq_sb[:, 256:512],
                fq_sb[:, 256:512], AF.Exp))
    qe_c = P.op("scalar", lambda e: e.activation(fq_sb[:, 768:1024],
                fq_sb[:, 768:1024], AF.Exp))
    P.wait("sync", "scalar", qe_c)
    P.wait("sync", "vector", P.pgn["vector"])
    P.dma("sync", fq_out, fq_sb[:], sSt)
    return



def mk_streams(P):
    return {"pool": PsumPool(P, "ps", 4),
            "SM": P.stream("sSM"), "LN": P.stream("sLN"), "IO": P.stream("sIO"),
            "BB": P.stream("sBB"), "RES": P.stream("sRES"), "ST": P.stream("sST"),
            "GM": P.stream("sGM"), "G0": P.stream("sG0"), "G1": P.stream("sG1"),
            "BB2": P.stream("sBB2"),
            "CC": P.stream("sCC")}


def _dram_io_A(nc):
    TE = T + 3
    t = {}
    t["xfm"] = nc.dram_tensor("xfm", [128, 8, TE], F32, kind="ExternalInput").ap()
    t["win"] = nc.dram_tensor("win", [1024, 2 * DIN], F32, kind="ExternalInput").ap()
    t["convw"] = nc.dram_tensor("convw", [128, 64], F32, kind="ExternalInput").ap()
    t["convb"] = nc.dram_tensor("convb", [128, 16], F32, kind="ExternalInput").ap()
    t["wx"] = nc.dram_tensor("wx", [DIN, 96], F32, kind="ExternalInput").ap()
    t["wdt"] = nc.dram_tensor("wdt", [DTR, DIN], F32, kind="ExternalInput").ap()
    t["bdt"] = nc.dram_tensor("bdt", [128, 16], F32, kind="ExternalInput").ap()
    t["alog"] = nc.dram_tensor("alog", [128, 256], F32, kind="ExternalInput").ap()
    return t


def build_launchA():
    nc = bass.Bass("TRN2", target_bir_lowering=False)
    P = Prog(nc)
    t = _dram_io_A(nc)
    t["xn_out"] = nc.dram_tensor("xn_out", [128, 8, T], BF16, kind="ExternalOutput").ap()
    t["res_out"] = nc.dram_tensor("res_out", [128, 16, T], F32, kind="ExternalOutput").ap()
    t["xconv_out"] = nc.dram_tensor("xconv_out", [128, 16, T], F32, kind="ExternalOutput").ap()
    t["delta_out"] = nc.dram_tensor("delta_out", [128, 16, T], F32, kind="ExternalOutput").ap()
    t["bc_out"] = nc.dram_tensor("bc_out", [32, T], F32, kind="ExternalOutput").ap()
    t["fq_out"] = nc.dram_tensor("fq_out", [128, 1024], F32, kind="ExternalOutput").ap()
    S = mk_streams(P)
    emit_A(P, S, t)
    return P.finish()

def emit_B2(P, S, t):
    nc = P.nc
    W = [544, 512, 512, 544]
    E = [0, 544, 1056, 1568]
    MB = 544
    u_in = t["u_in"]; dl_in = t["dl_in"]; bc_in = t["bc_in"]; res_in = t["res_in"]
    uex = t["uex"]; dlex = t["dlex"]; resex = t["resex"]; bcex = t["bcex"]
    alog = t["alog"]; dvec = t["dvec"]; wout = t["wout"]; fqa = t["fqa"]
    cmask = t["cmask"]; mam_out = t["mam_out"]

    u_sb = P.sbuf("u", (128, 16, MB), F32)
    dl_sb = P.sbuf("dl", (128, 16, MB), F32)
    du_sb = P.sbuf("du", (128, 16, MB), F32)
    bcb = P.sbuf("bcb", (128, 2, 2, MB), F32)
    y_sb = P.sbuf("y", (128, 16, MB), F32)
    da_sb = P.sbuf("da", (128, 2, MB), F32)
    dbu_sb = P.sbuf("dbu", (128, 16, MB), F32)
    state = P.sbuf("state", (128, 256), F32)
    BCPW = 4 * MB

    def bcb_off(pr, bc):
        return pr * 2 * MB + bc * MB
    alog_sb = P.sbuf("alogb", (128, 256), F32)
    a_sb = P.sbuf("a", (128, 256), F32)
    dv_sb = P.sbuf("dv", (128, 16), F32)
    cm_sb = P.sbuf("cm", (128, 40), F32)
    wo_strip = P.sbuf("wo", (128, 1, 16, 128), BF16)
    _duf = du_sb[:].rearrange("p d w -> p (d w)")
    fq_sb = _duf[:, 0:1024]
    qe_sb = _duf[:, 1024:1280]
    fe_sb = _duf[:, 1280:1536]
    pool = S["pool"]
    sSm = S["SM"]; sFq = S["IO"]; sLd = S["LN"]; sRes = S["RES"]
    sBcP = [S["BB"], S["BB2"]]; sWoP = [S["G0"], S["G0"]]; sSt = S["ST"]

    yb = du_sb[:].rearrange("p d w -> p (d w)")[:, 0:8 * MB].bitcast(BF16) \
        .rearrange("p (d w) -> p d w", d=16)
    mo = du_sb[:, 8:16]

    P.dma("sync", alog_sb[:], alog, sSm)
    P.dma("sync", dv_sb[:], dvec, sSm)
    sm_c = P.dma("sync", cm_sb[:], cmask, sSm)

    P.op("vector", lambda e: e.memset(state[:], 0.0))
    P.swait("scalar", sSm, sm_c)
    aexp_c = P.op("scalar", lambda e: e.activation(a_sb[:], alog_sb[:], AF.Exp))
    P.wait("vector", "scalar", aexp_c)
    a_ready = P.op("vector", lambda e: e.tensor_scalar_mul(a_sb[:], a_sb[:], -1.0))

    P.swait("vector", sSm, sm_c)
    fqv_c = 0
    for j in range(NCORES):
        if j > 0:
            P.wait("sync", "vector", fqv_c)
        fq_c = P.dma("sync", fq_sb, fqa[j], sFq)
        P.swait("vector", sFq, fq_c)
        c1 = P.op("vector", lambda e, j=j: e.tensor_scalar_mul(
            qe_sb, fq_sb[:, 256:512], cm_sb[:, 16 + j:17 + j]))
        P.wait("vector", "vector", c1)
        c2 = P.op("vector", lambda e, j=j: e.scalar_tensor_tensor(
            qe_sb, fq_sb[:, 768:1024], cm_sb[:, 24 + j:25 + j], qe_sb,
            ALU.mult, ALU.add))
        P.wait("vector", "vector", c2)
        c3 = P.op("vector", lambda e, j=j: e.tensor_scalar_add(
            qe_sb, qe_sb, cm_sb[:, 32 + j:33 + j]))
        c4 = P.op("vector", lambda e, j=j: e.tensor_scalar_mul(
            fe_sb, fq_sb[:, 0:256], cm_sb[:, 16 + j:17 + j]))
        P.wait("vector", "vector", c4)
        c5 = P.op("vector", lambda e, j=j: e.scalar_tensor_tensor(
            fe_sb, fq_sb[:, 512:768], cm_sb[:, 24 + j:25 + j], fe_sb,
            ALU.mult, ALU.add))
        P.wait("vector", "vector", c5)
        c6 = P.op("vector", lambda e: e.tensor_mul(state[:], state[:], qe_sb))
        P.wait("vector", "vector", c6)
        fqv_c = P.op("vector", lambda e: e.tensor_add(state[:], state[:], fe_sb))

    scan_cs = []
    y_c = 0
    wo_users = [0, 0]
    stp_l = None
    for s in range(4):
        w = W[s]
        m0 = E[s] - 32
        if s > 0:
            P.swait("sync", sSt)
            P.wait("sync", "vector", P.pgn["vector"])
            P.wait("sync", "scalar", P.pgn["scalar"])
            P.wait("sync", "gpsimd", P.pgn["gpsimd"])
            P.wait("gpsimd", "vector", P.pgn["vector"])
            P.wait("vector", "gpsimd", P.pgn["gpsimd"])
        if s == 0:
            P.dma("sync", u_sb[:, :, 0:32], uex[:, :, 0:32], sLd)
            P.dma("sync", u_sb[:, :, 32:544], u_in[:, :, 0:512], sLd)
            P.dma("sync", dl_sb[:, :, 0:32], dlex[:, :, 0:32], sLd)
            ld_c = P.dma("sync", dl_sb[:, :, 32:544], dl_in[:, :, 0:512], sLd)
        elif s == 3:
            P.dma("sync", u_sb[:, :, 0:512], u_in[:, :, 1536:2048], sLd)
            P.dma("sync", u_sb[:, :, 512:544], uex[:, :, 32:64], sLd)
            P.dma("sync", dl_sb[:, :, 0:512], dl_in[:, :, 1536:2048], sLd)
            ld_c = P.dma("sync", dl_sb[:, :, 512:544], dlex[:, :, 32:64], sLd)
        else:
            P.dma("sync", u_sb[:, :, 0:512], u_in[:, :, m0:m0 + 512], sLd)
            ld_c = P.dma("sync", dl_sb[:, :, 0:512], dl_in[:, :, m0:m0 + 512], sLd)
        def bc_load(n, pr, s=s, m0=m0):
            sBc = sBcP[pr]
            if s == 0:
                P.dma("sync", bcb[:, pr, :, 0:32],
                      bass.AP(bcex.tensor, n * 64, [[0, 128], [16 * 64, 2], [1, 32]]), sBc)
                return P.dma("sync", bcb[:, pr, :, 32:544],
                             bass.AP(bc_in.tensor, n * T, [[0, 128], [16 * T, 2], [1, 512]]), sBc)
            elif s == 3:
                P.dma("sync", bcb[:, pr, :, 0:512],
                      bass.AP(bc_in.tensor, n * T + 1536, [[0, 128], [16 * T, 2], [1, 512]]), sBc)
                return P.dma("sync", bcb[:, pr, :, 512:544],
                             bass.AP(bcex.tensor, n * 64 + 32, [[0, 128], [16 * 64, 2], [1, 32]]), sBc)
            else:
                return P.dma("sync", bcb[:, pr, :, 0:512],
                             bass.AP(bc_in.tensor, n * T + m0, [[0, 128], [16 * T, 2], [1, 512]]), sBc)

        P.swait("vector", sLd, ld_c)
        if s == 0:
            P.wait("vector", "vector", fqv_c)
            stp_l = [fqv_c] * NST
        du_c = P.op("vector", lambda e, o=du_sb[:, :, 0:w], a=dl_sb[:, :, 0:w],
                    b=u_sb[:, :, 0:w]: e.tensor_mul(o, a, b))
        P.wait("vector", "vector", du_c)
        for d in range(16):
            ud_c = P.op("vector", lambda e, o=u_sb[:, d, 0:w], a=u_sb[:, d, 0:w],
                        s1=dv_sb[:, d:d + 1]: e.tensor_scalar_mul(o, a, s1))
        P.swait("scalar", sLd, ld_c)
        if s == 0:
            P.wait("scalar", "vector", fqv_c)

        NIT = 16 * NST
        da_l = [0] * (NIT + 1)
        glob0 = s * NIT

        def issue_da(i):
            n, d = divmod(i, 16)
            ci = d * NST + n
            b = (glob0 + i) % 2
            if glob0 + i >= 2:
                P.wait("scalar", "vector", scan_cs[glob0 + i - 2])
            da_l[i] = P.op("scalar", lambda e, o=da_sb[:, b, 0:w],
                           i_=dl_sb[:, d, 0:w], sc=a_sb[:, ci:ci + 1]:
                           e.activation(o, i_, AF.Exp, scale=sc))

        bcl = [0] * (NST + 1)
        bcl[0] = bc_load(0, 0)
        issue_da(0)
        GD = 7
        zt_l = [0] * NST
        ztv_l = [0] * NST
        y_l = [0] * NST
        yv_l = [0] * NST
        stc_l = [0] * NST
        DPW = 16 * MB
        bufs = [dbu_sb, u_sb]
        for n in range(NST):
            pr = n % 2
            dbuB = bufs[n % 2]
            if n + 1 < NST:
                if n >= 1:
                    P.wait("sync", "gpsimd", zt_l[n - 1])
                    P.wait("sync", "vector", ztv_l[n - 1])
                bcl[n + 1] = bc_load(n + 1, (n + 1) % 2)
            P.swait("gpsimd", sBcP[pr], bcl[n])
            P.wait("gpsimd", "vector", du_c)
            if n == 1:
                P.wait("gpsimd", "gpsimd", y_l[0])
                P.wait("gpsimd", "vector", yv_l[0])
            elif n >= 2:
                P.wait("gpsimd", "gpsimd", y_l[n - 2])
                P.wait("gpsimd", "vector", yv_l[n - 2])
            dball = P.op("gpsimd", lambda e, o=dbuB[:, :, 0:w],
                         a=du_sb[:, :, 0:w],
                         b_=bass.AP(bcb, bcb_off(pr, 0),
                                    [[BCPW, 128], [0, 16], [1, w]]):
                         e.tensor_mul(o, a, b_))
            P.wait("vector", "gpsimd", dball)
            P.wait("vector", "vector", stp_l[n])
            for d in range(16):
                i = n * 16 + d
                ci = d * NST + n
                if i + 1 < NIT:
                    issue_da(i + 1)
                P.wait("vector", "scalar", da_l[i])
                sc_c = P.op("vector", lambda e, o=dbuB[:, d, 0:w],
                            a=da_sb[:, (glob0 + i) % 2, 0:w],
                            b_=dbuB[:, d, 0:w], ini=state[:, ci:ci + 1]:
                            e.tensor_tensor_scan(o, a, b_, ini, ALU.mult, ALU.add))
                scan_cs.append(sc_c)
            P.wait("vector", "vector", sc_c)
            stc_l[n] = P.op("vector", lambda e,
                            o=bass.AP(state, n, [[256, 128], [NST, 16]]),
                            i_=bass.AP(dbuB, w - 1, [[DPW, 128], [MB, 16]]):
                            e.tensor_copy(o, i_))
            P.wait("gpsimd", "vector", sc_c)
            P.wait("gpsimd", "vector", stc_l[n])
            zt_l[n] = P.op("gpsimd", lambda e, o=dbuB[:, 0:GD, 0:w],
                           a=dbuB[:, 0:GD, 0:w],
                           b_=bass.AP(bcb, bcb_off(pr, 1),
                                      [[BCPW, 128], [0, GD], [1, w]]):
                           e.tensor_mul(o, a, b_))
            P.wait("vector", "vector", stc_l[n])
            ztv_l[n] = P.op("vector", lambda e, o=dbuB[:, GD:16, 0:w],
                            a=dbuB[:, GD:16, 0:w],
                            b_=bass.AP(bcb, bcb_off(pr, 1),
                                       [[BCPW, 128], [0, 16 - GD], [1, w]]):
                            e.tensor_mul(o, a, b_))
            P.wait("gpsimd", "gpsimd", zt_l[n])
            P.wait("vector", "vector", ztv_l[n])
            if n == 0:
                P.wait("gpsimd", "vector", ud_c)
                P.wait("vector", "vector", ud_c)
                y_l[0] = P.op("gpsimd", lambda e, o=y_sb[:, 0:GD, 0:w],
                              a=dbuB[:, 0:GD, 0:w], b_=u_sb[:, 0:GD, 0:w]:
                              e.tensor_add(o, a, b_))
                yv_l[0] = P.op("vector", lambda e, o=y_sb[:, GD:16, 0:w],
                               a=dbuB[:, GD:16, 0:w], b_=u_sb[:, GD:16, 0:w]:
                               e.tensor_add(o, a, b_))
            else:
                y_l[n] = P.op("gpsimd", lambda e, o=y_sb[:, 0:GD, 0:w],
                              a=y_sb[:, 0:GD, 0:w], b_=dbuB[:, 0:GD, 0:w]:
                              e.tensor_add(o, a, b_))
                yv_l[n] = P.op("vector", lambda e, o=y_sb[:, GD:16, 0:w],
                               a=y_sb[:, GD:16, 0:w], b_=dbuB[:, GD:16, 0:w]:
                               e.tensor_add(o, a, b_))
        y_c = y_l[NST - 1]
        yv_c = yv_l[NST - 1]
        stp_l = stc_l

        P.wait("sync", "gpsimd", y_c)
        P.wait("sync", "vector", yv_c)
        if s == 0:
            P.dma("sync", u_sb[:, :, 0:32], resex[:, :, 0:32], sRes)
            res_c = P.dma("sync", u_sb[:, :, 32:544], res_in[:, :, 0:512], sRes)
        elif s == 3:
            P.dma("sync", u_sb[:, :, 0:512], res_in[:, :, 1536:2048], sRes)
            res_c = P.dma("sync", u_sb[:, :, 512:544], resex[:, :, 32:64], sRes)
        else:
            res_c = P.dma("sync", u_sb[:, :, 0:512], res_in[:, :, m0:m0 + 512], sRes)
        P.swait("scalar", sRes, res_c)
        for d in range(16):
            sil_c = P.op("scalar", lambda e, o=u_sb[:, d, 0:w], i=u_sb[:, d, 0:w]:
                         e.activation(o, i, AF.Silu))
        P.wait("vector", "scalar", sil_c)
        P.wait("vector", "gpsimd", y_c)
        P.wait("vector", "vector", yv_c)
        for d in range(16):
            yb_c = P.op("vector", lambda e, o=yb[:, d, 0:w], a=y_sb[:, d, 0:w],
                        b=u_sb[:, d, 0:w]: e.tensor_mul(o, a, b))

        for ot in range(8):
            sbx = 0
            if wo_users[sbx]:
                P.wait("sync", "tensor", wo_users[sbx])
            wo_c = P.dma("sync", wo_strip[:, sbx], wout[ot], sWoP[sbx])
            s1, tile = pool.acquire()
            P.swait("tensor", sWoP[sbx], wo_c)
            if ot == 0:
                P.wait("tensor", "vector", yb_c)
            for bi, (n0, nw) in enumerate(blocks_of(w, 512)):
                for kt in range(16):
                    mm_c = P.op("tensor", lambda e, o=tile[:, bi * 512:bi * 512 + nw],
                                l=wo_strip[:, sbx, kt, :], r=yb[:, kt, n0:n0 + nw],
                                s_=(kt == 0), p=(kt == 15):
                                e.matmul(o, l, r, start=s_, stop=p))
            wo_users[sbx] = mm_c
            P.wait("scalar", "tensor", mm_c)
            for bi, (n0, nw) in enumerate(blocks_of(w, 512)):
                ev_c = P.op("scalar", lambda e, o=mo[:, ot, n0:n0 + nw],
                            i=tile[:, bi * 512:bi * 512 + nw]: e.copy(o, i))
            pool.release(s1, "scalar", ev_c)
        P.wait("sync", "scalar", ev_c)
        P.dma("sync", mam_out[:, :, E[s]:E[s] + w], mo[:, :, 0:w], sSt)
    return


def build_launchB():
    nc = bass.Bass("TRN2", target_bir_lowering=False)
    P = Prog(nc)
    t = {}
    t["u_in"] = nc.dram_tensor("u_in", [128, 16, T], F32, kind="ExternalInput").ap()
    t["dl_in"] = nc.dram_tensor("dl_in", [128, 16, T], F32, kind="ExternalInput").ap()
    t["bc_in"] = nc.dram_tensor("bc_in", [32, T], F32, kind="ExternalInput").ap()
    t["res_in"] = nc.dram_tensor("res_in", [128, 16, T], F32, kind="ExternalInput").ap()
    t["alog"] = nc.dram_tensor("alog", [128, 256], F32, kind="ExternalInput").ap()
    t["dvec"] = nc.dram_tensor("dvec", [128, 16], F32, kind="ExternalInput").ap()
    t["wout"] = nc.dram_tensor("wout", [DIN, D], F32, kind="ExternalInput").ap()
    t["fqa"] = nc.dram_tensor("fqa", [8, 128, 512], F32, kind="ExternalInput").ap()
    t["cmask"] = nc.dram_tensor("cmask", [128, 16], F32, kind="ExternalInput").ap()
    t["mam_out"] = nc.dram_tensor("mam_out", [128, 8, T], F32, kind="ExternalOutput").ap()
    S = mk_streams(P)
    emit_B(P, S, t)
    return P.finish()

def emit_C(P, S, t):
    nc = P.nc
    X = t["xn_in"].shape[2]
    if X == 2112:
        SUB = [(0, 1056), (1056, 1056)]
    else:
        SUB = [(0, 1024), (1024, 1024)]
    MB = max(w for _, w in SUB)
    xn_in = t["xn_in"]; x_in = t["x_in"]; mam_in = t["mam_in"]
    w1T = t["w1T"]; w2T = t["w2T"]; h1T = t["h1T"]; h2T = t["h2T"]; w3T = t["w3T"]
    w1s = t["w1s"]; w2s = t["w2s"]; h1s = t["h1s"]; h2s = t["h2s"]; w3s = t["w3s"]
    x1_out = t["x1_out"]

    xnb = P.sbuf("xnb", (128, 8, MB), BF16)
    ha = P.sbuf("ha", (128, 32, MB), BF16)
    hb = P.sbuf("hb", (128, 32, MB), BF16)
    wst = P.sbuf("wst", (128, 2, 32, 128), BF16)
    fx = P.sbuf("fx", (128, 8, 544), F32)
    fm = P.sbuf("fm", (128, 8, 544), F32)
    t2_sb = P.sbuf("t2s", (128, 544), F32)
    s1_sb = P.sbuf("s1s", (128, 32), F32)
    s2_sb = P.sbuf("s2s", (128, 32), F32)
    s3_sb = P.sbuf("s3s", (128, 32), F32)
    s4_sb = P.sbuf("s4s", (128, 32), F32)
    s5_sb = P.sbuf("s5s", (128, 8), F32)
    pool = S["pool"]
    sSm = S["SM"]; sXn = S["LN"]; sIo = S["IO"]
    sWP = [S["G0"], S["G1"]]; sSt = S["ST"]

    P.dma("sync", s1_sb[:], w1s, sSm)
    P.dma("sync", s2_sb[:], w2s, sSm)
    P.dma("sync", s3_sb[:], h1s, sSm)
    P.dma("sync", s4_sb[:], h2s, sSm)
    sm_c = P.dma("sync", s5_sb[:], w3s, sSm)

    wu = [0, 0]
    h_c = 0
    ev_c = 0
    for (t0, w) in SUB:
        NT = blocks_of(w, 512)
        if ev_c:
            P.wait("sync", "tensor", P.pgn["tensor"])
        xn_c = P.dma("sync", xnb[:, :, 0:w], xn_in[:, :, t0:t0 + w], sXn)
        first = True
        for ot in range(32):
            pr = ot % 2
            if wu[pr]:
                P.wait("sync", "tensor", wu[pr])
            P.dma("sync", wst[:, pr, 0:8, :], w1T[ot], sWP[pr])
            wl_c = P.dma("sync", wst[:, pr, 8:16, :], w2T[ot], sWP[pr])
            for (n0, nw) in NT:
                s1t, tile = pool.acquire()
                P.swait("tensor", sWP[pr], wl_c)
                if first:
                    P.swait("tensor", sXn, xn_c)
                    first = False
                for kt in range(8):
                    P.op("tensor", lambda e, o=tile[:, 0:nw], l=wst[:, pr, kt, :],
                         r=xnb[:, kt, n0:n0 + nw], s_=(kt == 0), p=(kt == 7):
                         e.matmul(o, l, r, start=s_, stop=p))
                for kt in range(8):
                    mm_c = P.op("tensor", lambda e, o=tile[:, 512:512 + nw],
                                l=wst[:, pr, 8 + kt, :],
                                r=xnb[:, kt, n0:n0 + nw], s_=(kt == 0), p=(kt == 7):
                                e.matmul(o, l, r, start=s_, stop=p))
                wu[pr] = mm_c
                P.wait("scalar", "tensor", mm_c)
                if h_c:
                    P.wait("scalar", "vector", h_c)
                sl_c = P.op("scalar", lambda e, o=t2_sb[:, 0:nw], i=tile[:, 0:nw],
                            sc=s1_sb[:, ot:ot + 1]:
                            e.activation(o, i, AF.Silu, scale=sc))
                P.wait("vector", "tensor", mm_c)
                P.wait("vector", "scalar", sl_c)
                h_c = P.op("vector", lambda e, o=ha[:, ot, n0:n0 + nw], a=t2_sb[:, 0:nw],
                           b=tile[:, 512:512 + nw], s2=s2_sb[:, ot:ot + 1]:
                           e.scalar_tensor_tensor(o, b, s2, a, ALU.mult, ALU.mult))
                pool.release(s1t, "vector", h_c)
        h_was_vec = True

        for stage, (wr, ssc, hin, hout) in enumerate(
                [(h1T, s3_sb, ha, hb), (h2T, s4_sb, hb, ha)]):
            for ot in range(32):
                pr = ot % 2
                if wu[pr]:
                    P.wait("sync", "tensor", wu[pr])
                wl_c = P.dma("sync", wst[:, pr], wr[ot], sWP[pr])
                for (n0, nw) in NT:
                    s1t, tile = pool.acquire()
                    P.swait("tensor", sWP[pr], wl_c)
                    if ot == 0 and n0 == 0:
                        if h_was_vec:
                            P.wait("tensor", "vector", h_c)
                        else:
                            P.wait("tensor", "scalar", h_c)
                    for kt in range(32):
                        mm_c = P.op("tensor", lambda e, o=tile[:, 0:nw],
                                    l=wst[:, pr, kt, :], r=hin[:, kt, n0:n0 + nw],
                                    s_=(kt == 0), p=(kt == 31):
                                    e.matmul(o, l, r, start=s_, stop=p))
                    wu[pr] = mm_c
                    P.wait("scalar", "tensor", mm_c)
                    h_c = P.op("scalar", lambda e, o=hout[:, ot, n0:n0 + nw],
                               i=tile[:, 0:nw], sc=ssc[:, ot:ot + 1]:
                               e.activation(o, i, AF.Gelu, scale=sc))
                    pool.release(s1t, "scalar", h_c)
            h_was_vec = False

        for (n0, nw) in NT:
            P.swait("sync", sSt)
            P.wait("sync", "vector", P.pgn["vector"])
            P.dma("sync", fx[:, :, 0:nw], x_in[:, :, t0 + n0:t0 + n0 + nw], sIo)
            io_c = P.dma("sync", fm[:, :, 0:nw], mam_in[:, :, t0 + n0:t0 + n0 + nw], sIo)
            for ot in range(8):
                pr = ot % 2
                if wu[pr]:
                    P.wait("sync", "tensor", wu[pr])
                wl_c = P.dma("sync", wst[:, pr], w3T[ot], sWP[pr])
                s1t, tile = pool.acquire()
                P.swait("tensor", sWP[pr], wl_c)
                if ot == 0 and n0 == 0:
                    P.wait("tensor", "scalar", h_c)
                for kt in range(32):
                    mm_c = P.op("tensor", lambda e, o=tile[:, 0:nw],
                                l=wst[:, pr, kt, :], r=ha[:, kt, n0:n0 + nw],
                                s_=(kt == 0), p=(kt == 31):
                                e.matmul(o, l, r, start=s_, stop=p))
                wu[pr] = mm_c
                P.wait("vector", "tensor", mm_c)
                if ot == 0:
                    P.swait("vector", sIo, io_c)
                    P.swait("vector", sSm, sm_c)
                c1 = P.op("vector", lambda e, o=fm[:, ot, 0:nw],
                          a=fx[:, ot, 0:nw], b=fm[:, ot, 0:nw]:
                          e.tensor_add(o, a, b))
                P.wait("vector", "vector", c1)
                ev_c = P.op("vector", lambda e, o=fx[:, ot, 0:nw],
                            a=tile[:, 0:nw], s1=s5_sb[:, ot:ot + 1],
                            b=fm[:, ot, 0:nw]:
                            e.scalar_tensor_tensor(o, a, s1, b, ALU.mult, ALU.add))
                pool.release(s1t, "vector", ev_c)
            P.wait("sync", "vector", ev_c)
            P.dma("sync", x1_out[:, :, t0 + n0:t0 + n0 + nw], fx[:, :, 0:nw], sSt)
        h_c = 0
    return


def build_launchC():
    nc = bass.Bass("TRN2", target_bir_lowering=False)
    P = Prog(nc)
    t = {}
    t["xn_in"] = nc.dram_tensor("xn_in", [128, 8, T], BF16, kind="ExternalInput").ap()
    t["x_in"] = nc.dram_tensor("x_in", [128, 8, T], F32, kind="ExternalInput").ap()
    t["mam_in"] = nc.dram_tensor("mam_in", [128, 8, T], F32, kind="ExternalInput").ap()
    t["w1T"] = nc.dram_tensor("w1T", [D, HID], F32, kind="ExternalInput").ap()
    t["w2T"] = nc.dram_tensor("w2T", [D, HID], F32, kind="ExternalInput").ap()
    t["h1T"] = nc.dram_tensor("h1T", [HID, HID], F32, kind="ExternalInput").ap()
    t["h2T"] = nc.dram_tensor("h2T", [HID, HID], F32, kind="ExternalInput").ap()
    t["w3T"] = nc.dram_tensor("w3T", [HID, D], F32, kind="ExternalInput").ap()
    t["w1s"] = nc.dram_tensor("w1s", [128, 32], F32, kind="ExternalInput").ap()
    t["w2s"] = nc.dram_tensor("w2s", [128, 32], F32, kind="ExternalInput").ap()
    t["h1s"] = nc.dram_tensor("h1s", [128, 32], F32, kind="ExternalInput").ap()
    t["h2s"] = nc.dram_tensor("h2s", [128, 32], F32, kind="ExternalInput").ap()
    t["w3s"] = nc.dram_tensor("w3s", [128, 8], F32, kind="ExternalInput").ap()
    t["x1_out"] = nc.dram_tensor("x1_out", [128, 8, T], F32, kind="ExternalOutput").ap()
    S = mk_streams(P)
    emit_C(P, S, t)
    return P.finish()

def emit_D1(P, S, t):
    _dl = ["ln", "qk", "v", "attn", "scale", "all"].index(DBG_D1)
    nc = P.nc
    TEX = T + 2 * HLO
    x1e = t["x1e"]; wqT = t["wqT"]; wkT = t["wkT"]; wvT = t["wvT"]; woT = t["woT"]
    bq = t["bq"]; bk = t["bk"]; bvt = t["bvt"]; bo_ = t["bo_"]
    mask1 = t["mask1"]; mask2 = t["mask2"]; nrm_d = t["nrm_d"]; srow_d = t["srow_d"]
    xn1_out = t["xn1_out"]; at_out = t["at_out"]
    xn1b = P.sbuf("xn1b", (128, 8, TEX), BF16)
    arena = P.sbuf("arena", (128, 8192), F32)
    qb = P.sbuf("qb", (128, 8, T), BF16)
    kb = P.sbuf("kb", (128, 8, TEX), BF16)
    v_tm = P.sbuf("vtm", (128, 17, D), BF16)
    aob = P.sbuf("aob", (128, 8, T), BF16)
    m1_sb = P.sbuf("m1s", (128, 128), F32)
    m2_sb = P.sbuf("m2s", (64, 128), F32)
    e1_sb = P.sbuf("e1s", (128, 128), F32)
    e2_sb = P.sbuf("e2s", (64, 128), F32)
    e1b = P.sbuf("e1b", (128, 128), BF16)
    e2b = P.sbuf("e2b", (64, 128), BF16)
    wst = P.sbuf("wst", (128, 2, 8, 128), BF16)
    bq_sb = P.sbuf("bqs", (128, 8), F32)
    bk_sb = P.sbuf("bks", (128, 8), F32)
    bv_sb = P.sbuf("bvs", (128, D), F32)
    bo_sb = P.sbuf("bos", (128, 8), F32)
    ones1 = P.sbuf("ones1", (128, 1), F32)
    onesb = P.sbuf("onesb", (1, 128), F32)
    ones2 = P.sbuf("ones2", (128, 2), F32)
    onesb1 = P.sbuf("onesb1", (128, 1), BF16)
    pool = S["pool"]
    stats1 = qb[0:1].rearrange("p d w -> p (d w)")[:, 0:4096].bitcast(F32)

    xblk = arena[:].rearrange("p (a d w) -> p a d w", a=2, d=8)[:, 0]
    sqb = arena[:].rearrange("p (a d w) -> p a d w", a=2, d=8)[:, 1]
    rnb = arena[:].rearrange("p (d w) -> p d w", d=16)[:, 0:8]
    att_st = arena[:].rearrange("p (d w) -> p d w", d=16)[:, 8:16]
    nqb = arena[0:2].rearrange("p (x d w) -> p x d w", x=2, d=8)[:, 0]
    sums_st = arena[0:1, 4096:6144]
    q2_sb = arena[:, 6144:6656]

    def stat_row(r, nw):
        return stats1[0:1, r * 512:r * 512 + nw]

    sSm = S["SM"]; sLn = S["LN"]; sW = [S["G0"], S["G1"]]
    sMk = S["IO"]; sNr = S["BB"]; sSt = S["ST"]

    P.dma("sync", bq_sb[:], bq, sSm)
    P.dma("sync", bk_sb[:], bk, sSm)
    P.dma("sync", bv_sb[:], bvt, sSm)
    sm_c = P.dma("sync", bo_sb[:], bo_, sSm)
    iv = P.op("vector", lambda e: e.memset(ones2[:], 0.0))
    P.wait("vector", "vector", iv)
    P.op("vector", lambda e: e.memset(ones2[0:64, 0:1], 1.0))
    P.op("vector", lambda e: e.memset(ones2[64:128, 1:2], 1.0))
    P.op("vector", lambda e: e.memset(ones1[:], 1.0))
    P.op("vector", lambda e: e.memset(onesb[:], 1.0))
    on_c = P.op("vector", lambda e: e.memset(onesb1[:], 1.0))
    P.wait("tensor", "vector", on_c)

    ln_c = ln_stream(P, xdram=x1e, nd=8, tcols=TEX, out_bf=xn1b[:], xblk=xblk,
                     sq=sqb, stats=stat_row, ones1=ones1[:], onesc=onesb[:],
                     pool=pool, lds=sLn)
    P.wait("sync", "vector", ln_c)
    P.dma("sync", xn1_out, xn1b[:, :, HLO:HLO + T], sSt)

    wu = [0, 0]
    P.wait("tensor", "vector", ln_c)
    P.swait("scalar", sSm, sm_c)
    P.swait("vector", sSm, sm_c)

    def qk_phase(nrow, wr, bias_sb, out_bf, tcols, coff):
        nonlocal wu
        nv_c = nm_c = 0
        nz_c = P.pgn["vector"]
        P.wait("vector", "vector", P.pgn["vector"])
        for blk, (t0, nw) in enumerate(blocks_of(tcols, 512)):
            P.swait("vector", sNr)
            for ot in range(8):
                pr = ot % 2
                if wu[pr]:
                    P.wait("sync", "tensor", wu[pr])
                wl_c = P.dma("sync", wst[:, pr], wr[ot], sW[pr])
                s1t, tile = pool.acquire()
                P.swait("tensor", sW[pr], wl_c)
                for kt in range(8):
                    mm_c = P.op("tensor", lambda e, o=tile[:, 0:nw],
                                l=wst[:, pr, kt, :],
                                r=xn1b[:, kt, coff + t0:coff + t0 + nw],
                                s_=(kt == 0), p=(kt == 7):
                                e.matmul(o, l, r, start=s_, stop=p))
                wu[pr] = mm_c
                P.wait("scalar", "tensor", mm_c)
                if DBG_QK >= 1:
                    if nm_c:
                        P.wait("scalar", "tensor", nm_c)
                    sq_c = P.op("scalar", lambda e, o=q2_sb[:, 0:nw], i=tile[:, 0:nw],
                                b=bias_sb[:, ot:ot + 1]:
                                e.activation(o, i, AF.Square, bias=b))
                P.wait("vector", "tensor", mm_c)
                if DBG_QK >= 1:
                    P.wait("vector", "scalar", sq_c)
                ev_c = P.op("vector", lambda e, o=out_bf[:, ot, t0:t0 + nw],
                            i=tile[:, 0:nw], s1=bias_sb[:, ot:ot + 1]:
                            e.tensor_scalar_add(o, i, s1))
                pool.release(s1t, "vector", ev_c)
                if DBG_QK >= 1:
                    s2t, tile2 = pool.acquire()
                    P.wait("tensor", "scalar", sq_c)
                    nm_c = P.op("tensor", lambda e, o=tile2[0:2, 0:nw], l=ones2[:],
                                r=q2_sb[:, 0:nw]: e.matmul(o, l, r, start=True, stop=True))
                    P.wait("vector", "tensor", nm_c)
                    nv_c = P.op("vector", lambda e, o=nqb[:, ot, 0:nw],
                                i=tile2[0:2, 0:nw]: e.tensor_copy(o, i))
                    pool.release(s2t, "vector", nv_c)
            if DBG_QK < 2:
                continue
            P.wait("scalar", "vector", nv_c)
            c1 = P.op("scalar", lambda e, o=nqb[:, :, 0:nw], i=nqb[:, :, 0:nw],
                      b=P.cf(1e-12, 2): e.activation(o, i, AF.Ln, bias=b))
            P.wait("scalar", "scalar", c1)
            c2 = P.op("scalar", lambda e, o=nqb[:, :, 0:nw], i=nqb[:, :, 0:nw],
                      sc=P.cf(0.5, 2): e.activation(o, i, AF.Exp, scale=sc))
            P.wait("vector", "scalar", c2)
            c3 = P.op("vector", lambda e, o=nqb[:, :, 0:nw], i=nqb[:, :, 0:nw]:
                      e.tensor_scalar_add(o, i, float(EPS_ATT)))
            P.wait("vector", "vector", c3)
            c4 = P.op("vector", lambda e, o=nqb[:, :, 0:nw], i=nqb[:, :, 0:nw]:
                      e.reciprocal(o, i))
            if DBG_QK < 3:
                continue
            P.wait("sync", "vector", c4)
            dst_e = bass.AP(nrm_d, t0, [[16 * TEX, 1], [TEX, 8], [1, nw]])
            dst_o = bass.AP(nrm_d, 16 * TEX + t0, [[16 * TEX, 1], [TEX, 8], [1, nw]])
            P.dma("sync", dst_e, nqb[0:1, :, 0:nw], sNr)
            P.dma("sync", dst_o, nqb[1:2, :, 0:nw], sNr)
        for blk, (t0, nw) in enumerate(blocks_of(tcols, 512) if DBG_QK >= 4 else []):
            P.swait("sync", sNr)
            P.wait("sync", "vector", P.pgn["vector"])
            src_e = bass.AP(nrm_d, t0, [[0, 64], [TEX, 8], [1, nw]])
            src_o = bass.AP(nrm_d, 16 * TEX + t0, [[0, 64], [TEX, 8], [1, nw]])
            P.dma("sync", rnb[0:64, :, 0:nw], src_e, sNr)
            rb_c = P.dma("sync", rnb[64:128, :, 0:nw], src_o, sNr)
            P.swait("vector", sNr, rb_c)
            for ot in range(8):
                nz_c = P.op("vector", lambda e, o=out_bf[:, ot, t0:t0 + nw],
                            a=out_bf[:, ot, t0:t0 + nw], b=rnb[:, ot, 0:nw]:
                            e.tensor_mul(o, a, b))
        return nz_c

    if _dl >= 1:
        qn_c = qk_phase(0, wqT, bq_sb, qb[:], T, HLO)
        kn_c = qk_phase(1, wkT, bk_sb, kb[:], TEX, 0)

    vt_c = 0
    for tt in range(17 if _dl >= 2 else 0):
        rows = 128 if tt < 16 else 64
        for oc in range(8):
            pr = oc % 2
            o0 = oc * 128
            if wu[pr]:
                P.wait("sync", "tensor", wu[pr])
            wl_c = P.dma("sync", wst[:, pr], wvT[oc], sW[pr])
            s1t, tile = pool.acquire()
            P.swait("tensor", sW[pr], wl_c)
            for kt in range(8):
                mm_c = P.op("tensor", lambda e, o=tile[0:rows, 0:128],
                            l=xn1b[:, kt, tt * 128:tt * 128 + rows],
                            r=wst[:, pr, kt, :], s_=(kt == 0), p=(kt == 7):
                            e.matmul(o, l, r, start=s_, stop=p))
            wu[pr] = mm_c
            P.wait("vector", "tensor", mm_c)
            vt_c = P.op("vector", lambda e, o=v_tm[0:rows, tt, o0:o0 + 128],
                        a=tile[0:rows, 0:128], b=bv_sb[0:rows, o0:o0 + 128]:
                        e.tensor_add(o, a, b))
            pool.release(s1t, "vector", vt_c)

    P.wait("tensor", "vector", vt_c)
    ao_c = mk_c = sums_c = 0
    sums_ev = [0, 0]
    for qt in range(16 if _dl >= 3 else 0):
        if mk_c:
            P.wait("sync", "gpsimd", mk_c)
        P.dma("sync", m1_sb[:], mask1[qt], sMk)
        mk_ld = P.dma("sync", m2_sb[:], mask2[qt], sMk)
        P.swait("vector", sNr)
        s_sums, tsum = pool.acquire()
        for h in range(16):
            hb = (h % 2) * 64
            ho = h // 2
            sb_half = (h % 2) * 512
            s1t, tile = pool.acquire()
            P.op("tensor", lambda e, o=tile[:, 0:128],
                 l=kb[hb:hb + 64, ho, qt * 128:qt * 128 + 128],
                 r=qb[hb:hb + 64, ho, qt * 128:qt * 128 + 128]:
                 e.matmul(o, l, r, start=True, stop=True))
            sc2 = P.op("tensor", lambda e, o=tile[0:64, 512:640],
                       l=kb[hb:hb + 64, ho, qt * 128 + 128:qt * 128 + 192],
                       r=qb[hb:hb + 64, ho, qt * 128:qt * 128 + 128]:
                       e.matmul(o, l, r, start=True, stop=True))
            P.wait("scalar", "tensor", sc2)
            if mk_c:
                P.wait("scalar", "gpsimd", mk_c)
            P.op("scalar", lambda e, o=e1_sb[:], i=tile[:, 0:128]:
                 e.activation(o, i, AF.Exp))
            x2c = P.op("scalar", lambda e, o=e2_sb[:], i=tile[0:64, 512:640]:
                       e.activation(o, i, AF.Exp))
            P.wait("gpsimd", "scalar", x2c)
            if h == 0:
                P.swait("gpsimd", sMk, mk_ld)
            P.op("gpsimd", lambda e: e.tensor_mul(e1b[:], e1_sb[:], m1_sb[:]))
            m2c = P.op("gpsimd", lambda e: e.tensor_mul(e2b[:], e2_sb[:], m2_sb[:]))
            pool.release(s1t, "scalar", x2c)
            mk_c = m2c
            P.wait("tensor", "gpsimd", m2c)
            if sums_ev[h % 2]:
                P.wait("tensor", "vector", sums_ev[h % 2])
            P.op("tensor", lambda e, o=tsum[0:1, sb_half:sb_half + 128],
                 l=onesb1[:], r=e1b[:]: e.matmul(o, l, r, start=True, stop=False))
            sm_mm = P.op("tensor", lambda e, o=tsum[0:1, sb_half:sb_half + 128],
                         l=onesb1[0:64, :], r=e2b[:]:
                         e.matmul(o, l, r, start=False, stop=True))
            P.wait("vector", "tensor", sm_mm)
            sums_ev[h % 2] = P.op("vector",
                lambda e, o=sums_st[:, h * 128:(h + 1) * 128],
                       i=tsum[0:1, sb_half:sb_half + 128]: e.tensor_copy(o, i))
            s2t, tile2 = pool.acquire()
            P.op("tensor", lambda e, o=tile2[hb:hb + 64, 0:128],
                 l=v_tm[:, qt, h * 64:(h + 1) * 64], r=e1b[:]:
                 e.matmul(o, l, r, start=True, stop=False))
            av2 = P.op("tensor", lambda e, o=tile2[hb:hb + 64, 0:128],
                       l=v_tm[0:64, qt + 1, h * 64:(h + 1) * 64], r=e2b[:]:
                       e.matmul(o, l, r, start=False, stop=True))
            P.wait("vector", "tensor", av2)
            ao_c = P.op("vector",
                        lambda e, o=aob[hb:hb + 64, ho, qt * 128:(qt + 1) * 128],
                        i=tile2[hb:hb + 64, 0:128]: e.tensor_copy(o, i))
            pool.release(s2t, "vector", ao_c)
        pool.release(s_sums, "vector", sums_ev[1])
        P.wait("sync", "vector", max(sums_ev))
        dst = bass.AP(srow_d, qt * 128, [[16 * T, 1], [T, 16], [1, 128]])
        P.dma("sync", dst, sums_st.rearrange("p (h w) -> p h w", h=16), sNr)

    as_c = 0
    P.wait("vector", "vector", P.pgn["vector"])
    for blk in range(4 if _dl >= 4 else 0):
        t0 = blk * 512
        P.swait("sync", sNr)
        P.wait("sync", "vector", P.pgn["vector"])
        src_e = bass.AP(srow_d, t0, [[0, 64], [2 * T, 8], [1, 512]])
        src_o = bass.AP(srow_d, T + t0, [[0, 64], [2 * T, 8], [1, 512]])
        P.dma("sync", rnb[0:64, :, 0:512], src_e, sNr)
        rb_c = P.dma("sync", rnb[64:128, :, 0:512], src_o, sNr)
        P.swait("vector", sNr, rb_c)
        rc_c = P.op("vector", lambda e, o=rnb[:, :, 0:512], i=rnb[:, :, 0:512]:
                    e.reciprocal(o, i))
        P.wait("vector", "vector", rc_c)
        for ot in range(8):
            as_c = P.op("vector", lambda e, o=aob[:, ot, t0:t0 + 512],
                        a=aob[:, ot, t0:t0 + 512], b=rnb[:, ot, 0:512]:
                        e.tensor_mul(o, a, b))

    P.wait("tensor", "vector", as_c)
    ev_c = 0
    for blk in range(4 if _dl >= 5 else 0):
        t0 = blk * 512
        if blk > 0:
            P.swait("scalar", sSt)
        for ot in range(8):
            pr = ot % 2
            if wu[pr]:
                P.wait("sync", "tensor", wu[pr])
            wl_c = P.dma("sync", wst[:, pr], woT[ot], sW[pr])
            s1t, tile = pool.acquire()
            P.swait("tensor", sW[pr], wl_c)
            for kt in range(8):
                mm_c = P.op("tensor", lambda e, o=tile[:, 0:512],
                            l=wst[:, pr, kt, :], r=aob[:, kt, t0:t0 + 512],
                            s_=(kt == 0), p=(kt == 7):
                            e.matmul(o, l, r, start=s_, stop=p))
            wu[pr] = mm_c
            P.wait("scalar", "tensor", mm_c)
            ev_c = P.op("scalar", lambda e, o=att_st[:, ot, :], i=tile[:, 0:512],
                        b=bo_sb[:, ot:ot + 1]:
                        e.activation(o, i, AF.Identity, bias=b))
            pool.release(s1t, "scalar", ev_c)
        P.wait("sync", "scalar", ev_c)
        P.dma("sync", at_out[:, :, t0:t0 + 512], att_st[:], sSt)
    return


def build_launchD1():
    nc = bass.Bass("TRN2", target_bir_lowering=False)
    P = Prog(nc)
    TEX = T + 2 * HLO
    t = {}
    t["x1e"] = nc.dram_tensor("x1e", [128, 8, TEX], F32, kind="ExternalInput").ap()
    t["wqT"] = nc.dram_tensor("wqT", [D, D], F32, kind="ExternalInput").ap()
    t["wkT"] = nc.dram_tensor("wkT", [D, D], F32, kind="ExternalInput").ap()
    t["wvT"] = nc.dram_tensor("wvT", [D, D], F32, kind="ExternalInput").ap()
    t["woT"] = nc.dram_tensor("woT", [D, D], F32, kind="ExternalInput").ap()
    t["bq"] = nc.dram_tensor("bq", [128, 8], F32, kind="ExternalInput").ap()
    t["bk"] = nc.dram_tensor("bk", [128, 8], F32, kind="ExternalInput").ap()
    t["bvt"] = nc.dram_tensor("bvt", [128, D], F32, kind="ExternalInput").ap()
    t["bo_"] = nc.dram_tensor("bo_", [128, 8], F32, kind="ExternalInput").ap()
    t["mask1"] = nc.dram_tensor("mask1", [16, 128, 128], F32, kind="ExternalInput").ap()
    t["mask2"] = nc.dram_tensor("mask2", [16, 64, 128], F32, kind="ExternalInput").ap()
    t["nrm_d"] = nc.dram_tensor("nrm_d", [2, 16, TEX], F32)
    t["srow_d"] = nc.dram_tensor("srow_d", [16, T], F32)
    t["xn1_out"] = nc.dram_tensor("xn1_out", [128, 8, T], BF16, kind="ExternalOutput").ap()
    t["at_out"] = nc.dram_tensor("at_out", [128, 8, T], F32, kind="ExternalOutput").ap()
    S = mk_streams(P)
    emit_D1(P, S, t)
    return P.finish()

def emit_D2(P, S, t):
    nc = P.nc
    TH = 1024
    NSUB = T // TH
    xn1_in = t["xn1_in"]; x1_in = t["x1_in"]; at_in = t["at_in"]
    gate_w = t["gate_w"]; gbt = t["gbt"]; w1e = t["w1e"]; b1e = t["b1e"]
    w2e = t["w2e"]; b2e = t["b2e"]; ws1 = t["ws1"]; bs1 = t["bs1"]
    ws2 = t["ws2"]; bs2 = t["bs2"]; ident = t["ident"]; comb_d = t["comb_d"]
    x2_out = t["x2_out"]

    xn1b = P.sbuf("xn1b", (128, 8, T), BF16)
    hm = P.sbuf("hm", (128, 32, TH), BF16)
    acc = P.sbuf("acc", (128, 8, TH), F32)
    cbb = P.sbuf("cbb", (128, 8, TH), F32)
    stage = P.sbuf("stage", (128, 8, 512), F32)
    wst = P.sbuf("wst", (128, 2, 32, 128), BF16)
    lg_sb = P.sbuf("lg", (128, 16, 8), F32)
    m1_sb = P.sbuf("m1", (128, 16), F32)
    m2_sb = P.sbuf("m2", (128, 16), F32)
    eq1_sb = P.sbuf("eq1", (128, 16, 8), F32)
    eq2_sb = P.sbuf("eq2", (128, 16, 8), F32)
    tm_sb = P.sbuf("tm", (128, 8), F32)
    gw_sb = P.sbuf("gw", (128, 16, 2), F32)
    cmb_sb = P.sbuf("cmb", (128, 16, 8), F32)
    cfm_sb = P.sbuf("cfm", (8, T), F32)
    id_sb = P.sbuf("idn", (128, 128), F32)
    gr_sb = P.sbuf("gr", (128, 8, 8), BF16)
    gbt_sb = P.sbuf("gbt_s", (128, 8), F32)
    b1_sb = P.sbuf("b1s", (128, 8, 32), F32)
    b2_sb = P.sbuf("b2s", (128, 8, 8), F32)
    bs1_sb = P.sbuf("bs1s", (128, 32), F32)
    bs2_sb = P.sbuf("bs2s", (128, 8), F32)
    pool = S["pool"]

    sSm = S["SM"]; sXn = S["LN"]; sW = [S["G0"], S["G1"]]
    sCb = S["BB"]; sIo = S["IO"]; sSt = S["ST"]; sGm = S["GM"]

    P.dma("sync", gr_sb[:], gate_w, sGm)
    P.dma("sync", gbt_sb[:], gbt, sSm)
    P.dma("sync", b1_sb[:], b1e, sSm)
    P.dma("sync", b2_sb[:], b2e, sSm)
    P.dma("sync", bs1_sb[:], bs1, sSm)
    P.dma("sync", bs2_sb[:], bs2, sSm)
    P.dma("sync", id_sb[:], ident, sSm)
    sm_c = P.dma("sync", xn1b[:], xn1_in, sXn)

    P.swait("tensor", sXn, sm_c)
    P.swait("tensor", sGm, sGm.n)
    P.swait("vector", sSm, sSm.n)
    lg_c = 0
    for tt in range(16):
        s1t, tile = pool.acquire()
        for kt in range(8):
            mm_c = P.op("tensor", lambda e, o=tile[0:128, 0:8],
                        l=xn1b[:, kt, tt * 128:(tt + 1) * 128], r=gr_sb[:, kt, :],
                        s_=(kt == 0), p=(kt == 7):
                        e.matmul(o, l, r, start=s_, stop=p))
        P.wait("vector", "tensor", mm_c)
        lg_c = P.op("vector", lambda e, o=lg_sb[:, tt, :], a=tile[0:128, 0:8],
                    b=gbt_sb[:]: e.tensor_add(o, a, b))
        pool.release(s1t, "vector", lg_c)
    P.wait("vector", "vector", lg_c)
    m1_c = P.op("vector", lambda e: e.reduce_max(m1_sb[:], lg_sb[:], axis=AX.X))
    eq_c = 0
    for tt in range(16):
        P.wait("vector", "vector", m1_c if tt == 0 else eq_c)
        e1c = P.op("vector", lambda e, o=eq1_sb[:, tt, :], a=lg_sb[:, tt, :],
                   s1=m1_sb[:, tt:tt + 1]:
                   e.tensor_single_scalar(o, a, s1, ALU.is_equal))
        P.wait("vector", "vector", e1c)
        eq_c = P.op("vector", lambda e, o=tm_sb[:], a=eq1_sb[:, tt, :],
                    b=lg_sb[:, tt, :]:
                    e.scalar_tensor_tensor(o, a, -1e30, b, ALU.mult, ALU.add))
        P.wait("vector", "vector", eq_c)
        m2c = P.op("vector", lambda e, o=m2_sb[:, tt:tt + 1], i=tm_sb[:]:
                   e.reduce_max(o, i, axis=AX.X))
        P.wait("vector", "vector", m2c)
        eq_c = P.op("vector", lambda e, o=eq2_sb[:, tt, :], a=tm_sb[:],
                    s1=m2_sb[:, tt:tt + 1]:
                    e.tensor_single_scalar(o, a, s1, ALU.is_equal))
    P.wait("vector", "vector", eq_c)
    d_c = P.op("vector", lambda e: e.tensor_sub(m1_sb[:], m1_sb[:], m2_sb[:]))
    P.wait("scalar", "vector", d_c)
    sg_c = P.op("scalar", lambda e, o=gw_sb[:, :, 0], i=m1_sb[:]:
                e.activation(o, i, AF.Sigmoid))
    P.wait("vector", "scalar", sg_c)
    g2_c = P.op("vector", lambda e, o=gw_sb[:, :, 1], i=gw_sb[:, :, 0]:
                e.tensor_scalar(o, i, -1.0, 1.0, ALU.mult, ALU.add))
    cm_c = 0
    for tt in range(16):
        P.wait("vector", "vector", g2_c if tt == 0 else cm_c)
        c1 = P.op("vector", lambda e, o=cmb_sb[:, tt, :], a=eq1_sb[:, tt, :],
                  s1=gw_sb[:, tt:tt + 1, 0]: e.tensor_scalar_mul(o, a, s1))
        P.wait("vector", "vector", c1)
        cm_c = P.op("vector", lambda e, o=cmb_sb[:, tt, :], a=eq2_sb[:, tt, :],
                    s1=gw_sb[:, tt:tt + 1, 1], b=cmb_sb[:, tt, :]:
                    e.scalar_tensor_tensor(o, a, s1, b, ALU.mult, ALU.add))
    P.wait("tensor", "vector", cm_c)
    P.swait("tensor", sSm, sSm.n)
    tr_c = 0
    for tt in range(16):
        s1t, tile = pool.acquire()
        t_c = P.op("tensor", lambda e, o=tile[0:8, 0:128], i=cmb_sb[:, tt, :],
                   idn=id_sb[:]: e.transpose(o, i, idn))
        P.wait("vector", "tensor", t_c)
        tr_c = P.op("vector", lambda e, o=cfm_sb[:, tt * 128:(tt + 1) * 128],
                    i=tile[0:8, 0:128]: e.tensor_copy(o, i))
        pool.release(s1t, "vector", tr_c)
    P.wait("sync", "vector", tr_c)
    cb_st = P.dma("sync", bass.AP(comb_d, 0, [[T, 8], [1, T]]), cfm_sb[:], sCb)

    wu = [0, 0]
    ac_c = 0
    for s in range(NSUB):
        t0 = s * TH
        P.swait("sync", sCb)
        if ac_c:
            P.wait("sync", "vector", P.pgn["vector"])
        cb_c = P.dma("sync", cbb[:],
                     bass.AP(comb_d, t0, [[0, 128], [T, 8], [1, TH]]), sCb)
        P.swait("vector", sCb, cb_c)
        P.wait("vector", "vector", P.pgn["vector"])
        z_c = P.op("vector", lambda e: e.memset(acc[:], 0.0))

        for ex in range(NEXP + 1):
            shared = ex == NEXP
            w1r = ws1 if shared else w1e[ex]
            w2r = ws2 if shared else w2e[ex]
            bcol1 = bs1_sb if shared else b1_sb[:, ex, :]
            bcol2 = bs2_sb if shared else b2_sb[:, ex, :]
            for ot in range(32):
                pr = ot % 2
                if wu[pr]:
                    P.wait("sync", "tensor", wu[pr])
                wl_c = P.dma("sync", wst[:, pr, 0:8, :], w1r[ot], sW[pr])
                s1t, tile = pool.acquire()
                P.swait("tensor", sW[pr], wl_c)
                for nt in range(2):
                    for kt in range(8):
                        mm_c = P.op("tensor", lambda e, o=tile[:, nt * 512:nt * 512 + 512],
                                    l=wst[:, pr, kt, :],
                                    r=xn1b[:, kt, t0 + nt * 512:t0 + nt * 512 + 512],
                                    s_=(kt == 0), p=(kt == 7):
                                    e.matmul(o, l, r, start=s_, stop=p))
                wu[pr] = mm_c
                P.wait("scalar", "tensor", mm_c)
                h_c = P.op("scalar", lambda e, o=hm[:, ot, :], i=tile[:, 0:1024],
                           b=bcol1[:, ot:ot + 1]:
                           e.activation(o, i, AF.Gelu, bias=b))
                pool.release(s1t, "scalar", h_c)
            for ot in range(8):
                pr = ot % 2
                if wu[pr]:
                    P.wait("sync", "tensor", wu[pr])
                wl_c = P.dma("sync", wst[:, pr], w2r[ot], sW[pr])
                s1t, tile = pool.acquire()
                P.swait("tensor", sW[pr], wl_c)
                if ot == 0:
                    P.wait("tensor", "scalar", h_c)
                for nt in range(2):
                    for kt in range(32):
                        mm_c = P.op("tensor", lambda e, o=tile[:, nt * 512:nt * 512 + 512],
                                    l=wst[:, pr, kt, :], r=hm[:, kt, nt * 512:nt * 512 + 512],
                                    s_=(kt == 0), p=(kt == 31):
                                    e.matmul(o, l, r, start=s_, stop=p))
                wu[pr] = mm_c
                P.wait("vector", "tensor", mm_c)
                if ex == 0 and ot == 0:
                    P.wait("vector", "vector", z_c)
                for nt in range(2):
                    if ac_c:
                        P.wait("vector", "vector", ac_c)
                    if shared:
                        c1 = P.op("vector", lambda e, o=stage[:, ot, :],
                                  a=tile[:, nt * 512:nt * 512 + 512],
                                  s1=bcol2[:, ot:ot + 1]:
                                  e.tensor_scalar_add(o, a, s1))
                    else:
                        c1 = P.op("vector", lambda e, o=stage[:, ot, :],
                                  a=tile[:, nt * 512:nt * 512 + 512],
                                  s1=bcol2[:, ot:ot + 1],
                                  b=cbb[:, ex, nt * 512:nt * 512 + 512]:
                                  e.scalar_tensor_tensor(o, a, s1, b, ALU.add, ALU.mult))
                    P.wait("vector", "vector", c1)
                    ac_c = P.op("vector", lambda e, o=acc[:, ot, nt * 512:nt * 512 + 512],
                                a=acc[:, ot, nt * 512:nt * 512 + 512],
                                b=stage[:, ot, :]: e.tensor_add(o, a, b))
                pool.release(s1t, "vector", ac_c)

        for half in range(2):
            hs = t0 + half * 512
            P.wait("sync", "vector", P.pgn["vector"])
            io_c = P.dma("sync", stage[:], x1_in[:, :, hs:hs + 512], sIo)
            P.swait("vector", sIo, io_c)
            for ot in range(8):
                a1_c = P.op("vector", lambda e, o=acc[:, ot, half * 512:half * 512 + 512],
                            a=acc[:, ot, half * 512:half * 512 + 512],
                            b=stage[:, ot, :]: e.tensor_add(o, a, b))
            P.wait("sync", "vector", a1_c)
            io_c = P.dma("sync", stage[:], at_in[:, :, hs:hs + 512], sIo)
            P.swait("vector", sIo, io_c)
            for ot in range(8):
                a2_c = P.op("vector", lambda e, o=acc[:, ot, half * 512:half * 512 + 512],
                            a=acc[:, ot, half * 512:half * 512 + 512],
                            b=stage[:, ot, :]: e.tensor_add(o, a, b))
            P.wait("sync", "vector", a2_c)
            P.dma("sync", x2_out[:, :, hs:hs + 512],
                  acc[:, :, half * 512:half * 512 + 512], sSt)
        P.swait("vector", sSt)
    return


def build_launchD2():
    nc = bass.Bass("TRN2", target_bir_lowering=False)
    P = Prog(nc)
    t = {}
    t["xn1_in"] = nc.dram_tensor("xn1_in", [128, 8, T], BF16, kind="ExternalInput").ap()
    t["x1_in"] = nc.dram_tensor("x1_in", [128, 8, T], F32, kind="ExternalInput").ap()
    t["at_in"] = nc.dram_tensor("at_in", [128, 8, T], F32, kind="ExternalInput").ap()
    t["gate_w"] = nc.dram_tensor("gate_w", [D, NEXP], F32, kind="ExternalInput").ap()
    t["gbt"] = nc.dram_tensor("gbt", [128, NEXP], F32, kind="ExternalInput").ap()
    t["w1e"] = nc.dram_tensor("w1e", [NEXP, D, HID], F32, kind="ExternalInput").ap()
    t["b1e"] = nc.dram_tensor("b1e", [128, NEXP, 32], F32, kind="ExternalInput").ap()
    t["w2e"] = nc.dram_tensor("w2e", [NEXP, HID, D], F32, kind="ExternalInput").ap()
    t["b2e"] = nc.dram_tensor("b2e", [128, NEXP, 8], F32, kind="ExternalInput").ap()
    t["ws1"] = nc.dram_tensor("ws1", [D, HID], F32, kind="ExternalInput").ap()
    t["bs1"] = nc.dram_tensor("bs1", [128, 32], F32, kind="ExternalInput").ap()
    t["ws2"] = nc.dram_tensor("ws2", [HID, D], F32, kind="ExternalInput").ap()
    t["bs2"] = nc.dram_tensor("bs2", [128, 8], F32, kind="ExternalInput").ap()
    t["ident"] = nc.dram_tensor("ident", [128, 128], F32, kind="ExternalInput").ap()
    t["comb_d"] = nc.dram_tensor("comb_d", [NEXP, T], F32)
    t["x2_out"] = nc.dram_tensor("x2_out", [128, 8, T], F32, kind="ExternalOutput").ap()
    S = mk_streams(P)
    emit_D2(P, S, t)
    return P.finish()

def _fm(aT):
    Dd, Tt = aT.shape
    return np.ascontiguousarray(aT.reshape(Dd // 128, 128, Tt).transpose(1, 0, 2))


def _cols(v):
    return np.ascontiguousarray(np.asarray(v).reshape(-1, 128).T)




def build_launch2():
    nc = bass.Bass("TRN2", target_bir_lowering=False)
    P = Prog(nc)
    TEX = T + 2 * HLO
    EI = lambda nm, sh, dt=F32: nc.dram_tensor(nm, sh, dt, kind="ExternalInput").ap()
    t = {}
    for nm, sh, *dt in [("u_in", [128, 16, T]), ("dl_in", [128, 16, T]),
                   ("res_in", [128, 16, T]), ("bc_in", [32, T]),
                   ("uex", [128, 16, 64]), ("dlex", [128, 16, 64]),
                   ("resex", [128, 16, 64]), ("bcex", [32, 64]),
                   ("alog", [128, 256]), ("dvec", [128, 16]),
                   ("wout", [8, 128, 16, 128], BF16), ("fqa", [8, 128, 1024]),
                   ("cmask", [128, 40]),
                   ("x2fm", [128, 8, TEX]),
                   ("w1T", [32, 128, 8, 128], BF16), ("w2T", [32, 128, 8, 128], BF16),
                   ("h1T", [32, 128, 32, 128], BF16),
                   ("h2T", [32, 128, 32, 128], BF16),
                   ("w3T", [8, 128, 32, 128], BF16), ("w1s", [128, 32]),
                   ("w2s", [128, 32]), ("h1s", [128, 32]), ("h2s", [128, 32]),
                   ("w3s", [128, 8]),
                   ("wqT", [8, 128, 8, 128], BF16), ("wkT", [8, 128, 8, 128], BF16),
                   ("wvT", [8, 128, 8, 128], BF16), ("woT", [8, 128, 8, 128], BF16),
                   ("bq", [128, 8]),
                   ("bk", [128, 8]), ("bvt", [128, D]), ("bo_", [128, 8]),
                   ("mask1", [16, 128, 128]), ("mask2", [16, 64, 128]),
                   ("gate_w", [128, 8, NEXP], BF16), ("gbt", [128, NEXP]),
                   ("w1e", [NEXP, 32, 128, 8, 128], BF16), ("b1e", [128, NEXP, 32]),
                   ("w2e", [NEXP, 8, 128, 32, 128], BF16), ("b2e", [128, NEXP, 8]),
                   ("ws1", [32, 128, 8, 128], BF16), ("bs1", [128, 32]),
                   ("ws2", [8, 128, 32, 128], BF16),
                   ("bs2", [128, 8]), ("ident", [128, 128])]:
        t[nm] = EI(nm, sh, dt[0] if dt else F32)
    t["xn_ext"] = nc.dram_tensor("xn_ext", [128, 8, TEX], BF16,
                                 kind="ExternalInput").ap()
    ID = lambda nm, sh, dt=F32: nc.dram_tensor(nm, sh, dt)
    mam_d = ID("mam_d", [128, 8, TEX])
    x1e_d = ID("x1e_d", [128, 8, TEX])
    xn1_d = ID("xn1_d", [128, 8, T], BF16)
    at_d = ID("at_d", [128, 8, T])
    nrm_d = ID("nrm_d", [2, 16, TEX])
    srow_d = ID("srow_d", [16, T])
    comb_d = ID("comb_d", [NEXP, T])
    x2_out = nc.dram_tensor("x2_out", [128, 8, T], F32, kind="ExternalOutput").ap()

    S = mk_streams(P)
    tB = dict(t, mam_out=mam_d.ap())
    with P.scope("B"):
        emit_B2(P, S, tB)
    P.barrier()
    tC = dict(t, xn_in=t["xn_ext"], x_in=t["x2fm"], mam_in=mam_d.ap(),
              x1_out=x1e_d.ap())
    with P.scope("C"):
        emit_C(P, S, tC)
    P.barrier()
    tD1 = dict(t, x1e=x1e_d.ap(), nrm_d=nrm_d, srow_d=srow_d,
               xn1_out=xn1_d.ap(), at_out=at_d.ap())
    with P.scope("E"):
        emit_D1(P, S, tD1)
    P.barrier()
    tD2 = dict(t, xn1_in=xn1_d.ap(), x1_in=x1e_d.ap()[:, :, HLO:HLO + T],
               at_in=at_d.ap(), comb_d=comb_d, x2_out=x2_out)
    with P.scope("F"):
        emit_D2(P, S, tD2)
    return P.finish()


def _maps_A(inp):
    x = inp["L0_attn_x"]
    alog_l = np.ascontiguousarray(
        inp["L0_attn_A_log"].reshape(16, 128, 16).transpose(1, 0, 2).reshape(128, 256))
    cw = np.ascontiguousarray(
        inp["L0_attn_conv_w"][:, 0, :].reshape(4, 16, 128).transpose(2, 1, 0).reshape(128, 64))
    common = dict(win=inp["L0_attn_W_in"], convw=cw,
                  convb=_cols(inp["L0_attn_conv_b"]), wx=inp["L0_attn_W_x"],
                  wdt=inp["L0_attn_W_dt"], bdt=_cols(inp["L0_attn_b_dt"]), alog=alog_l)
    maps = []
    for c in range(NCORES):
        b, q = c // 4, c % 4
        t0 = q * T
        xe = np.zeros((T + 3, D), np.float32)
        xe[3:] = x[b, t0:t0 + T]
        if q > 0:
            xe[0:3] = x[b, t0 - 3:t0]
        maps.append(dict(common, xfm=_fm(np.ascontiguousarray(xe.T))))
    return maps, alog_l


def _maps_B(inp, a_res, alog_l):
    fqa = np.stack([a_res[c]["fq_out"] for c in range(NCORES)])
    maps = []
    for c in range(NCORES):
        b = c // 4
        m = np.zeros(8, np.float32)
        m[4 * b: c] = 1.0
        cm = np.zeros((128, 16), np.float32)
        cm[:, 0:8] = m
        cm[:, 8:16] = 1.0 - m
        maps.append(dict(u_in=a_res[c]["xconv_out"], dl_in=a_res[c]["delta_out"],
                         bc_in=a_res[c]["bc_out"], res_in=a_res[c]["res_out"],
                         alog=alog_l, dvec=_cols(inp["L0_attn_D"]),
                         wout=inp["L0_attn_W_out"], fqa=fqa, cmask=cm))
    return maps


def _maps_C(inp, x, a_res, b_res):
    tr = lambda a: np.ascontiguousarray(a.T)
    maps = []
    for c in range(NCORES):
        b, q = c // 4, c % 4
        maps.append(dict(
            xn_in=a_res[c]["xn_out"], mam_in=b_res[c]["mam_out"],
            x_in=_fm(np.ascontiguousarray(x[b, q * T:(q + 1) * T].T)),
            w1T=tr(inp["L0_ffn_w1_w"]), w2T=tr(inp["L0_ffn_w2_w"]),
            h1T=tr(inp["L0_ffn_h1_w"]), h2T=tr(inp["L0_ffn_h2_w"]),
            w3T=tr(inp["L0_ffn_w3_w"]),
            w1s=_cols(inp["L0_ffn_w1_s"]), w2s=_cols(inp["L0_ffn_w2_s"]),
            h1s=_cols(inp["L0_ffn_h1_s"]), h2s=_cols(inp["L0_ffn_h2_s"]),
            w3s=_cols(inp["L0_ffn_w3_s"])))
    return maps


def _unfm(a):
    return np.ascontiguousarray(np.asarray(a).transpose(1, 0, 2).reshape(-1, a.shape[2]))


def _maps_D1(inp, c_res):
    x1 = {c: _unfm(c_res[c]["x1_out"]) for c in range(NCORES)}
    tr = lambda a: np.ascontiguousarray(a.T)
    maps = []
    tk = np.arange(192)[:, None]
    tq = np.arange(128)[None, :]
    dd = tk - tq - 32
    band = (np.abs(dd) <= 32) & (dd % 4 == 0)
    for c in range(NCORES):
        b, q = c // 4, c % 4
        xe = np.zeros((D, T + 2 * HLO), np.float32)
        xe[:, HLO:HLO + T] = x1[c]
        if q > 0:
            xe[:, 0:HLO] = x1[c - 1][:, T - HLO:]
        if q < 3:
            xe[:, HLO + T:] = x1[c + 1][:, 0:HLO]
        m = np.zeros((16, 192, 128), np.float32)
        for qt in range(16):
            xext = qt * 128 + np.arange(192)
            valid = np.ones(192, bool)
            if q == 0:
                valid &= xext >= HLO
            if q == 3:
                valid &= xext < HLO + T
            m[qt] = band & valid[:, None]
        maps.append(dict(
            x1e=_fm(xe),
            wqT=st(tr(inp["L1_attn_Wq"]), 8, 8), wkT=st(tr(inp["L1_attn_Wk"]), 8, 8),
            wvT=st(tr(inp["L1_attn_Wv"]), 8, 8), woT=st(tr(inp["L1_attn_Wo"]), 8, 8),
            bq=_cols(inp["L1_attn_bq"]), bk=_cols(inp["L1_attn_bk"]),
            bvt=np.ascontiguousarray(np.tile(inp["L1_attn_bv"], (128, 1))),
            bo_=_cols(inp["L1_attn_bo"]),
            mask1=np.ascontiguousarray(m[:, 0:128, :]),
            mask2=np.ascontiguousarray(m[:, 128:192, :])))
    return maps


def _maps_D2(inp, c_res, d1_res):
    b1c = np.ascontiguousarray(inp["L1_ffn_b1"].reshape(8, 32, 128).transpose(2, 0, 1))
    b2c = np.ascontiguousarray(inp["L1_ffn_b2"].reshape(8, 8, 128).transpose(2, 0, 1))
    maps = []
    for c in range(NCORES):
        maps.append(dict(
            xn1_in=d1_res[c]["xn1_out"], x1_in=c_res[c]["x1_out"],
            at_in=d1_res[c]["at_out"],
            gate_w=np.ascontiguousarray(
                inp["L1_ffn_gate_w"].reshape(8, 128, NEXP).transpose(1, 0, 2)).astype(BFH),
            gbt=np.ascontiguousarray(np.tile(inp["L1_ffn_gate_b"], (128, 1))),
            w1e=np.ascontiguousarray(
                inp["L1_ffn_w1"].reshape(NEXP, 8, 128, 32, 128).transpose(0, 3, 2, 1, 4)).astype(BFH),
            b1e=b1c,
            w2e=np.ascontiguousarray(
                inp["L1_ffn_w2"].reshape(NEXP, 32, 128, 8, 128).transpose(0, 3, 2, 1, 4)).astype(BFH),
            b2e=b2c,
            ws1=st(inp["L1_ffn_ws1"], 8, 32), bs1=_cols(inp["L1_ffn_bs1"]),
            ws2=st(inp["L1_ffn_ws2"], 32, 8), bs2=_cols(inp["L1_ffn_bs2"]),
            ident=np.eye(128, dtype=np.float32)))
    return maps


def _maps_L2(inp, x, a_res, alog_l):
    import ml_dtypes
    BFH = ml_dtypes.bfloat16
    tr = lambda a: np.ascontiguousarray(a.T)
    st = lambda wT, KT, NO: np.ascontiguousarray(
        wT.reshape(KT, 128, NO, 128).transpose(2, 1, 0, 3)).astype(BFH)
    fqa = np.stack([a_res[c]["fq_out"] for c in range(NCORES)])
    b1c = np.ascontiguousarray(inp["L1_ffn_b1"].reshape(8, 32, 128).transpose(2, 0, 1))
    b2c = np.ascontiguousarray(inp["L1_ffn_b2"].reshape(8, 8, 128).transpose(2, 0, 1))
    tk = np.arange(192)[:, None]
    tq = np.arange(128)[None, :]
    dd = tk - tq - 32
    band = (np.abs(dd) <= 32) & (dd % 4 == 0)
    TEX = T + 2 * HLO
    maps = []
    for c in range(NCORES):
        b, q = c // 4, c % 4
        h1 = np.zeros(8, np.float32)
        h2 = np.zeros(8, np.float32)
        if q > 0:
            h1[4 * b: c - 1] = 1.0
            h2[c - 1] = 1.0
        cm = np.zeros((128, 40), np.float32)
        cm[:, 16:24] = h1
        cm[:, 24:32] = h2
        cm[:, 32:40] = 1.0 - h1 - h2
        def ext_of(key, P_, F_=16):
            e = np.zeros((128, F_, 64), np.float32)
            if q > 0:
                e[:, :, 0:32] = a_res[c - 1][key][:, :, T - 32:]
            if q < 3:
                e[:, :, 32:64] = a_res[c + 1][key][:, :, 0:32]
            return e
        uex = ext_of("xconv_out", None)
        dlex = ext_of("delta_out", None)
        resex = ext_of("res_out", None)
        bcex = np.zeros((32, 64), np.float32)
        if q > 0:
            bcex[:, 0:32] = a_res[c - 1]["bc_out"][:, T - 32:]
        if q < 3:
            bcex[:, 32:64] = a_res[c + 1]["bc_out"][:, 0:32]
        t0 = q * T
        xe = np.zeros((D, TEX), np.float32)
        xe[:, HLO:HLO + T] = x[b, t0:t0 + T].T
        if q > 0:
            xe[:, 0:HLO] = x[b, t0 - HLO:t0].T
        if q < 3:
            xe[:, HLO + T:] = x[b, t0 + T:t0 + T + HLO].T
        import ml_dtypes
        xne = np.zeros((128, 8, TEX), dtype=np.asarray(a_res[c]["xn_out"]).dtype)
        xne[:, :, HLO:HLO + T] = a_res[c]["xn_out"]
        if q > 0:
            xne[:, :, 0:HLO] = a_res[c - 1]["xn_out"][:, :, T - HLO:]
        if q < 3:
            xne[:, :, HLO + T:] = a_res[c + 1]["xn_out"][:, :, 0:HLO]
        m = np.zeros((16, 192, 128), np.float32)
        for qt in range(16):
            xext = qt * 128 + np.arange(192)
            valid = np.ones(192, bool)
            if q == 0:
                valid &= xext >= HLO
            if q == 3:
                valid &= xext < HLO + T
            m[qt] = band & valid[:, None]
        maps.append(dict(
            u_in=a_res[c]["xconv_out"], dl_in=a_res[c]["delta_out"],
            res_in=a_res[c]["res_out"], bc_in=a_res[c]["bc_out"],
            uex=uex, dlex=dlex, resex=resex, bcex=bcex,
            alog=alog_l, dvec=_cols(inp["L0_attn_D"]),
            wout=st(inp["L0_attn_W_out"], 16, 8),
            fqa=fqa, cmask=cm, x2fm=_fm(xe), xn_ext=xne,
            w1T=st(tr(inp["L0_ffn_w1_w"]), 8, 32), w2T=st(tr(inp["L0_ffn_w2_w"]), 8, 32),
            h1T=st(tr(inp["L0_ffn_h1_w"]), 32, 32), h2T=st(tr(inp["L0_ffn_h2_w"]), 32, 32),
            w3T=st(tr(inp["L0_ffn_w3_w"]), 32, 8),
            w1s=_cols(inp["L0_ffn_w1_s"]), w2s=_cols(inp["L0_ffn_w2_s"]),
            h1s=_cols(inp["L0_ffn_h1_s"]), h2s=_cols(inp["L0_ffn_h2_s"]),
            w3s=_cols(inp["L0_ffn_w3_s"]),
            wqT=st(tr(inp["L1_attn_Wq"]), 8, 8), wkT=st(tr(inp["L1_attn_Wk"]), 8, 8),
            wvT=st(tr(inp["L1_attn_Wv"]), 8, 8), woT=st(tr(inp["L1_attn_Wo"]), 8, 8),
            bq=_cols(inp["L1_attn_bq"]), bk=_cols(inp["L1_attn_bk"]),
            bvt=np.ascontiguousarray(np.tile(inp["L1_attn_bv"], (128, 1))),
            bo_=_cols(inp["L1_attn_bo"]),
            mask1=np.ascontiguousarray(m[:, 0:128, :]),
            mask2=np.ascontiguousarray(m[:, 128:192, :]),
            gate_w=np.ascontiguousarray(
                inp["L1_ffn_gate_w"].reshape(8, 128, NEXP).transpose(1, 0, 2)).astype(BFH),
            gbt=np.ascontiguousarray(np.tile(inp["L1_ffn_gate_b"], (128, 1))),
            w1e=np.ascontiguousarray(
                inp["L1_ffn_w1"].reshape(NEXP, 8, 128, 32, 128).transpose(0, 3, 2, 1, 4)).astype(BFH),
            b1e=b1c,
            w2e=np.ascontiguousarray(
                inp["L1_ffn_w2"].reshape(NEXP, 32, 128, 8, 128).transpose(0, 3, 2, 1, 4)).astype(BFH),
            b2e=b2c,
            ws1=st(inp["L1_ffn_ws1"], 8, 32), bs1=_cols(inp["L1_ffn_bs1"]),
            ws2=st(inp["L1_ffn_ws2"], 32, 8), bs2=_cols(inp["L1_ffn_bs2"]),
            ident=np.eye(128, dtype=np.float32)))
    return maps


def kernel(**inp):
    inp = {k: np.asarray(v) for k, v in inp.items()}
    x = inp["L0_attn_x"]
    cores = list(range(NCORES))

    maps, alog_l = _maps_A(inp)
    a_res = run_bass_kernel_spmd(build_launchA(), maps, core_ids=cores).results
    l2_res = run_bass_kernel_spmd(build_launch2(), _maps_L2(inp, x, a_res, alog_l),
                                  core_ids=cores).results
    out = np.zeros((2, 4 * T, D), np.float32)
    for c in range(NCORES):
        b, q = c // 4, c % 4
        out[b, q * T:(q + 1) * T] = _unfm(l2_res[c]["x2_out"]).T
    return out


# version 12
# speedup vs baseline: 1.0283x; 1.0005x over previous
import contextlib
import numpy as np

import concourse.bass as bass
import concourse.mybir as mybir
from concourse.bass_utils import run_bass_kernel_spmd

F32 = mybir.dt.float32
BF16 = mybir.dt.bfloat16
AF = mybir.ActivationFunctionType
ALU = mybir.AluOpType
AX = mybir.AxisListType

NCORES = 8
T = 2048
D = 1024
DIN = 2048
NST = 16
DTR = 64
HID = 4096
NH, DH = 16, 64
KWIN, DIL = 8, 4
HLO = KWIN * DIL
NEXP = 8
EPS_LN = 1e-5
EPS_ATT = 1e-6

ENGINES = ("sync", "gpsimd", "scalar", "vector", "tensor")
DBG_D1 = "all"
DBG_QK = 5


class Prog:

    def __init__(self, nc):
        self.nc = nc
        self.ctx = contextlib.ExitStack()
        self._scopes = [self.ctx]
        self._prefix = [""]
        self.q = {e: [] for e in ENGINES}
        self.pg = {}
        self.pgn = {e: 0 for e in ENGINES}
        for e in ENGINES:
            self.pg[e] = self.ctx.enter_context(nc.semaphore(f"pg_{e}"))
        self._cvals = {}
        self._cbuf = self.ctx.enter_context(
            nc.sbuf_tensor("consts", (128, 16), F32))
        self.pgn["vector"] += 16
        self.streams = []

    @contextlib.contextmanager
    def scope(self, name):
        es = contextlib.ExitStack()
        self._scopes.append(es)
        self._prefix.append(name + "_")
        try:
            yield
        finally:
            self._prefix.pop()
            self._scopes.pop()
            es.close()

    def barrier(self):
        targets = [(self.pg[e2], self.pgn[e2]) for e2 in ENGINES
                   if self.pgn[e2] > 0]
        targets += [(s.h, s.n) for s in self.streams if s.n > 0]
        for e in ENGINES:
            for h, v in targets:
                self.q[e].append(lambda en, s=h, v=v: en.wait_ge(s, v))

    def sbuf(self, name, shape, dtype):
        return self._scopes[-1].enter_context(
            self.nc.sbuf_tensor(self._prefix[-1] + name, list(shape), dtype))

    def psum(self, name, shape, dtype=F32):
        return self._scopes[-1].enter_context(
            self.nc.psum_tensor(self._prefix[-1] + name, list(shape), dtype))

    def cf(self, value, parts=128):
        if value not in self._cvals:
            i = len(self._cvals)
            assert i < 16
            self._cvals[value] = i
        return self._cbuf[0:parts, self._cvals[value]:self._cvals[value] + 1]

    def op(self, eng, fn, step=1):
        self.pgn[eng] += step
        h = self.pg[eng]
        def run(e, fn=fn, h=h, step=step):
            fn(e).then_inc(h, step)
        self.q[eng].append(run)
        return self.pgn[eng]

    class _Stream:
        __slots__ = ("h", "n")

    def stream(self, name):
        s = Prog._Stream()
        s.h = self.ctx.enter_context(self.nc.semaphore(name))
        s.n = 0
        self.streams.append(s)
        return s

    def dma(self, eng, out, in_, stream):
        stream.n += 16
        def run(e, o=out, i=in_, h=stream.h):
            e.dma_start(out=o, in_=i).then_inc(h, 16)
        self.q[eng].append(run)
        return stream.n

    def swait(self, eng, stream, val=None):
        v = stream.n if val is None else val
        if v > 0:
            self.q[eng].append(lambda e, s=stream.h, v=v: e.wait_ge(s, v))

    def wait(self, eng, src_eng, val):
        if val <= 0:
            return
        self.q[eng].append(lambda e, s=self.pg[src_eng], v=val: e.wait_ge(s, v))

    def finish(self):
        if True:
            items = dict(self._cvals)
            cbuf = self._cbuf
            h = self.pg["vector"]
            def memsets(e, items=items, cbuf=cbuf, h=h):
                for v, i in items.items():
                    e.memset(cbuf[:, i:i + 1], float(v)).then_inc(h, 1)
                for _ in range(16 - len(items)):
                    e.nop().then_inc(h, 1)
            self.q["vector"].insert(0, memsets)
            for eng in ENGINES:
                if eng != "vector":
                    self.q[eng].insert(0, lambda e, s=h: e.wait_ge(s, 16))
        finals = [(self.pg[e], self.pgn[e]) for e in ENGINES if self.pgn[e] > 0]
        finals += [(s.h, s.n) for s in self.streams if s.n > 0]
        def final_waits(e, finals=finals):
            for s, v in finals:
                e.wait_ge(s, v)
        self.q["sync"].append(final_waits)
        with self.ctx:
            with self.nc.Block() as block:
                for e in ENGINES:
                    fns = self.q[e]
                    def runner(eng, fns=fns):
                        for fn in fns:
                            fn(eng)
                    getattr(block, e)(runner)
        return self.nc


class PsumPool:

    def __init__(self, P, name, n, shape=(128, 1024)):
        self.P = P
        self.n = n
        self.tiles = [P.psum(f"{name}{i}", shape) for i in range(n)]
        self.rel = [None] * n
        self.uses = 0

    def acquire(self):
        i = self.uses % self.n
        self.uses += 1
        if self.rel[i] is not None:
            eng, cnt = self.rel[i]
            self.P.wait("tensor", eng, cnt)
            self.rel[i] = None
        return i, self.tiles[i]

    def release(self, slot, eng, cnt):
        self.rel[slot] = (eng, cnt)


def cdiv(a, b):
    return (a + b - 1) // b


def blocks_of(total, w):
    return [(i, min(w, total - i)) for i in range(0, total, w)]


def ln_stream(P, *, xdram, nd, tcols, out_bf, xblk, sq, stats, ones1, onesc,
              pool, ld_eng="sync", guard0=0, lds=None):
    ndim = nd * 128
    apply_cnt = guard0
    for bi, (t0, nw) in enumerate(blocks_of(tcols, 512)):
        P.wait(ld_eng, "vector", apply_cnt)
        ldv = P.dma(ld_eng, xblk[:, :, 0:nw], xdram[:, :, t0:t0 + nw], lds)
        s1, tile = pool.acquire()
        P.swait("scalar", lds, ldv)
        for d in range(nd):
            sq_c = P.op("scalar",
                        lambda e, o=sq[:, d, 0:nw], i=xblk[:, d, 0:nw]:
                        e.activation(o, i, AF.Square))
        P.swait("tensor", lds, ldv)
        for d in range(nd):
            P.op("tensor",
                 lambda e, o=tile[0:1, 0:nw], l=ones1, r=xblk[:, d, 0:nw],
                        s=(d == 0), p=(d == nd - 1):
                 e.matmul(o, l, r, start=s, stop=p))
        P.wait("tensor", "scalar", sq_c)
        for d in range(nd):
            mm_c = P.op("tensor",
                        lambda e, o=tile[0:1, 512:512 + nw], l=ones1,
                               r=sq[:, d, 0:nw], s=(d == 0), p=(d == nd - 1):
                        e.matmul(o, l, r, start=s, stop=p))
        P.wait("vector", "tensor", mm_c)
        c1 = P.op("vector", lambda e, o=stats(3, nw), i=tile[0:1, 0:nw]:
                  e.tensor_scalar_mul(o, i, 1.0 / ndim))
        c2 = P.op("vector", lambda e, o=stats(1, nw), i=tile[0:1, 512:512 + nw]:
                  e.tensor_scalar_mul(o, i, 1.0 / ndim))
        pool.release(s1, "vector", c2)
        P.wait("vector", "vector", c1)
        c3 = P.op("vector", lambda e, o=stats(0, nw), a=stats(3, nw):
                  e.tensor_mul(o, a, a))
        P.wait("vector", "vector", c3)
        var_c = P.op("vector", lambda e, o=stats(1, nw), a=stats(1, nw),
                     b=stats(0, nw): e.tensor_sub(o, a, b))
        P.wait("scalar", "vector", var_c)
        c4 = P.op("scalar", lambda e, o=stats(2, nw), i=stats(1, nw),
                  b=P.cf(EPS_LN, 1): e.activation(o, i, AF.Ln, bias=b))
        P.wait("scalar", "scalar", c4)
        rstd_c = P.op("scalar", lambda e, o=stats(2, nw), i=stats(2, nw),
                      sc=P.cf(-0.5, 1): e.activation(o, i, AF.Exp, scale=sc))
        s2, tile2 = pool.acquire()
        P.wait("tensor", "scalar", rstd_c)
        P.wait("tensor", "vector", c1)
        P.op("tensor", lambda e, o=tile2[:, 0:nw], l=onesc, r=stats(3, nw):
             e.matmul(o, l, r, start=True, stop=True))
        bc_c = P.op("tensor", lambda e, o=tile2[:, 512:512 + nw], l=onesc,
                    r=stats(2, nw): e.matmul(o, l, r, start=True, stop=True))
        P.wait("vector", "tensor", bc_c)
        for d in range(nd):
            cs = P.op("vector", lambda e, o=sq[:, d, 0:nw], a=xblk[:, d, 0:nw],
                      b=tile2[:, 0:nw]: e.tensor_sub(o, a, b))
            P.wait("vector", "vector", cs)
            apply_cnt = P.op("vector", lambda e, o=out_bf[:, d, t0:t0 + nw],
                             a=sq[:, d, 0:nw], b=tile2[:, 512:512 + nw]:
                             e.tensor_mul(o, a, b))
        pool.release(s2, "vector", apply_cnt)
    return apply_cnt


def emit_A(P, S, t):
    nc = P.nc
    TE = T + 3
    TS = 512
    NSUB = T // TS
    xfm = t["xfm"]; win = t["win"]; convw = t["convw"]; convb = t["convb"]
    wx = t["wx"]; wdt = t["wdt"]; bdt = t["bdt"]; alog = t["alog"]
    xn_out = t["xn_out"]; res_out = t["res_out"]; xconv_out = t["xconv_out"]
    delta_out = t["delta_out"]; bc_out = t["bc_out"]; fq_out = t["fq_out"]

    xnb = P.sbuf("xnb", (128, 8, T), BF16)
    xnh = P.sbuf("xnh", (128, 8, 4), BF16)
    big1 = P.sbuf("big1", (128, 8192), F32)
    xr = P.sbuf("xr", (128, 16, TS + 3), F32)
    xcdu = P.sbuf("xcdu", (128, 16, TS), F32)
    xcb = P.sbuf("xcb", (128, 16, TS), BF16)
    wstrip = P.sbuf("wstrip", (128, 2, 8, 128), BF16)
    res_st = P.sbuf("res_st", (128, 2, TS), F32)
    da_sb = P.sbuf("da", (128, 2, TS), F32)
    acc_sb = P.sbuf("acc", (128, 2, TS), F32)
    scan_sb = P.sbuf("scan", (128, TS), F32)
    state = P.sbuf("state", (128, 256), F32)
    fq_sb = P.sbuf("fq", (128, 1024), F32)
    stats1 = P.sbuf("stats1", (1, 2048), F32)
    ones1 = P.sbuf("ones1", (128, 1), F32)
    onesb = P.sbuf("onesb", (1, 128), F32)
    cw_sb = P.sbuf("cw", (128, 64), F32)
    cb_sb = P.sbuf("cb", (128, 16), F32)
    wx_sb = P.sbuf("wxb", (128, 16, 96), BF16)
    wdt_sb = P.sbuf("wdtb", (64, DIN), BF16)
    bdt_sb = P.sbuf("bdtb", (128, 16), F32)
    alog_sb = P.sbuf("alogb", (128, 256), F32)
    a_sb = P.sbuf("a", (128, 256), F32)
    drw_sb = P.sbuf("drw", (96, TS), F32)
    drwb = P.sbuf("drwb", (64, TS), BF16)
    sd_sb = P.sbuf("sd", (128, 16), F32)
    sdt_sb = P.sbuf("sdt", (128, 16), F32)

    def stat_row(r, nw):
        return stats1[0:1, r * 512:r * 512 + nw]

    dbua_sb = xcb[:].rearrange("p d w -> p (d w)").bitcast(F32) \
        .rearrange("p (d w) -> p d w", d=8)
    xblk = big1[:].rearrange("p (a d w) -> p a d w", a=2, d=8)[:, 0]
    sqb = big1[:].rearrange("p (a d w) -> p a d w", a=2, d=8)[:, 1]
    bb_sb = big1[:].rearrange("p (d w) -> p d w", d=16)
    dl_sb = xr[:, :, 3:]

    pool = S["pool"]
    sSm = S["SM"]; sGm = S["GM"]; sLn = S["LN"]
    sWinP = [S["G0"], S["G1"]]; sSt = S["ST"]; sBb = S["BB"]

    P.dma("sync", cw_sb[:], convw, sSm)
    P.dma("sync", cb_sb[:], convb, sSm)
    P.dma("sync", bdt_sb[:], bdt, sSm)
    P.dma("sync", alog_sb[:], alog, sSm)
    ld_small = P.dma("sync", alog_sb[:], alog, sSm) if False else sSm.n
    P.dma("gpsimd", wx_sb[:], wx.rearrange("(kt p) o -> p kt o", p=128), sGm)
    gld_small = P.dma("gpsimd", wdt_sb[:], wdt, sGm)

    P.op("vector", lambda e: e.memset(ones1[:], 1.0))
    P.op("vector", lambda e: e.memset(onesb[:], 1.0))
    P.op("vector", lambda e: e.memset(state[:], 0.0))
    init_v = P.op("vector", lambda e: e.memset(sd_sb[:], 0.0))
    P.wait("tensor", "vector", init_v)

    P.swait("scalar", sSm, sSm.n)
    aexp_c = P.op("scalar", lambda e: e.activation(a_sb[:], alog_sb[:], AF.Exp))
    P.wait("vector", "scalar", aexp_c)
    a_ready = P.op("vector", lambda e: e.tensor_scalar_mul(a_sb[:], a_sb[:], -1.0))

    lnm_c = ln_stream(P, xdram=xfm[:, :, 3:], nd=8, tcols=T, out_bf=xnb[:],
              xblk=xblk, sq=sqb, stats=stat_row, ones1=ones1[:], onesc=onesb[:],
              pool=pool, lds=sLn)
    ln_stream(P, xdram=xfm[:, :, 0:3], nd=8, tcols=3, out_bf=xnh[:, :, 0:3],
              xblk=xblk, sq=sqb, stats=stat_row, ones1=ones1[:],
              onesc=onesb[:], pool=pool, guard0=lnm_c, lds=sLn)
    xn_v = P.pgn["vector"]
    P.wait("sync", "vector", xn_v)
    P.dma("sync", xn_out, xnb[:], sSt)

    win_r = win.rearrange("(kt p) o -> p kt o", p=128)

    strip_users = []
    res_sts = []
    scan_cs = []
    nwin = 0
    P.wait("tensor", "vector", xn_v)
    for s in range(NSUB):
        t0 = s * TS
        if s > 0:
            P.swait("vector", sSt)
            P.swait("scalar", sSt)
            P.wait("vector", "scalar", P.pgn["scalar"])
        for ot in range(32):
            is_x = ot < 16
            sbx = nwin % 2
            if nwin >= 2:
                P.wait("gpsimd", "tensor", strip_users[nwin - 2])
            wld = P.dma("gpsimd", wstrip[:, sbx],
                        win_r[:, :, ot * 128:(ot + 1) * 128], sWinP[sbx])
            nwin += 1
            s1, tile = pool.acquire()
            P.swait("tensor", sWinP[sbx], wld)
            for kt in range(8):
                mm_c = P.op("tensor",
                            lambda e, o=tile[:, 0:TS], l=wstrip[:, sbx, kt, :],
                                   r=xnb[:, kt, t0:t0 + TS], s_=(kt == 0), p=(kt == 7):
                            e.matmul(o, l, r, start=s_, stop=p))
            if is_x:
                hsrc = xnh if s == 0 else xnb
                hof = 0 if s == 0 else t0 - 3
                for kt in range(8):
                    mm_c = P.op("tensor",
                                lambda e, o=tile[:, 512:515], l=wstrip[:, sbx, kt, :],
                                       r=hsrc[:, kt, hof:hof + 3], s_=(kt == 0), p=(kt == 7):
                                e.matmul(o, l, r, start=s_, stop=p, skip_group_check=True))
            strip_users.append(mm_c)
            P.wait("vector", "tensor", mm_c)
            if is_x:
                P.op("vector", lambda e, o=xr[:, ot, 3:3 + TS], i=tile[:, 0:TS]:
                     e.tensor_copy(o, i))
                ev_c = P.op("vector", lambda e, o=xr[:, ot, 0:3], i=tile[:, 512:515]:
                            e.tensor_copy(o, i))
            else:
                r = ot - 16
                if len(res_sts) >= 2:
                    P.swait("vector", sSt)
                ev_c = P.op("vector", lambda e, o=res_st[:, r % 2, :], i=tile[:, 0:TS]:
                            e.tensor_copy(o, i))
                P.wait("sync", "vector", ev_c)
                res_sts.append(P.dma("sync", res_out[:, r, s * TS:(s + 1) * TS],
                                     res_st[:, r % 2, :], sSt))
            pool.release(s1, "vector", ev_c)

        P.wait("vector", "vector", P.pgn["vector"])
        if s == 0:
            P.swait("vector", sSm, sSm.n)
        for d in range(16):
            c0 = P.op("vector", lambda e, o=acc_sb[:, 0, :], a=xr[:, d, 0:TS],
                      s1=cw_sb[:, d * 4:d * 4 + 1]: e.tensor_scalar_mul(o, a, s1))
            for j in (1, 2, 3):
                P.wait("vector", "vector", c0)
                c0 = P.op("vector",
                          lambda e, o=acc_sb[:, j % 2, :], a=xr[:, d, j:j + TS],
                                 s1=cw_sb[:, d * 4 + j:d * 4 + j + 1],
                                 b=acc_sb[:, (j - 1) % 2, :]:
                          e.scalar_tensor_tensor(o, a, s1, b, ALU.mult, ALU.add))
            P.wait("vector", "vector", c0)
            conv_c = P.op("vector", lambda e, o=xcdu[:, d, :], i=acc_sb[:, 1, :]:
                          e.tensor_copy(o, i))
        P.wait("scalar", "vector", conv_c)
        for d in range(16):
            silu_c = P.op("scalar", lambda e, o=xcdu[:, d, :], i=xcdu[:, d, :],
                          b=cb_sb[:, d:d + 1]: e.activation(o, i, AF.Silu, bias=b))
        P.wait("vector", "scalar", silu_c)
        for d in range(16):
            xcb_c = P.op("vector", lambda e, o=xcb[:, d, :], i=xcdu[:, d, :]:
                         e.tensor_copy(o, i))
        P.wait("sync", "scalar", silu_c)
        xc_st = P.dma("sync", xconv_out[:, :, s * TS:(s + 1) * TS], xcdu[:], sSt)

        P.wait("tensor", "vector", xcb_c)
        if s == 0:
            P.swait("tensor", sGm, gld_small)
        s1, tile = pool.acquire()
        for kt in range(16):
            mm_c = P.op("tensor", lambda e, o=tile[0:96, 0:TS], l=wx_sb[:, kt, :],
                        r=xcb[:, kt, :], s_=(kt == 0), p=(kt == 15):
                        e.matmul(o, l, r, start=s_, stop=p))
        P.wait("vector", "tensor", mm_c)
        drw_c = P.op("vector", lambda e, o=drw_sb[:, :], i=tile[0:96, 0:TS]:
                     e.tensor_copy(o, i))
        pool.release(s1, "vector", drw_c)
        P.wait("vector", "vector", drw_c)
        drb_c = P.op("vector", lambda e, o=drwb[:], i=drw_sb[0:64, :]:
                     e.tensor_copy(o, i))
        P.wait("sync", "vector", drw_c)
        bc_st = P.dma("sync", bc_out[:, s * TS:(s + 1) * TS], drw_sb[64:96, :], sSt)

        P.wait("tensor", "vector", drb_c)
        dl_c = 0
        for ot in range(16):
            s1, tile = pool.acquire()
            mm_c = P.op("tensor", lambda e, o=tile[:, 0:TS],
                        l=wdt_sb[:, ot * 128:(ot + 1) * 128], r=drwb[:]:
                        e.matmul(o, l, r, start=True, stop=True))
            P.wait("scalar", "tensor", mm_c)
            if dl_c:
                P.wait("scalar", "vector", dl_c)
            c1 = P.op("scalar", lambda e, o=acc_sb[:, 0, :], i=tile[:, 0:TS],
                      b=bdt_sb[:, ot:ot + 1]: e.activation(o, i, AF.Abs, bias=b))
            P.wait("scalar", "scalar", c1)
            c2 = P.op("scalar", lambda e, o=acc_sb[:, 0, :], i=acc_sb[:, 0, :],
                      sc=P.cf(-1.0, 128): e.activation(o, i, AF.Exp, scale=sc))
            P.wait("scalar", "scalar", c2)
            c3 = P.op("scalar", lambda e, o=acc_sb[:, 0, :], i=acc_sb[:, 0, :],
                      b=P.cf(1.0, 128): e.activation(o, i, AF.Ln, bias=b))
            sp_c = P.op("scalar", lambda e, o=acc_sb[:, 1, :], i=tile[:, 0:TS],
                        b=bdt_sb[:, ot:ot + 1]: e.activation(o, i, AF.Relu, bias=b))
            pool.release(s1, "scalar", sp_c)
            P.wait("vector", "scalar", sp_c)
            P.wait("vector", "scalar", c3)
            dl_c = P.op("vector", lambda e, o=dl_sb[:, ot, :], a=acc_sb[:, 0, :],
                        b=acc_sb[:, 1, :]: e.tensor_add(o, a, b))
        P.wait("sync", "vector", dl_c)
        P.dma("sync", delta_out[:, :, s * TS:(s + 1) * TS], dl_sb, sSt)

        P.swait("vector", sSt)
        for d in range(16):
            du_c = P.op("vector", lambda e, o=xcdu[:, d, :], a=dl_sb[:, d, :],
                        b=xcdu[:, d, :]: e.tensor_mul(o, a, b))
        sdt_c = P.op("vector", lambda e, o=sdt_sb[:], i=dl_sb:
                     e.reduce_sum(o, i, axis=AX.X))
        P.wait("vector", "vector", sdt_c)
        sd_c = P.op("vector", lambda e: e.tensor_add(sd_sb[:], sd_sb[:], sdt_sb[:]))

        P.swait("sync", sSt)
        bb_src = bass.AP(bc_out.tensor, s * TS, [[0, 128], [T, 16], [1, TS]])
        bb_ld = P.dma("sync", bb_sb, bb_src, sBb)

        P.wait("scalar", "vector", dl_c)
        if s == 0:
            P.wait("scalar", "vector", a_ready)
        P.swait("vector", sBb, bb_ld)
        P.wait("vector", "vector", du_c)
        P.wait("vector", "tensor", P.pgn["tensor"])
        for d in range(16):
            if d >= 1:
                P.wait("vector", "vector", scan_cs[-1])
            dba_c = P.op("vector", lambda e, o=dbua_sb[:],
                         a=bass.AP(xcdu, d * TS, [[16 * TS, 128], [0, 8], [1, TS]]),
                         b=bb_sb[:, 0:8, :]: e.tensor_mul(o, a, b))
            for n in range(NST):
                ci = d * NST + n
                cg = s * 256 + ci
                buf = cg % 2
                if len(scan_cs) >= 2:
                    P.wait("scalar", "vector", scan_cs[-2])
                da_c = P.op("scalar", lambda e, o=da_sb[:, buf, :], i=dl_sb[:, d, :],
                            sc=a_sb[:, ci:ci + 1]:
                            e.activation(o, i, AF.Exp, scale=sc))
                if n == 8:
                    P.wait("vector", "vector", scan_cs[-1])
                    dba_c = P.op("vector", lambda e, o=dbua_sb[:], d=d,
                                 a=bass.AP(xcdu, d * TS, [[16 * TS, 128], [0, 8], [1, TS]]),
                                 b=bb_sb[:, 8:16, :]: e.tensor_mul(o, a, b))
                P.wait("vector", "scalar", da_c)
                if n in (0, 8):
                    P.wait("vector", "vector", dba_c)
                sc_c = P.op("vector", lambda e, o=scan_sb[:], a=da_sb[:, buf, :],
                            b=dbua_sb[:, n % 8, :], ini=state[:, ci:ci + 1]:
                            e.tensor_tensor_scan(o, a, b, ini, ALU.mult, ALU.add))
                P.wait("vector", "vector", sc_c)
                if s == NSUB - 1:
                    P.op("vector", lambda e, o=fq_sb[:, 512 + ci:513 + ci],
                         i=scan_sb[:, 479:480]: e.tensor_copy(o, i))
                scan_cs.append(P.op("vector", lambda e, o=state[:, ci:ci + 1],
                                    i=scan_sb[:, TS - 1:TS]: e.tensor_copy(o, i)))

    P.wait("vector", "vector", scan_cs[-1])
    P.op("vector", lambda e: e.tensor_copy(fq_sb[:, 0:256], state[:]))
    P.wait("vector", "vector", sd_c)
    t_c = P.op("vector", lambda e: e.reduce_sum(sdt_sb[:], dl_sb[:, :, 480:512],
                                                axis=AX.X))
    P.wait("vector", "vector", t_c)
    s2_c = P.op("vector", lambda e: e.tensor_sub(sdt_sb[:], sd_sb[:], sdt_sb[:]))
    for d in range(16):
        q_c = P.op("vector", lambda e, o=fq_sb[:, 256 + d * 16:256 + (d + 1) * 16],
                   a=a_sb[:, d * 16:(d + 1) * 16], s1=sd_sb[:, d:d + 1]:
                   e.tensor_scalar_mul(o, a, s1))
    P.wait("vector", "vector", s2_c)
    for d in range(16):
        q_c = P.op("vector", lambda e, o=fq_sb[:, 768 + d * 16:768 + (d + 1) * 16],
                   a=a_sb[:, d * 16:(d + 1) * 16], s1=sdt_sb[:, d:d + 1]:
                   e.tensor_scalar_mul(o, a, s1))
    P.wait("scalar", "vector", q_c)
    qe_c = P.op("scalar", lambda e: e.activation(fq_sb[:, 256:512],
                fq_sb[:, 256:512], AF.Exp))
    qe_c = P.op("scalar", lambda e: e.activation(fq_sb[:, 768:1024],
                fq_sb[:, 768:1024], AF.Exp))
    P.wait("sync", "scalar", qe_c)
    P.wait("sync", "vector", P.pgn["vector"])
    P.dma("sync", fq_out, fq_sb[:], sSt)
    return



def mk_streams(P):
    return {"pool": PsumPool(P, "ps", 4),
            "SM": P.stream("sSM"), "LN": P.stream("sLN"), "IO": P.stream("sIO"),
            "BB": P.stream("sBB"), "RES": P.stream("sRES"), "ST": P.stream("sST"),
            "GM": P.stream("sGM"), "G0": P.stream("sG0"), "G1": P.stream("sG1"),
            "BB2": P.stream("sBB2"),
            "CC": P.stream("sCC")}


def _dram_io_A(nc):
    TE = T + 3
    t = {}
    t["xfm"] = nc.dram_tensor("xfm", [128, 8, TE], F32, kind="ExternalInput").ap()
    t["win"] = nc.dram_tensor("win", [1024, 2 * DIN], F32, kind="ExternalInput").ap()
    t["convw"] = nc.dram_tensor("convw", [128, 64], F32, kind="ExternalInput").ap()
    t["convb"] = nc.dram_tensor("convb", [128, 16], F32, kind="ExternalInput").ap()
    t["wx"] = nc.dram_tensor("wx", [DIN, 96], F32, kind="ExternalInput").ap()
    t["wdt"] = nc.dram_tensor("wdt", [DTR, DIN], F32, kind="ExternalInput").ap()
    t["bdt"] = nc.dram_tensor("bdt", [128, 16], F32, kind="ExternalInput").ap()
    t["alog"] = nc.dram_tensor("alog", [128, 256], F32, kind="ExternalInput").ap()
    return t


def build_launchA():
    nc = bass.Bass("TRN2", target_bir_lowering=False)
    P = Prog(nc)
    t = _dram_io_A(nc)
    t["xn_out"] = nc.dram_tensor("xn_out", [128, 8, T], BF16, kind="ExternalOutput").ap()
    t["res_out"] = nc.dram_tensor("res_out", [128, 16, T], F32, kind="ExternalOutput").ap()
    t["xconv_out"] = nc.dram_tensor("xconv_out", [128, 16, T], F32, kind="ExternalOutput").ap()
    t["delta_out"] = nc.dram_tensor("delta_out", [128, 16, T], F32, kind="ExternalOutput").ap()
    t["bc_out"] = nc.dram_tensor("bc_out", [32, T], F32, kind="ExternalOutput").ap()
    t["fq_out"] = nc.dram_tensor("fq_out", [128, 1024], F32, kind="ExternalOutput").ap()
    S = mk_streams(P)
    emit_A(P, S, t)
    return P.finish()

def emit_B2(P, S, t):
    nc = P.nc
    W = [544, 512, 512, 544]
    E = [0, 544, 1056, 1568]
    MB = 544
    u_in = t["u_in"]; dl_in = t["dl_in"]; bc_in = t["bc_in"]; res_in = t["res_in"]
    uex = t["uex"]; dlex = t["dlex"]; resex = t["resex"]; bcex = t["bcex"]
    alog = t["alog"]; dvec = t["dvec"]; wout = t["wout"]; fqa = t["fqa"]
    cmask = t["cmask"]; mam_out = t["mam_out"]

    u_sb = P.sbuf("u", (128, 16, MB), F32)
    dl_sb = P.sbuf("dl", (128, 16, MB), F32)
    du_sb = P.sbuf("du", (128, 16, MB), F32)
    bcb = P.sbuf("bcb", (128, 4, 2, MB), F32)
    y_sb = P.sbuf("y", (128, 16, MB), F32)
    da_sb = P.sbuf("da", (128, 2, MB), F32)
    dbu_sb = P.sbuf("dbu", (128, 16, MB), F32)
    state = P.sbuf("state", (128, 256), F32)
    BCPW = 8 * MB

    def bcb_off(pr, bc):
        return pr * 2 * MB + bc * MB
    alog_sb = P.sbuf("alogb", (128, 256), F32)
    a_sb = P.sbuf("a", (128, 256), F32)
    dv_sb = P.sbuf("dv", (128, 16), F32)
    cm_sb = P.sbuf("cm", (128, 40), F32)
    wo_strip = P.sbuf("wo", (128, 1, 16, 128), BF16)
    _duf = du_sb[:].rearrange("p d w -> p (d w)")
    fq_sb = _duf[:, 0:1024]
    qe_sb = _duf[:, 1024:1280]
    fe_sb = _duf[:, 1280:1536]
    pool = S["pool"]
    sSm = S["SM"]; sFq = S["IO"]; sLd = S["LN"]; sRes = S["RES"]
    sBcP = [S["BB"], S["BB2"]]; sWoP = [S["G0"], S["G0"]]; sSt = S["ST"]

    yb = du_sb[:].rearrange("p d w -> p (d w)")[:, 0:8 * MB].bitcast(BF16) \
        .rearrange("p (d w) -> p d w", d=16)
    mo = du_sb[:, 8:16]

    P.dma("sync", alog_sb[:], alog, sSm)
    P.dma("sync", dv_sb[:], dvec, sSm)
    sm_c = P.dma("sync", cm_sb[:], cmask, sSm)

    P.op("vector", lambda e: e.memset(state[:], 0.0))
    P.swait("scalar", sSm, sm_c)
    aexp_c = P.op("scalar", lambda e: e.activation(a_sb[:], alog_sb[:], AF.Exp))
    P.wait("vector", "scalar", aexp_c)
    a_ready = P.op("vector", lambda e: e.tensor_scalar_mul(a_sb[:], a_sb[:], -1.0))

    P.swait("vector", sSm, sm_c)
    fqv_c = 0
    for j in range(NCORES):
        if j > 0:
            P.wait("sync", "vector", fqv_c)
        fq_c = P.dma("sync", fq_sb, fqa[j], sFq)
        P.swait("vector", sFq, fq_c)
        c1 = P.op("vector", lambda e, j=j: e.tensor_scalar_mul(
            qe_sb, fq_sb[:, 256:512], cm_sb[:, 16 + j:17 + j]))
        P.wait("vector", "vector", c1)
        c2 = P.op("vector", lambda e, j=j: e.scalar_tensor_tensor(
            qe_sb, fq_sb[:, 768:1024], cm_sb[:, 24 + j:25 + j], qe_sb,
            ALU.mult, ALU.add))
        P.wait("vector", "vector", c2)
        c3 = P.op("vector", lambda e, j=j: e.tensor_scalar_add(
            qe_sb, qe_sb, cm_sb[:, 32 + j:33 + j]))
        c4 = P.op("vector", lambda e, j=j: e.tensor_scalar_mul(
            fe_sb, fq_sb[:, 0:256], cm_sb[:, 16 + j:17 + j]))
        P.wait("vector", "vector", c4)
        c5 = P.op("vector", lambda e, j=j: e.scalar_tensor_tensor(
            fe_sb, fq_sb[:, 512:768], cm_sb[:, 24 + j:25 + j], fe_sb,
            ALU.mult, ALU.add))
        P.wait("vector", "vector", c5)
        c6 = P.op("vector", lambda e: e.tensor_mul(state[:], state[:], qe_sb))
        P.wait("vector", "vector", c6)
        fqv_c = P.op("vector", lambda e: e.tensor_add(state[:], state[:], fe_sb))

    scan_cs = []
    y_c = 0
    wo_users = [0, 0]
    stp_l = None
    for s in range(4):
        w = W[s]
        m0 = E[s] - 32
        if s > 0:
            P.swait("sync", sSt)
            P.wait("sync", "vector", P.pgn["vector"])
            P.wait("sync", "scalar", P.pgn["scalar"])
            P.wait("sync", "gpsimd", P.pgn["gpsimd"])
            P.wait("gpsimd", "vector", P.pgn["vector"])
            P.wait("vector", "gpsimd", P.pgn["gpsimd"])
        if s == 0:
            P.dma("sync", u_sb[:, :, 0:32], uex[:, :, 0:32], sLd)
            P.dma("sync", u_sb[:, :, 32:544], u_in[:, :, 0:512], sLd)
            P.dma("sync", dl_sb[:, :, 0:32], dlex[:, :, 0:32], sLd)
            ld_c = P.dma("sync", dl_sb[:, :, 32:544], dl_in[:, :, 0:512], sLd)
        elif s == 3:
            P.dma("sync", u_sb[:, :, 0:512], u_in[:, :, 1536:2048], sLd)
            P.dma("sync", u_sb[:, :, 512:544], uex[:, :, 32:64], sLd)
            P.dma("sync", dl_sb[:, :, 0:512], dl_in[:, :, 1536:2048], sLd)
            ld_c = P.dma("sync", dl_sb[:, :, 512:544], dlex[:, :, 32:64], sLd)
        else:
            P.dma("sync", u_sb[:, :, 0:512], u_in[:, :, m0:m0 + 512], sLd)
            ld_c = P.dma("sync", dl_sb[:, :, 0:512], dl_in[:, :, m0:m0 + 512], sLd)
        def bc_load(n, sl, s=s, m0=m0):
            pr = n % 2
            sBc = sBcP[pr]
            if s == 0:
                P.dma("sync", bcb[:, sl, :, 0:32],
                      bass.AP(bcex.tensor, n * 64, [[0, 128], [16 * 64, 2], [1, 32]]), sBc)
                return P.dma("sync", bcb[:, sl, :, 32:544],
                             bass.AP(bc_in.tensor, n * T, [[0, 128], [16 * T, 2], [1, 512]]), sBc)
            elif s == 3:
                P.dma("sync", bcb[:, sl, :, 0:512],
                      bass.AP(bc_in.tensor, n * T + 1536, [[0, 128], [16 * T, 2], [1, 512]]), sBc)
                return P.dma("sync", bcb[:, sl, :, 512:544],
                             bass.AP(bcex.tensor, n * 64 + 32, [[0, 128], [16 * 64, 2], [1, 32]]), sBc)
            else:
                return P.dma("sync", bcb[:, sl, :, 0:512],
                             bass.AP(bc_in.tensor, n * T + m0, [[0, 128], [16 * T, 2], [1, 512]]), sBc)

        P.swait("vector", sLd, ld_c)
        if s == 0:
            P.wait("vector", "vector", fqv_c)
            stp_l = [fqv_c] * NST
        du_c = P.op("vector", lambda e, o=du_sb[:, :, 0:w], a=dl_sb[:, :, 0:w],
                    b=u_sb[:, :, 0:w]: e.tensor_mul(o, a, b))
        P.wait("vector", "vector", du_c)
        for d in range(16):
            ud_c = P.op("vector", lambda e, o=u_sb[:, d, 0:w], a=u_sb[:, d, 0:w],
                        s1=dv_sb[:, d:d + 1]: e.tensor_scalar_mul(o, a, s1))
        P.swait("scalar", sLd, ld_c)
        if s == 0:
            P.wait("scalar", "vector", fqv_c)

        NIT = 16 * NST
        da_l = [0] * (NIT + 1)
        glob0 = s * NIT

        def issue_da(i):
            n, d = divmod(i, 16)
            ci = d * NST + n
            b = (glob0 + i) % 2
            if glob0 + i >= 2:
                P.wait("scalar", "vector", scan_cs[glob0 + i - 2])
            da_l[i] = P.op("scalar", lambda e, o=da_sb[:, b, 0:w],
                           i_=dl_sb[:, d, 0:w], sc=a_sb[:, ci:ci + 1]:
                           e.activation(o, i_, AF.Exp, scale=sc))

        bcl = [0] * (NST + 2)
        bcl[0] = bc_load(0, 0)
        bcl[1] = bc_load(1, 1)
        issue_da(0)
        GD = 7
        zt_l = [0] * NST
        ztv_l = [0] * NST
        y_l = [0] * NST
        yv_l = [0] * NST
        stc_l = [0] * NST
        dball_l = [0] * NST
        DPW = 16 * MB
        bufs = [dbu_sb, u_sb]

        def emit_dball(m):
            dbuB = bufs[m % 2]
            P.swait("gpsimd", sBcP[m % 2], bcl[m])
            P.wait("gpsimd", "vector", du_c)
            if m == 1:
                P.wait("gpsimd", "gpsimd", y_l[0])
                P.wait("gpsimd", "vector", yv_l[0])
            elif m >= 2:
                P.wait("gpsimd", "gpsimd", y_l[m - 2])
                P.wait("gpsimd", "vector", yv_l[m - 2])
            dball_l[m] = P.op("gpsimd", lambda e, o=dbuB[:, :, 0:w],
                              a=du_sb[:, :, 0:w],
                              b_=bass.AP(bcb, bcb_off(m % 4, 0),
                                         [[BCPW, 128], [0, 16], [1, w]]):
                              e.tensor_mul(o, a, b_))

        emit_dball(0)
        for n in range(NST):
            pr = n % 2
            dbuB = bufs[n % 2]
            if n + 2 < NST:
                if n >= 2:
                    P.wait("sync", "gpsimd", zt_l[n - 2])
                    P.wait("sync", "vector", ztv_l[n - 2])
                bcl[n + 2] = bc_load(n + 2, (n + 2) % 4)
            if 1 <= n < NST - 1 and n + 1 >= 2:
                emit_dball(n + 1)
            P.wait("vector", "gpsimd", dball_l[n])
            P.wait("vector", "vector", stp_l[n])
            for d in range(16):
                i = n * 16 + d
                ci = d * NST + n
                if i + 1 < NIT:
                    issue_da(i + 1)
                P.wait("vector", "scalar", da_l[i])
                sc_c = P.op("vector", lambda e, o=dbuB[:, d, 0:w],
                            a=da_sb[:, (glob0 + i) % 2, 0:w],
                            b_=dbuB[:, d, 0:w], ini=state[:, ci:ci + 1]:
                            e.tensor_tensor_scan(o, a, b_, ini, ALU.mult, ALU.add))
                scan_cs.append(sc_c)
            P.wait("vector", "vector", sc_c)
            stc_l[n] = P.op("vector", lambda e,
                            o=bass.AP(state, n, [[256, 128], [NST, 16]]),
                            i_=bass.AP(dbuB, w - 1, [[DPW, 128], [MB, 16]]):
                            e.tensor_copy(o, i_))
            P.wait("gpsimd", "vector", sc_c)
            P.wait("gpsimd", "vector", stc_l[n])
            zt_l[n] = P.op("gpsimd", lambda e, o=dbuB[:, 0:GD, 0:w],
                           a=dbuB[:, 0:GD, 0:w],
                           b_=bass.AP(bcb, bcb_off(n % 4, 1),
                                      [[BCPW, 128], [0, GD], [1, w]]):
                           e.tensor_mul(o, a, b_))
            P.wait("vector", "vector", stc_l[n])
            ztv_l[n] = P.op("vector", lambda e, o=dbuB[:, GD:16, 0:w],
                            a=dbuB[:, GD:16, 0:w],
                            b_=bass.AP(bcb, bcb_off(n % 4, 1),
                                       [[BCPW, 128], [0, 16 - GD], [1, w]]):
                            e.tensor_mul(o, a, b_))
            P.wait("gpsimd", "gpsimd", zt_l[n])
            P.wait("vector", "vector", ztv_l[n])
            if n == 0:
                P.wait("gpsimd", "vector", ud_c)
                P.wait("vector", "vector", ud_c)
                y_l[0] = P.op("gpsimd", lambda e, o=y_sb[:, 0:GD, 0:w],
                              a=dbuB[:, 0:GD, 0:w], b_=u_sb[:, 0:GD, 0:w]:
                              e.tensor_add(o, a, b_))
                yv_l[0] = P.op("vector", lambda e, o=y_sb[:, GD:16, 0:w],
                               a=dbuB[:, GD:16, 0:w], b_=u_sb[:, GD:16, 0:w]:
                               e.tensor_add(o, a, b_))
                if NST > 1:
                    emit_dball(1)
            else:
                y_l[n] = P.op("gpsimd", lambda e, o=y_sb[:, 0:GD, 0:w],
                              a=y_sb[:, 0:GD, 0:w], b_=dbuB[:, 0:GD, 0:w]:
                              e.tensor_add(o, a, b_))
                yv_l[n] = P.op("vector", lambda e, o=y_sb[:, GD:16, 0:w],
                               a=y_sb[:, GD:16, 0:w], b_=dbuB[:, GD:16, 0:w]:
                               e.tensor_add(o, a, b_))
        y_c = y_l[NST - 1]
        yv_c = yv_l[NST - 1]
        stp_l = stc_l

        P.wait("sync", "gpsimd", y_c)
        P.wait("sync", "vector", yv_c)
        if s == 0:
            P.dma("sync", u_sb[:, :, 0:32], resex[:, :, 0:32], sRes)
            res_c = P.dma("sync", u_sb[:, :, 32:544], res_in[:, :, 0:512], sRes)
        elif s == 3:
            P.dma("sync", u_sb[:, :, 0:512], res_in[:, :, 1536:2048], sRes)
            res_c = P.dma("sync", u_sb[:, :, 512:544], resex[:, :, 32:64], sRes)
        else:
            res_c = P.dma("sync", u_sb[:, :, 0:512], res_in[:, :, m0:m0 + 512], sRes)
        P.swait("scalar", sRes, res_c)
        for d in range(16):
            sil_c = P.op("scalar", lambda e, o=u_sb[:, d, 0:w], i=u_sb[:, d, 0:w]:
                         e.activation(o, i, AF.Silu))
        P.wait("vector", "scalar", sil_c)
        P.wait("vector", "gpsimd", y_c)
        P.wait("vector", "vector", yv_c)
        for d in range(16):
            yb_c = P.op("vector", lambda e, o=yb[:, d, 0:w], a=y_sb[:, d, 0:w],
                        b=u_sb[:, d, 0:w]: e.tensor_mul(o, a, b))

        for ot in range(8):
            sbx = 0
            if wo_users[sbx]:
                P.wait("sync", "tensor", wo_users[sbx])
            wo_c = P.dma("sync", wo_strip[:, sbx], wout[ot], sWoP[sbx])
            s1, tile = pool.acquire()
            P.swait("tensor", sWoP[sbx], wo_c)
            if ot == 0:
                P.wait("tensor", "vector", yb_c)
            for bi, (n0, nw) in enumerate(blocks_of(w, 512)):
                for kt in range(16):
                    mm_c = P.op("tensor", lambda e, o=tile[:, bi * 512:bi * 512 + nw],
                                l=wo_strip[:, sbx, kt, :], r=yb[:, kt, n0:n0 + nw],
                                s_=(kt == 0), p=(kt == 15):
                                e.matmul(o, l, r, start=s_, stop=p))
            wo_users[sbx] = mm_c
            P.wait("scalar", "tensor", mm_c)
            for bi, (n0, nw) in enumerate(blocks_of(w, 512)):
                ev_c = P.op("scalar", lambda e, o=mo[:, ot, n0:n0 + nw],
                            i=tile[:, bi * 512:bi * 512 + nw]: e.copy(o, i))
            pool.release(s1, "scalar", ev_c)
        P.wait("sync", "scalar", ev_c)
        P.dma("sync", mam_out[:, :, E[s]:E[s] + w], mo[:, :, 0:w], sSt)
    return


def build_launchB():
    nc = bass.Bass("TRN2", target_bir_lowering=False)
    P = Prog(nc)
    t = {}
    t["u_in"] = nc.dram_tensor("u_in", [128, 16, T], F32, kind="ExternalInput").ap()
    t["dl_in"] = nc.dram_tensor("dl_in", [128, 16, T], F32, kind="ExternalInput").ap()
    t["bc_in"] = nc.dram_tensor("bc_in", [32, T], F32, kind="ExternalInput").ap()
    t["res_in"] = nc.dram_tensor("res_in", [128, 16, T], F32, kind="ExternalInput").ap()
    t["alog"] = nc.dram_tensor("alog", [128, 256], F32, kind="ExternalInput").ap()
    t["dvec"] = nc.dram_tensor("dvec", [128, 16], F32, kind="ExternalInput").ap()
    t["wout"] = nc.dram_tensor("wout", [DIN, D], F32, kind="ExternalInput").ap()
    t["fqa"] = nc.dram_tensor("fqa", [8, 128, 512], F32, kind="ExternalInput").ap()
    t["cmask"] = nc.dram_tensor("cmask", [128, 16], F32, kind="ExternalInput").ap()
    t["mam_out"] = nc.dram_tensor("mam_out", [128, 8, T], F32, kind="ExternalOutput").ap()
    S = mk_streams(P)
    emit_B(P, S, t)
    return P.finish()

def emit_C(P, S, t):
    nc = P.nc
    X = t["xn_in"].shape[2]
    if X == 2112:
        SUB = [(0, 1056), (1056, 1056)]
    else:
        SUB = [(0, 1024), (1024, 1024)]
    MB = max(w for _, w in SUB)
    xn_in = t["xn_in"]; x_in = t["x_in"]; mam_in = t["mam_in"]
    w1T = t["w1T"]; w2T = t["w2T"]; h1T = t["h1T"]; h2T = t["h2T"]; w3T = t["w3T"]
    w1s = t["w1s"]; w2s = t["w2s"]; h1s = t["h1s"]; h2s = t["h2s"]; w3s = t["w3s"]
    x1_out = t["x1_out"]

    xnb = P.sbuf("xnb", (128, 8, MB), BF16)
    ha = P.sbuf("ha", (128, 32, MB), BF16)
    hb = P.sbuf("hb", (128, 32, MB), BF16)
    wst = P.sbuf("wst", (128, 2, 32, 128), BF16)
    fx = P.sbuf("fx", (128, 8, 544), F32)
    fm = P.sbuf("fm", (128, 8, 544), F32)
    t2_sb = P.sbuf("t2s", (128, 544), F32)
    s1_sb = P.sbuf("s1s", (128, 32), F32)
    s2_sb = P.sbuf("s2s", (128, 32), F32)
    s3_sb = P.sbuf("s3s", (128, 32), F32)
    s4_sb = P.sbuf("s4s", (128, 32), F32)
    s5_sb = P.sbuf("s5s", (128, 8), F32)
    pool = S["pool"]
    sSm = S["SM"]; sXn = S["LN"]; sIo = S["IO"]
    sWP = [S["G0"], S["G1"]]; sSt = S["ST"]

    P.dma("sync", s1_sb[:], w1s, sSm)
    P.dma("sync", s2_sb[:], w2s, sSm)
    P.dma("sync", s3_sb[:], h1s, sSm)
    P.dma("sync", s4_sb[:], h2s, sSm)
    sm_c = P.dma("sync", s5_sb[:], w3s, sSm)

    wu = [0, 0]
    h_c = 0
    ev_c = 0
    for (t0, w) in SUB:
        NT = blocks_of(w, 512)
        if ev_c:
            P.wait("sync", "tensor", P.pgn["tensor"])
        xn_c = P.dma("sync", xnb[:, :, 0:w], xn_in[:, :, t0:t0 + w], sXn)
        first = True
        for ot in range(32):
            pr = ot % 2
            if wu[pr]:
                P.wait("sync", "tensor", wu[pr])
            P.dma("sync", wst[:, pr, 0:8, :], w1T[ot], sWP[pr])
            wl_c = P.dma("sync", wst[:, pr, 8:16, :], w2T[ot], sWP[pr])
            for (n0, nw) in NT:
                s1t, tile = pool.acquire()
                P.swait("tensor", sWP[pr], wl_c)
                if first:
                    P.swait("tensor", sXn, xn_c)
                    first = False
                for kt in range(8):
                    P.op("tensor", lambda e, o=tile[:, 0:nw], l=wst[:, pr, kt, :],
                         r=xnb[:, kt, n0:n0 + nw], s_=(kt == 0), p=(kt == 7):
                         e.matmul(o, l, r, start=s_, stop=p))
                for kt in range(8):
                    mm_c = P.op("tensor", lambda e, o=tile[:, 512:512 + nw],
                                l=wst[:, pr, 8 + kt, :],
                                r=xnb[:, kt, n0:n0 + nw], s_=(kt == 0), p=(kt == 7):
                                e.matmul(o, l, r, start=s_, stop=p))
                wu[pr] = mm_c
                P.wait("scalar", "tensor", mm_c)
                if h_c:
                    P.wait("scalar", "vector", h_c)
                sl_c = P.op("scalar", lambda e, o=t2_sb[:, 0:nw], i=tile[:, 0:nw],
                            sc=s1_sb[:, ot:ot + 1]:
                            e.activation(o, i, AF.Silu, scale=sc))
                P.wait("vector", "tensor", mm_c)
                P.wait("vector", "scalar", sl_c)
                h_c = P.op("vector", lambda e, o=ha[:, ot, n0:n0 + nw], a=t2_sb[:, 0:nw],
                           b=tile[:, 512:512 + nw], s2=s2_sb[:, ot:ot + 1]:
                           e.scalar_tensor_tensor(o, b, s2, a, ALU.mult, ALU.mult))
                pool.release(s1t, "vector", h_c)
        h_was_vec = True

        for stage, (wr, ssc, hin, hout) in enumerate(
                [(h1T, s3_sb, ha, hb), (h2T, s4_sb, hb, ha)]):
            for ot in range(32):
                pr = ot % 2
                if wu[pr]:
                    P.wait("sync", "tensor", wu[pr])
                wl_c = P.dma("sync", wst[:, pr], wr[ot], sWP[pr])
                for (n0, nw) in NT:
                    s1t, tile = pool.acquire()
                    P.swait("tensor", sWP[pr], wl_c)
                    if ot == 0 and n0 == 0:
                        if h_was_vec:
                            P.wait("tensor", "vector", h_c)
                        else:
                            P.wait("tensor", "scalar", h_c)
                    for kt in range(32):
                        mm_c = P.op("tensor", lambda e, o=tile[:, 0:nw],
                                    l=wst[:, pr, kt, :], r=hin[:, kt, n0:n0 + nw],
                                    s_=(kt == 0), p=(kt == 31):
                                    e.matmul(o, l, r, start=s_, stop=p))
                    wu[pr] = mm_c
                    P.wait("scalar", "tensor", mm_c)
                    h_c = P.op("scalar", lambda e, o=hout[:, ot, n0:n0 + nw],
                               i=tile[:, 0:nw], sc=ssc[:, ot:ot + 1]:
                               e.activation(o, i, AF.Gelu, scale=sc))
                    pool.release(s1t, "scalar", h_c)
            h_was_vec = False

        for (n0, nw) in NT:
            P.swait("sync", sSt)
            P.wait("sync", "vector", P.pgn["vector"])
            P.dma("sync", fx[:, :, 0:nw], x_in[:, :, t0 + n0:t0 + n0 + nw], sIo)
            io_c = P.dma("sync", fm[:, :, 0:nw], mam_in[:, :, t0 + n0:t0 + n0 + nw], sIo)
            for ot in range(8):
                pr = ot % 2
                if wu[pr]:
                    P.wait("sync", "tensor", wu[pr])
                wl_c = P.dma("sync", wst[:, pr], w3T[ot], sWP[pr])
                s1t, tile = pool.acquire()
                P.swait("tensor", sWP[pr], wl_c)
                if ot == 0 and n0 == 0:
                    P.wait("tensor", "scalar", h_c)
                for kt in range(32):
                    mm_c = P.op("tensor", lambda e, o=tile[:, 0:nw],
                                l=wst[:, pr, kt, :], r=ha[:, kt, n0:n0 + nw],
                                s_=(kt == 0), p=(kt == 31):
                                e.matmul(o, l, r, start=s_, stop=p))
                wu[pr] = mm_c
                P.wait("vector", "tensor", mm_c)
                if ot == 0:
                    P.swait("vector", sIo, io_c)
                    P.swait("vector", sSm, sm_c)
                c1 = P.op("vector", lambda e, o=fm[:, ot, 0:nw],
                          a=fx[:, ot, 0:nw], b=fm[:, ot, 0:nw]:
                          e.tensor_add(o, a, b))
                P.wait("vector", "vector", c1)
                ev_c = P.op("vector", lambda e, o=fx[:, ot, 0:nw],
                            a=tile[:, 0:nw], s1=s5_sb[:, ot:ot + 1],
                            b=fm[:, ot, 0:nw]:
                            e.scalar_tensor_tensor(o, a, s1, b, ALU.mult, ALU.add))
                pool.release(s1t, "vector", ev_c)
            P.wait("sync", "vector", ev_c)
            P.dma("sync", x1_out[:, :, t0 + n0:t0 + n0 + nw], fx[:, :, 0:nw], sSt)
        h_c = 0
    return


def build_launchC():
    nc = bass.Bass("TRN2", target_bir_lowering=False)
    P = Prog(nc)
    t = {}
    t["xn_in"] = nc.dram_tensor("xn_in", [128, 8, T], BF16, kind="ExternalInput").ap()
    t["x_in"] = nc.dram_tensor("x_in", [128, 8, T], F32, kind="ExternalInput").ap()
    t["mam_in"] = nc.dram_tensor("mam_in", [128, 8, T], F32, kind="ExternalInput").ap()
    t["w1T"] = nc.dram_tensor("w1T", [D, HID], F32, kind="ExternalInput").ap()
    t["w2T"] = nc.dram_tensor("w2T", [D, HID], F32, kind="ExternalInput").ap()
    t["h1T"] = nc.dram_tensor("h1T", [HID, HID], F32, kind="ExternalInput").ap()
    t["h2T"] = nc.dram_tensor("h2T", [HID, HID], F32, kind="ExternalInput").ap()
    t["w3T"] = nc.dram_tensor("w3T", [HID, D], F32, kind="ExternalInput").ap()
    t["w1s"] = nc.dram_tensor("w1s", [128, 32], F32, kind="ExternalInput").ap()
    t["w2s"] = nc.dram_tensor("w2s", [128, 32], F32, kind="ExternalInput").ap()
    t["h1s"] = nc.dram_tensor("h1s", [128, 32], F32, kind="ExternalInput").ap()
    t["h2s"] = nc.dram_tensor("h2s", [128, 32], F32, kind="ExternalInput").ap()
    t["w3s"] = nc.dram_tensor("w3s", [128, 8], F32, kind="ExternalInput").ap()
    t["x1_out"] = nc.dram_tensor("x1_out", [128, 8, T], F32, kind="ExternalOutput").ap()
    S = mk_streams(P)
    emit_C(P, S, t)
    return P.finish()

def emit_D1(P, S, t):
    _dl = ["ln", "qk", "v", "attn", "scale", "all"].index(DBG_D1)
    nc = P.nc
    TEX = T + 2 * HLO
    x1e = t["x1e"]; wqT = t["wqT"]; wkT = t["wkT"]; wvT = t["wvT"]; woT = t["woT"]
    bq = t["bq"]; bk = t["bk"]; bvt = t["bvt"]; bo_ = t["bo_"]
    mask1 = t["mask1"]; mask2 = t["mask2"]; nrm_d = t["nrm_d"]; srow_d = t["srow_d"]
    xn1_out = t["xn1_out"]; at_out = t["at_out"]
    xn1b = P.sbuf("xn1b", (128, 8, TEX), BF16)
    arena = P.sbuf("arena", (128, 8192), F32)
    qb = P.sbuf("qb", (128, 8, T), BF16)
    kb = P.sbuf("kb", (128, 8, TEX), BF16)
    v_tm = P.sbuf("vtm", (128, 17, D), BF16)
    aob = P.sbuf("aob", (128, 8, T), BF16)
    m1_sb = P.sbuf("m1s", (128, 128), F32)
    m2_sb = P.sbuf("m2s", (64, 128), F32)
    e1_sb = P.sbuf("e1s", (128, 128), F32)
    e2_sb = P.sbuf("e2s", (64, 128), F32)
    e1b = P.sbuf("e1b", (128, 128), BF16)
    e2b = P.sbuf("e2b", (64, 128), BF16)
    wst = P.sbuf("wst", (128, 2, 8, 128), BF16)
    bq_sb = P.sbuf("bqs", (128, 8), F32)
    bk_sb = P.sbuf("bks", (128, 8), F32)
    bv_sb = P.sbuf("bvs", (128, D), F32)
    bo_sb = P.sbuf("bos", (128, 8), F32)
    ones1 = P.sbuf("ones1", (128, 1), F32)
    onesb = P.sbuf("onesb", (1, 128), F32)
    ones2 = P.sbuf("ones2", (128, 2), F32)
    onesb1 = P.sbuf("onesb1", (128, 1), BF16)
    pool = S["pool"]
    stats1 = qb[0:1].rearrange("p d w -> p (d w)")[:, 0:4096].bitcast(F32)

    xblk = arena[:].rearrange("p (a d w) -> p a d w", a=2, d=8)[:, 0]
    sqb = arena[:].rearrange("p (a d w) -> p a d w", a=2, d=8)[:, 1]
    rnb = arena[:].rearrange("p (d w) -> p d w", d=16)[:, 0:8]
    att_st = arena[:].rearrange("p (d w) -> p d w", d=16)[:, 8:16]
    nqb = arena[0:2].rearrange("p (x d w) -> p x d w", x=2, d=8)[:, 0]
    sums_st = arena[0:1, 4096:6144]
    q2_sb = arena[:, 6144:6656]

    def stat_row(r, nw):
        return stats1[0:1, r * 512:r * 512 + nw]

    sSm = S["SM"]; sLn = S["LN"]; sW = [S["G0"], S["G1"]]
    sMk = S["IO"]; sNr = S["BB"]; sSt = S["ST"]

    P.dma("sync", bq_sb[:], bq, sSm)
    P.dma("sync", bk_sb[:], bk, sSm)
    P.dma("sync", bv_sb[:], bvt, sSm)
    sm_c = P.dma("sync", bo_sb[:], bo_, sSm)
    iv = P.op("vector", lambda e: e.memset(ones2[:], 0.0))
    P.wait("vector", "vector", iv)
    P.op("vector", lambda e: e.memset(ones2[0:64, 0:1], 1.0))
    P.op("vector", lambda e: e.memset(ones2[64:128, 1:2], 1.0))
    P.op("vector", lambda e: e.memset(ones1[:], 1.0))
    P.op("vector", lambda e: e.memset(onesb[:], 1.0))
    on_c = P.op("vector", lambda e: e.memset(onesb1[:], 1.0))
    P.wait("tensor", "vector", on_c)

    ln_c = ln_stream(P, xdram=x1e, nd=8, tcols=TEX, out_bf=xn1b[:], xblk=xblk,
                     sq=sqb, stats=stat_row, ones1=ones1[:], onesc=onesb[:],
                     pool=pool, lds=sLn)
    P.wait("sync", "vector", ln_c)
    P.dma("sync", xn1_out, xn1b[:, :, HLO:HLO + T], sSt)

    wu = [0, 0]
    P.wait("tensor", "vector", ln_c)
    P.swait("scalar", sSm, sm_c)
    P.swait("vector", sSm, sm_c)

    def qk_phase(nrow, wr, bias_sb, out_bf, tcols, coff):
        nonlocal wu
        nv_c = nm_c = 0
        nz_c = P.pgn["vector"]
        P.wait("vector", "vector", P.pgn["vector"])
        for blk, (t0, nw) in enumerate(blocks_of(tcols, 512)):
            P.swait("vector", sNr)
            for ot in range(8):
                pr = ot % 2
                if wu[pr]:
                    P.wait("sync", "tensor", wu[pr])
                wl_c = P.dma("sync", wst[:, pr], wr[ot], sW[pr])
                s1t, tile = pool.acquire()
                P.swait("tensor", sW[pr], wl_c)
                for kt in range(8):
                    mm_c = P.op("tensor", lambda e, o=tile[:, 0:nw],
                                l=wst[:, pr, kt, :],
                                r=xn1b[:, kt, coff + t0:coff + t0 + nw],
                                s_=(kt == 0), p=(kt == 7):
                                e.matmul(o, l, r, start=s_, stop=p))
                wu[pr] = mm_c
                P.wait("scalar", "tensor", mm_c)
                if DBG_QK >= 1:
                    if nm_c:
                        P.wait("scalar", "tensor", nm_c)
                    sq_c = P.op("scalar", lambda e, o=q2_sb[:, 0:nw], i=tile[:, 0:nw],
                                b=bias_sb[:, ot:ot + 1]:
                                e.activation(o, i, AF.Square, bias=b))
                P.wait("vector", "tensor", mm_c)
                if DBG_QK >= 1:
                    P.wait("vector", "scalar", sq_c)
                ev_c = P.op("vector", lambda e, o=out_bf[:, ot, t0:t0 + nw],
                            i=tile[:, 0:nw], s1=bias_sb[:, ot:ot + 1]:
                            e.tensor_scalar_add(o, i, s1))
                pool.release(s1t, "vector", ev_c)
                if DBG_QK >= 1:
                    s2t, tile2 = pool.acquire()
                    P.wait("tensor", "scalar", sq_c)
                    nm_c = P.op("tensor", lambda e, o=tile2[0:2, 0:nw], l=ones2[:],
                                r=q2_sb[:, 0:nw]: e.matmul(o, l, r, start=True, stop=True))
                    P.wait("vector", "tensor", nm_c)
                    nv_c = P.op("vector", lambda e, o=nqb[:, ot, 0:nw],
                                i=tile2[0:2, 0:nw]: e.tensor_copy(o, i))
                    pool.release(s2t, "vector", nv_c)
            if DBG_QK < 2:
                continue
            P.wait("scalar", "vector", nv_c)
            c1 = P.op("scalar", lambda e, o=nqb[:, :, 0:nw], i=nqb[:, :, 0:nw],
                      b=P.cf(1e-12, 2): e.activation(o, i, AF.Ln, bias=b))
            P.wait("scalar", "scalar", c1)
            c2 = P.op("scalar", lambda e, o=nqb[:, :, 0:nw], i=nqb[:, :, 0:nw],
                      sc=P.cf(0.5, 2): e.activation(o, i, AF.Exp, scale=sc))
            P.wait("vector", "scalar", c2)
            c3 = P.op("vector", lambda e, o=nqb[:, :, 0:nw], i=nqb[:, :, 0:nw]:
                      e.tensor_scalar_add(o, i, float(EPS_ATT)))
            P.wait("vector", "vector", c3)
            c4 = P.op("vector", lambda e, o=nqb[:, :, 0:nw], i=nqb[:, :, 0:nw]:
                      e.reciprocal(o, i))
            if DBG_QK < 3:
                continue
            P.wait("sync", "vector", c4)
            dst_e = bass.AP(nrm_d, t0, [[16 * TEX, 1], [TEX, 8], [1, nw]])
            dst_o = bass.AP(nrm_d, 16 * TEX + t0, [[16 * TEX, 1], [TEX, 8], [1, nw]])
            P.dma("sync", dst_e, nqb[0:1, :, 0:nw], sNr)
            P.dma("sync", dst_o, nqb[1:2, :, 0:nw], sNr)
        for blk, (t0, nw) in enumerate(blocks_of(tcols, 512) if DBG_QK >= 4 else []):
            P.swait("sync", sNr)
            P.wait("sync", "vector", P.pgn["vector"])
            src_e = bass.AP(nrm_d, t0, [[0, 64], [TEX, 8], [1, nw]])
            src_o = bass.AP(nrm_d, 16 * TEX + t0, [[0, 64], [TEX, 8], [1, nw]])
            P.dma("sync", rnb[0:64, :, 0:nw], src_e, sNr)
            rb_c = P.dma("sync", rnb[64:128, :, 0:nw], src_o, sNr)
            P.swait("vector", sNr, rb_c)
            for ot in range(8):
                nz_c = P.op("vector", lambda e, o=out_bf[:, ot, t0:t0 + nw],
                            a=out_bf[:, ot, t0:t0 + nw], b=rnb[:, ot, 0:nw]:
                            e.tensor_mul(o, a, b))
        return nz_c

    if _dl >= 1:
        qn_c = qk_phase(0, wqT, bq_sb, qb[:], T, HLO)
        kn_c = qk_phase(1, wkT, bk_sb, kb[:], TEX, 0)

    vt_c = 0
    for tt in range(17 if _dl >= 2 else 0):
        rows = 128 if tt < 16 else 64
        for oc in range(8):
            pr = oc % 2
            o0 = oc * 128
            if wu[pr]:
                P.wait("sync", "tensor", wu[pr])
            wl_c = P.dma("sync", wst[:, pr], wvT[oc], sW[pr])
            s1t, tile = pool.acquire()
            P.swait("tensor", sW[pr], wl_c)
            for kt in range(8):
                mm_c = P.op("tensor", lambda e, o=tile[0:rows, 0:128],
                            l=xn1b[:, kt, tt * 128:tt * 128 + rows],
                            r=wst[:, pr, kt, :], s_=(kt == 0), p=(kt == 7):
                            e.matmul(o, l, r, start=s_, stop=p))
            wu[pr] = mm_c
            P.wait("vector", "tensor", mm_c)
            vt_c = P.op("vector", lambda e, o=v_tm[0:rows, tt, o0:o0 + 128],
                        a=tile[0:rows, 0:128], b=bv_sb[0:rows, o0:o0 + 128]:
                        e.tensor_add(o, a, b))
            pool.release(s1t, "vector", vt_c)

    P.wait("tensor", "vector", vt_c)
    ao_c = mk_c = sums_c = 0
    sums_ev = [0, 0]
    for qt in range(16 if _dl >= 3 else 0):
        if mk_c:
            P.wait("sync", "gpsimd", mk_c)
        P.dma("sync", m1_sb[:], mask1[qt], sMk)
        mk_ld = P.dma("sync", m2_sb[:], mask2[qt], sMk)
        P.swait("vector", sNr)
        s_sums, tsum = pool.acquire()
        for h in range(16):
            hb = (h % 2) * 64
            ho = h // 2
            sb_half = (h % 2) * 512
            s1t, tile = pool.acquire()
            P.op("tensor", lambda e, o=tile[:, 0:128],
                 l=kb[hb:hb + 64, ho, qt * 128:qt * 128 + 128],
                 r=qb[hb:hb + 64, ho, qt * 128:qt * 128 + 128]:
                 e.matmul(o, l, r, start=True, stop=True))
            sc2 = P.op("tensor", lambda e, o=tile[0:64, 512:640],
                       l=kb[hb:hb + 64, ho, qt * 128 + 128:qt * 128 + 192],
                       r=qb[hb:hb + 64, ho, qt * 128:qt * 128 + 128]:
                       e.matmul(o, l, r, start=True, stop=True))
            P.wait("scalar", "tensor", sc2)
            if mk_c:
                P.wait("scalar", "gpsimd", mk_c)
            P.op("scalar", lambda e, o=e1_sb[:], i=tile[:, 0:128]:
                 e.activation(o, i, AF.Exp))
            x2c = P.op("scalar", lambda e, o=e2_sb[:], i=tile[0:64, 512:640]:
                       e.activation(o, i, AF.Exp))
            P.wait("gpsimd", "scalar", x2c)
            if h == 0:
                P.swait("gpsimd", sMk, mk_ld)
            P.op("gpsimd", lambda e: e.tensor_mul(e1b[:], e1_sb[:], m1_sb[:]))
            m2c = P.op("gpsimd", lambda e: e.tensor_mul(e2b[:], e2_sb[:], m2_sb[:]))
            pool.release(s1t, "scalar", x2c)
            mk_c = m2c
            P.wait("tensor", "gpsimd", m2c)
            if sums_ev[h % 2]:
                P.wait("tensor", "vector", sums_ev[h % 2])
            P.op("tensor", lambda e, o=tsum[0:1, sb_half:sb_half + 128],
                 l=onesb1[:], r=e1b[:]: e.matmul(o, l, r, start=True, stop=False))
            sm_mm = P.op("tensor", lambda e, o=tsum[0:1, sb_half:sb_half + 128],
                         l=onesb1[0:64, :], r=e2b[:]:
                         e.matmul(o, l, r, start=False, stop=True))
            P.wait("vector", "tensor", sm_mm)
            sums_ev[h % 2] = P.op("vector",
                lambda e, o=sums_st[:, h * 128:(h + 1) * 128],
                       i=tsum[0:1, sb_half:sb_half + 128]: e.tensor_copy(o, i))
            s2t, tile2 = pool.acquire()
            P.op("tensor", lambda e, o=tile2[hb:hb + 64, 0:128],
                 l=v_tm[:, qt, h * 64:(h + 1) * 64], r=e1b[:]:
                 e.matmul(o, l, r, start=True, stop=False))
            av2 = P.op("tensor", lambda e, o=tile2[hb:hb + 64, 0:128],
                       l=v_tm[0:64, qt + 1, h * 64:(h + 1) * 64], r=e2b[:]:
                       e.matmul(o, l, r, start=False, stop=True))
            P.wait("vector", "tensor", av2)
            ao_c = P.op("vector",
                        lambda e, o=aob[hb:hb + 64, ho, qt * 128:(qt + 1) * 128],
                        i=tile2[hb:hb + 64, 0:128]: e.tensor_copy(o, i))
            pool.release(s2t, "vector", ao_c)
        pool.release(s_sums, "vector", sums_ev[1])
        P.wait("sync", "vector", max(sums_ev))
        dst = bass.AP(srow_d, qt * 128, [[16 * T, 1], [T, 16], [1, 128]])
        P.dma("sync", dst, sums_st.rearrange("p (h w) -> p h w", h=16), sNr)

    as_c = 0
    P.wait("vector", "vector", P.pgn["vector"])
    for blk in range(4 if _dl >= 4 else 0):
        t0 = blk * 512
        P.swait("sync", sNr)
        P.wait("sync", "vector", P.pgn["vector"])
        src_e = bass.AP(srow_d, t0, [[0, 64], [2 * T, 8], [1, 512]])
        src_o = bass.AP(srow_d, T + t0, [[0, 64], [2 * T, 8], [1, 512]])
        P.dma("sync", rnb[0:64, :, 0:512], src_e, sNr)
        rb_c = P.dma("sync", rnb[64:128, :, 0:512], src_o, sNr)
        P.swait("vector", sNr, rb_c)
        rc_c = P.op("vector", lambda e, o=rnb[:, :, 0:512], i=rnb[:, :, 0:512]:
                    e.reciprocal(o, i))
        P.wait("vector", "vector", rc_c)
        for ot in range(8):
            as_c = P.op("vector", lambda e, o=aob[:, ot, t0:t0 + 512],
                        a=aob[:, ot, t0:t0 + 512], b=rnb[:, ot, 0:512]:
                        e.tensor_mul(o, a, b))

    P.wait("tensor", "vector", as_c)
    ev_c = 0
    for blk in range(4 if _dl >= 5 else 0):
        t0 = blk * 512
        if blk > 0:
            P.swait("scalar", sSt)
        for ot in range(8):
            pr = ot % 2
            if wu[pr]:
                P.wait("sync", "tensor", wu[pr])
            wl_c = P.dma("sync", wst[:, pr], woT[ot], sW[pr])
            s1t, tile = pool.acquire()
            P.swait("tensor", sW[pr], wl_c)
            for kt in range(8):
                mm_c = P.op("tensor", lambda e, o=tile[:, 0:512],
                            l=wst[:, pr, kt, :], r=aob[:, kt, t0:t0 + 512],
                            s_=(kt == 0), p=(kt == 7):
                            e.matmul(o, l, r, start=s_, stop=p))
            wu[pr] = mm_c
            P.wait("scalar", "tensor", mm_c)
            ev_c = P.op("scalar", lambda e, o=att_st[:, ot, :], i=tile[:, 0:512],
                        b=bo_sb[:, ot:ot + 1]:
                        e.activation(o, i, AF.Identity, bias=b))
            pool.release(s1t, "scalar", ev_c)
        P.wait("sync", "scalar", ev_c)
        P.dma("sync", at_out[:, :, t0:t0 + 512], att_st[:], sSt)
    return


def build_launchD1():
    nc = bass.Bass("TRN2", target_bir_lowering=False)
    P = Prog(nc)
    TEX = T + 2 * HLO
    t = {}
    t["x1e"] = nc.dram_tensor("x1e", [128, 8, TEX], F32, kind="ExternalInput").ap()
    t["wqT"] = nc.dram_tensor("wqT", [D, D], F32, kind="ExternalInput").ap()
    t["wkT"] = nc.dram_tensor("wkT", [D, D], F32, kind="ExternalInput").ap()
    t["wvT"] = nc.dram_tensor("wvT", [D, D], F32, kind="ExternalInput").ap()
    t["woT"] = nc.dram_tensor("woT", [D, D], F32, kind="ExternalInput").ap()
    t["bq"] = nc.dram_tensor("bq", [128, 8], F32, kind="ExternalInput").ap()
    t["bk"] = nc.dram_tensor("bk", [128, 8], F32, kind="ExternalInput").ap()
    t["bvt"] = nc.dram_tensor("bvt", [128, D], F32, kind="ExternalInput").ap()
    t["bo_"] = nc.dram_tensor("bo_", [128, 8], F32, kind="ExternalInput").ap()
    t["mask1"] = nc.dram_tensor("mask1", [16, 128, 128], F32, kind="ExternalInput").ap()
    t["mask2"] = nc.dram_tensor("mask2", [16, 64, 128], F32, kind="ExternalInput").ap()
    t["nrm_d"] = nc.dram_tensor("nrm_d", [2, 16, TEX], F32)
    t["srow_d"] = nc.dram_tensor("srow_d", [16, T], F32)
    t["xn1_out"] = nc.dram_tensor("xn1_out", [128, 8, T], BF16, kind="ExternalOutput").ap()
    t["at_out"] = nc.dram_tensor("at_out", [128, 8, T], F32, kind="ExternalOutput").ap()
    S = mk_streams(P)
    emit_D1(P, S, t)
    return P.finish()

def emit_D2(P, S, t):
    nc = P.nc
    TH = 1024
    NSUB = T // TH
    xn1_in = t["xn1_in"]; x1_in = t["x1_in"]; at_in = t["at_in"]
    gate_w = t["gate_w"]; gbt = t["gbt"]; w1e = t["w1e"]; b1e = t["b1e"]
    w2e = t["w2e"]; b2e = t["b2e"]; ws1 = t["ws1"]; bs1 = t["bs1"]
    ws2 = t["ws2"]; bs2 = t["bs2"]; ident = t["ident"]; comb_d = t["comb_d"]
    x2_out = t["x2_out"]

    xn1b = P.sbuf("xn1b", (128, 8, T), BF16)
    hm = P.sbuf("hm", (128, 32, TH), BF16)
    acc = P.sbuf("acc", (128, 8, TH), F32)
    cbb = P.sbuf("cbb", (128, 8, TH), F32)
    stage = P.sbuf("stage", (128, 8, 512), F32)
    wst = P.sbuf("wst", (128, 2, 32, 128), BF16)
    lg_sb = P.sbuf("lg", (128, 16, 8), F32)
    m1_sb = P.sbuf("m1", (128, 16), F32)
    m2_sb = P.sbuf("m2", (128, 16), F32)
    eq1_sb = P.sbuf("eq1", (128, 16, 8), F32)
    eq2_sb = P.sbuf("eq2", (128, 16, 8), F32)
    tm_sb = P.sbuf("tm", (128, 8), F32)
    gw_sb = P.sbuf("gw", (128, 16, 2), F32)
    cmb_sb = P.sbuf("cmb", (128, 16, 8), F32)
    cfm_sb = P.sbuf("cfm", (8, T), F32)
    id_sb = P.sbuf("idn", (128, 128), F32)
    gr_sb = P.sbuf("gr", (128, 8, 8), BF16)
    gbt_sb = P.sbuf("gbt_s", (128, 8), F32)
    b1_sb = P.sbuf("b1s", (128, 8, 32), F32)
    b2_sb = P.sbuf("b2s", (128, 8, 8), F32)
    bs1_sb = P.sbuf("bs1s", (128, 32), F32)
    bs2_sb = P.sbuf("bs2s", (128, 8), F32)
    pool = S["pool"]

    sSm = S["SM"]; sXn = S["LN"]; sW = [S["G0"], S["G1"]]
    sCb = S["BB"]; sIo = S["IO"]; sSt = S["ST"]; sGm = S["GM"]

    P.dma("sync", gr_sb[:], gate_w, sGm)
    P.dma("sync", gbt_sb[:], gbt, sSm)
    P.dma("sync", b1_sb[:], b1e, sSm)
    P.dma("sync", b2_sb[:], b2e, sSm)
    P.dma("sync", bs1_sb[:], bs1, sSm)
    P.dma("sync", bs2_sb[:], bs2, sSm)
    P.dma("sync", id_sb[:], ident, sSm)
    sm_c = P.dma("sync", xn1b[:], xn1_in, sXn)

    P.swait("tensor", sXn, sm_c)
    P.swait("tensor", sGm, sGm.n)
    P.swait("vector", sSm, sSm.n)
    lg_c = 0
    for tt in range(16):
        s1t, tile = pool.acquire()
        for kt in range(8):
            mm_c = P.op("tensor", lambda e, o=tile[0:128, 0:8],
                        l=xn1b[:, kt, tt * 128:(tt + 1) * 128], r=gr_sb[:, kt, :],
                        s_=(kt == 0), p=(kt == 7):
                        e.matmul(o, l, r, start=s_, stop=p))
        P.wait("vector", "tensor", mm_c)
        lg_c = P.op("vector", lambda e, o=lg_sb[:, tt, :], a=tile[0:128, 0:8],
                    b=gbt_sb[:]: e.tensor_add(o, a, b))
        pool.release(s1t, "vector", lg_c)
    P.wait("vector", "vector", lg_c)
    m1_c = P.op("vector", lambda e: e.reduce_max(m1_sb[:], lg_sb[:], axis=AX.X))
    eq_c = 0
    for tt in range(16):
        P.wait("vector", "vector", m1_c if tt == 0 else eq_c)
        e1c = P.op("vector", lambda e, o=eq1_sb[:, tt, :], a=lg_sb[:, tt, :],
                   s1=m1_sb[:, tt:tt + 1]:
                   e.tensor_single_scalar(o, a, s1, ALU.is_equal))
        P.wait("vector", "vector", e1c)
        eq_c = P.op("vector", lambda e, o=tm_sb[:], a=eq1_sb[:, tt, :],
                    b=lg_sb[:, tt, :]:
                    e.scalar_tensor_tensor(o, a, -1e30, b, ALU.mult, ALU.add))
        P.wait("vector", "vector", eq_c)
        m2c = P.op("vector", lambda e, o=m2_sb[:, tt:tt + 1], i=tm_sb[:]:
                   e.reduce_max(o, i, axis=AX.X))
        P.wait("vector", "vector", m2c)
        eq_c = P.op("vector", lambda e, o=eq2_sb[:, tt, :], a=tm_sb[:],
                    s1=m2_sb[:, tt:tt + 1]:
                    e.tensor_single_scalar(o, a, s1, ALU.is_equal))
    P.wait("vector", "vector", eq_c)
    d_c = P.op("vector", lambda e: e.tensor_sub(m1_sb[:], m1_sb[:], m2_sb[:]))
    P.wait("scalar", "vector", d_c)
    sg_c = P.op("scalar", lambda e, o=gw_sb[:, :, 0], i=m1_sb[:]:
                e.activation(o, i, AF.Sigmoid))
    P.wait("vector", "scalar", sg_c)
    g2_c = P.op("vector", lambda e, o=gw_sb[:, :, 1], i=gw_sb[:, :, 0]:
                e.tensor_scalar(o, i, -1.0, 1.0, ALU.mult, ALU.add))
    cm_c = 0
    for tt in range(16):
        P.wait("vector", "vector", g2_c if tt == 0 else cm_c)
        c1 = P.op("vector", lambda e, o=cmb_sb[:, tt, :], a=eq1_sb[:, tt, :],
                  s1=gw_sb[:, tt:tt + 1, 0]: e.tensor_scalar_mul(o, a, s1))
        P.wait("vector", "vector", c1)
        cm_c = P.op("vector", lambda e, o=cmb_sb[:, tt, :], a=eq2_sb[:, tt, :],
                    s1=gw_sb[:, tt:tt + 1, 1], b=cmb_sb[:, tt, :]:
                    e.scalar_tensor_tensor(o, a, s1, b, ALU.mult, ALU.add))
    P.wait("tensor", "vector", cm_c)
    P.swait("tensor", sSm, sSm.n)
    tr_c = 0
    for tt in range(16):
        s1t, tile = pool.acquire()
        t_c = P.op("tensor", lambda e, o=tile[0:8, 0:128], i=cmb_sb[:, tt, :],
                   idn=id_sb[:]: e.transpose(o, i, idn))
        P.wait("vector", "tensor", t_c)
        tr_c = P.op("vector", lambda e, o=cfm_sb[:, tt * 128:(tt + 1) * 128],
                    i=tile[0:8, 0:128]: e.tensor_copy(o, i))
        pool.release(s1t, "vector", tr_c)
    P.wait("sync", "vector", tr_c)
    cb_st = P.dma("sync", bass.AP(comb_d, 0, [[T, 8], [1, T]]), cfm_sb[:], sCb)

    wu = [0, 0]
    ac_c = 0
    for s in range(NSUB):
        t0 = s * TH
        P.swait("sync", sCb)
        if ac_c:
            P.wait("sync", "vector", P.pgn["vector"])
        cb_c = P.dma("sync", cbb[:],
                     bass.AP(comb_d, t0, [[0, 128], [T, 8], [1, TH]]), sCb)
        P.swait("vector", sCb, cb_c)
        P.wait("vector", "vector", P.pgn["vector"])
        z_c = P.op("vector", lambda e: e.memset(acc[:], 0.0))

        for ex in range(NEXP + 1):
            shared = ex == NEXP
            w1r = ws1 if shared else w1e[ex]
            w2r = ws2 if shared else w2e[ex]
            bcol1 = bs1_sb if shared else b1_sb[:, ex, :]
            bcol2 = bs2_sb if shared else b2_sb[:, ex, :]
            for ot in range(32):
                pr = ot % 2
                if wu[pr]:
                    P.wait("sync", "tensor", wu[pr])
                wl_c = P.dma("sync", wst[:, pr, 0:8, :], w1r[ot], sW[pr])
                s1t, tile = pool.acquire()
                P.swait("tensor", sW[pr], wl_c)
                for nt in range(2):
                    for kt in range(8):
                        mm_c = P.op("tensor", lambda e, o=tile[:, nt * 512:nt * 512 + 512],
                                    l=wst[:, pr, kt, :],
                                    r=xn1b[:, kt, t0 + nt * 512:t0 + nt * 512 + 512],
                                    s_=(kt == 0), p=(kt == 7):
                                    e.matmul(o, l, r, start=s_, stop=p))
                wu[pr] = mm_c
                P.wait("scalar", "tensor", mm_c)
                h_c = P.op("scalar", lambda e, o=hm[:, ot, :], i=tile[:, 0:1024],
                           b=bcol1[:, ot:ot + 1]:
                           e.activation(o, i, AF.Gelu, bias=b))
                pool.release(s1t, "scalar", h_c)
            for ot in range(8):
                pr = ot % 2
                if wu[pr]:
                    P.wait("sync", "tensor", wu[pr])
                wl_c = P.dma("sync", wst[:, pr], w2r[ot], sW[pr])
                s1t, tile = pool.acquire()
                P.swait("tensor", sW[pr], wl_c)
                if ot == 0:
                    P.wait("tensor", "scalar", h_c)
                for nt in range(2):
                    for kt in range(32):
                        mm_c = P.op("tensor", lambda e, o=tile[:, nt * 512:nt * 512 + 512],
                                    l=wst[:, pr, kt, :], r=hm[:, kt, nt * 512:nt * 512 + 512],
                                    s_=(kt == 0), p=(kt == 31):
                                    e.matmul(o, l, r, start=s_, stop=p))
                wu[pr] = mm_c
                P.wait("vector", "tensor", mm_c)
                if ex == 0 and ot == 0:
                    P.wait("vector", "vector", z_c)
                for nt in range(2):
                    if ac_c:
                        P.wait("vector", "vector", ac_c)
                    if shared:
                        c1 = P.op("vector", lambda e, o=stage[:, ot, :],
                                  a=tile[:, nt * 512:nt * 512 + 512],
                                  s1=bcol2[:, ot:ot + 1]:
                                  e.tensor_scalar_add(o, a, s1))
                    else:
                        c1 = P.op("vector", lambda e, o=stage[:, ot, :],
                                  a=tile[:, nt * 512:nt * 512 + 512],
                                  s1=bcol2[:, ot:ot + 1],
                                  b=cbb[:, ex, nt * 512:nt * 512 + 512]:
                                  e.scalar_tensor_tensor(o, a, s1, b, ALU.add, ALU.mult))
                    P.wait("vector", "vector", c1)
                    ac_c = P.op("vector", lambda e, o=acc[:, ot, nt * 512:nt * 512 + 512],
                                a=acc[:, ot, nt * 512:nt * 512 + 512],
                                b=stage[:, ot, :]: e.tensor_add(o, a, b))
                pool.release(s1t, "vector", ac_c)

        for half in range(2):
            hs = t0 + half * 512
            P.wait("sync", "vector", P.pgn["vector"])
            io_c = P.dma("sync", stage[:], x1_in[:, :, hs:hs + 512], sIo)
            P.swait("vector", sIo, io_c)
            for ot in range(8):
                a1_c = P.op("vector", lambda e, o=acc[:, ot, half * 512:half * 512 + 512],
                            a=acc[:, ot, half * 512:half * 512 + 512],
                            b=stage[:, ot, :]: e.tensor_add(o, a, b))
            P.wait("sync", "vector", a1_c)
            io_c = P.dma("sync", stage[:], at_in[:, :, hs:hs + 512], sIo)
            P.swait("vector", sIo, io_c)
            for ot in range(8):
                a2_c = P.op("vector", lambda e, o=acc[:, ot, half * 512:half * 512 + 512],
                            a=acc[:, ot, half * 512:half * 512 + 512],
                            b=stage[:, ot, :]: e.tensor_add(o, a, b))
            P.wait("sync", "vector", a2_c)
            P.dma("sync", x2_out[:, :, hs:hs + 512],
                  acc[:, :, half * 512:half * 512 + 512], sSt)
        P.swait("vector", sSt)
    return


def build_launchD2():
    nc = bass.Bass("TRN2", target_bir_lowering=False)
    P = Prog(nc)
    t = {}
    t["xn1_in"] = nc.dram_tensor("xn1_in", [128, 8, T], BF16, kind="ExternalInput").ap()
    t["x1_in"] = nc.dram_tensor("x1_in", [128, 8, T], F32, kind="ExternalInput").ap()
    t["at_in"] = nc.dram_tensor("at_in", [128, 8, T], F32, kind="ExternalInput").ap()
    t["gate_w"] = nc.dram_tensor("gate_w", [D, NEXP], F32, kind="ExternalInput").ap()
    t["gbt"] = nc.dram_tensor("gbt", [128, NEXP], F32, kind="ExternalInput").ap()
    t["w1e"] = nc.dram_tensor("w1e", [NEXP, D, HID], F32, kind="ExternalInput").ap()
    t["b1e"] = nc.dram_tensor("b1e", [128, NEXP, 32], F32, kind="ExternalInput").ap()
    t["w2e"] = nc.dram_tensor("w2e", [NEXP, HID, D], F32, kind="ExternalInput").ap()
    t["b2e"] = nc.dram_tensor("b2e", [128, NEXP, 8], F32, kind="ExternalInput").ap()
    t["ws1"] = nc.dram_tensor("ws1", [D, HID], F32, kind="ExternalInput").ap()
    t["bs1"] = nc.dram_tensor("bs1", [128, 32], F32, kind="ExternalInput").ap()
    t["ws2"] = nc.dram_tensor("ws2", [HID, D], F32, kind="ExternalInput").ap()
    t["bs2"] = nc.dram_tensor("bs2", [128, 8], F32, kind="ExternalInput").ap()
    t["ident"] = nc.dram_tensor("ident", [128, 128], F32, kind="ExternalInput").ap()
    t["comb_d"] = nc.dram_tensor("comb_d", [NEXP, T], F32)
    t["x2_out"] = nc.dram_tensor("x2_out", [128, 8, T], F32, kind="ExternalOutput").ap()
    S = mk_streams(P)
    emit_D2(P, S, t)
    return P.finish()

def _fm(aT):
    Dd, Tt = aT.shape
    return np.ascontiguousarray(aT.reshape(Dd // 128, 128, Tt).transpose(1, 0, 2))


def _cols(v):
    return np.ascontiguousarray(np.asarray(v).reshape(-1, 128).T)




def build_launch2():
    nc = bass.Bass("TRN2", target_bir_lowering=False)
    P = Prog(nc)
    TEX = T + 2 * HLO
    EI = lambda nm, sh, dt=F32: nc.dram_tensor(nm, sh, dt, kind="ExternalInput").ap()
    t = {}
    for nm, sh, *dt in [("u_in", [128, 16, T]), ("dl_in", [128, 16, T]),
                   ("res_in", [128, 16, T]), ("bc_in", [32, T]),
                   ("uex", [128, 16, 64]), ("dlex", [128, 16, 64]),
                   ("resex", [128, 16, 64]), ("bcex", [32, 64]),
                   ("alog", [128, 256]), ("dvec", [128, 16]),
                   ("wout", [8, 128, 16, 128], BF16), ("fqa", [8, 128, 1024]),
                   ("cmask", [128, 40]),
                   ("x2fm", [128, 8, TEX]),
                   ("w1T", [32, 128, 8, 128], BF16), ("w2T", [32, 128, 8, 128], BF16),
                   ("h1T", [32, 128, 32, 128], BF16),
                   ("h2T", [32, 128, 32, 128], BF16),
                   ("w3T", [8, 128, 32, 128], BF16), ("w1s", [128, 32]),
                   ("w2s", [128, 32]), ("h1s", [128, 32]), ("h2s", [128, 32]),
                   ("w3s", [128, 8]),
                   ("wqT", [8, 128, 8, 128], BF16), ("wkT", [8, 128, 8, 128], BF16),
                   ("wvT", [8, 128, 8, 128], BF16), ("woT", [8, 128, 8, 128], BF16),
                   ("bq", [128, 8]),
                   ("bk", [128, 8]), ("bvt", [128, D]), ("bo_", [128, 8]),
                   ("mask1", [16, 128, 128]), ("mask2", [16, 64, 128]),
                   ("gate_w", [128, 8, NEXP], BF16), ("gbt", [128, NEXP]),
                   ("w1e", [NEXP, 32, 128, 8, 128], BF16), ("b1e", [128, NEXP, 32]),
                   ("w2e", [NEXP, 8, 128, 32, 128], BF16), ("b2e", [128, NEXP, 8]),
                   ("ws1", [32, 128, 8, 128], BF16), ("bs1", [128, 32]),
                   ("ws2", [8, 128, 32, 128], BF16),
                   ("bs2", [128, 8]), ("ident", [128, 128])]:
        t[nm] = EI(nm, sh, dt[0] if dt else F32)
    t["xn_ext"] = nc.dram_tensor("xn_ext", [128, 8, TEX], BF16,
                                 kind="ExternalInput").ap()
    ID = lambda nm, sh, dt=F32: nc.dram_tensor(nm, sh, dt)
    mam_d = ID("mam_d", [128, 8, TEX])
    x1e_d = ID("x1e_d", [128, 8, TEX])
    xn1_d = ID("xn1_d", [128, 8, T], BF16)
    at_d = ID("at_d", [128, 8, T])
    nrm_d = ID("nrm_d", [2, 16, TEX])
    srow_d = ID("srow_d", [16, T])
    comb_d = ID("comb_d", [NEXP, T])
    x2_out = nc.dram_tensor("x2_out", [128, 8, T], F32, kind="ExternalOutput").ap()

    S = mk_streams(P)
    tB = dict(t, mam_out=mam_d.ap())
    with P.scope("B"):
        emit_B2(P, S, tB)
    P.barrier()
    tC = dict(t, xn_in=t["xn_ext"], x_in=t["x2fm"], mam_in=mam_d.ap(),
              x1_out=x1e_d.ap())
    with P.scope("C"):
        emit_C(P, S, tC)
    P.barrier()
    tD1 = dict(t, x1e=x1e_d.ap(), nrm_d=nrm_d, srow_d=srow_d,
               xn1_out=xn1_d.ap(), at_out=at_d.ap())
    with P.scope("E"):
        emit_D1(P, S, tD1)
    P.barrier()
    tD2 = dict(t, xn1_in=xn1_d.ap(), x1_in=x1e_d.ap()[:, :, HLO:HLO + T],
               at_in=at_d.ap(), comb_d=comb_d, x2_out=x2_out)
    with P.scope("F"):
        emit_D2(P, S, tD2)
    return P.finish()


def _maps_A(inp):
    x = inp["L0_attn_x"]
    alog_l = np.ascontiguousarray(
        inp["L0_attn_A_log"].reshape(16, 128, 16).transpose(1, 0, 2).reshape(128, 256))
    cw = np.ascontiguousarray(
        inp["L0_attn_conv_w"][:, 0, :].reshape(4, 16, 128).transpose(2, 1, 0).reshape(128, 64))
    common = dict(win=inp["L0_attn_W_in"], convw=cw,
                  convb=_cols(inp["L0_attn_conv_b"]), wx=inp["L0_attn_W_x"],
                  wdt=inp["L0_attn_W_dt"], bdt=_cols(inp["L0_attn_b_dt"]), alog=alog_l)
    maps = []
    for c in range(NCORES):
        b, q = c // 4, c % 4
        t0 = q * T
        xe = np.zeros((T + 3, D), np.float32)
        xe[3:] = x[b, t0:t0 + T]
        if q > 0:
            xe[0:3] = x[b, t0 - 3:t0]
        maps.append(dict(common, xfm=_fm(np.ascontiguousarray(xe.T))))
    return maps, alog_l


def _maps_B(inp, a_res, alog_l):
    fqa = np.stack([a_res[c]["fq_out"] for c in range(NCORES)])
    maps = []
    for c in range(NCORES):
        b = c // 4
        m = np.zeros(8, np.float32)
        m[4 * b: c] = 1.0
        cm = np.zeros((128, 16), np.float32)
        cm[:, 0:8] = m
        cm[:, 8:16] = 1.0 - m
        maps.append(dict(u_in=a_res[c]["xconv_out"], dl_in=a_res[c]["delta_out"],
                         bc_in=a_res[c]["bc_out"], res_in=a_res[c]["res_out"],
                         alog=alog_l, dvec=_cols(inp["L0_attn_D"]),
                         wout=inp["L0_attn_W_out"], fqa=fqa, cmask=cm))
    return maps


def _maps_C(inp, x, a_res, b_res):
    tr = lambda a: np.ascontiguousarray(a.T)
    maps = []
    for c in range(NCORES):
        b, q = c // 4, c % 4
        maps.append(dict(
            xn_in=a_res[c]["xn_out"], mam_in=b_res[c]["mam_out"],
            x_in=_fm(np.ascontiguousarray(x[b, q * T:(q + 1) * T].T)),
            w1T=tr(inp["L0_ffn_w1_w"]), w2T=tr(inp["L0_ffn_w2_w"]),
            h1T=tr(inp["L0_ffn_h1_w"]), h2T=tr(inp["L0_ffn_h2_w"]),
            w3T=tr(inp["L0_ffn_w3_w"]),
            w1s=_cols(inp["L0_ffn_w1_s"]), w2s=_cols(inp["L0_ffn_w2_s"]),
            h1s=_cols(inp["L0_ffn_h1_s"]), h2s=_cols(inp["L0_ffn_h2_s"]),
            w3s=_cols(inp["L0_ffn_w3_s"])))
    return maps


def _unfm(a):
    return np.ascontiguousarray(np.asarray(a).transpose(1, 0, 2).reshape(-1, a.shape[2]))


def _maps_D1(inp, c_res):
    x1 = {c: _unfm(c_res[c]["x1_out"]) for c in range(NCORES)}
    tr = lambda a: np.ascontiguousarray(a.T)
    maps = []
    tk = np.arange(192)[:, None]
    tq = np.arange(128)[None, :]
    dd = tk - tq - 32
    band = (np.abs(dd) <= 32) & (dd % 4 == 0)
    for c in range(NCORES):
        b, q = c // 4, c % 4
        xe = np.zeros((D, T + 2 * HLO), np.float32)
        xe[:, HLO:HLO + T] = x1[c]
        if q > 0:
            xe[:, 0:HLO] = x1[c - 1][:, T - HLO:]
        if q < 3:
            xe[:, HLO + T:] = x1[c + 1][:, 0:HLO]
        m = np.zeros((16, 192, 128), np.float32)
        for qt in range(16):
            xext = qt * 128 + np.arange(192)
            valid = np.ones(192, bool)
            if q == 0:
                valid &= xext >= HLO
            if q == 3:
                valid &= xext < HLO + T
            m[qt] = band & valid[:, None]
        maps.append(dict(
            x1e=_fm(xe),
            wqT=st(tr(inp["L1_attn_Wq"]), 8, 8), wkT=st(tr(inp["L1_attn_Wk"]), 8, 8),
            wvT=st(tr(inp["L1_attn_Wv"]), 8, 8), woT=st(tr(inp["L1_attn_Wo"]), 8, 8),
            bq=_cols(inp["L1_attn_bq"]), bk=_cols(inp["L1_attn_bk"]),
            bvt=np.ascontiguousarray(np.tile(inp["L1_attn_bv"], (128, 1))),
            bo_=_cols(inp["L1_attn_bo"]),
            mask1=np.ascontiguousarray(m[:, 0:128, :]),
            mask2=np.ascontiguousarray(m[:, 128:192, :])))
    return maps


def _maps_D2(inp, c_res, d1_res):
    b1c = np.ascontiguousarray(inp["L1_ffn_b1"].reshape(8, 32, 128).transpose(2, 0, 1))
    b2c = np.ascontiguousarray(inp["L1_ffn_b2"].reshape(8, 8, 128).transpose(2, 0, 1))
    maps = []
    for c in range(NCORES):
        maps.append(dict(
            xn1_in=d1_res[c]["xn1_out"], x1_in=c_res[c]["x1_out"],
            at_in=d1_res[c]["at_out"],
            gate_w=np.ascontiguousarray(
                inp["L1_ffn_gate_w"].reshape(8, 128, NEXP).transpose(1, 0, 2)).astype(BFH),
            gbt=np.ascontiguousarray(np.tile(inp["L1_ffn_gate_b"], (128, 1))),
            w1e=np.ascontiguousarray(
                inp["L1_ffn_w1"].reshape(NEXP, 8, 128, 32, 128).transpose(0, 3, 2, 1, 4)).astype(BFH),
            b1e=b1c,
            w2e=np.ascontiguousarray(
                inp["L1_ffn_w2"].reshape(NEXP, 32, 128, 8, 128).transpose(0, 3, 2, 1, 4)).astype(BFH),
            b2e=b2c,
            ws1=st(inp["L1_ffn_ws1"], 8, 32), bs1=_cols(inp["L1_ffn_bs1"]),
            ws2=st(inp["L1_ffn_ws2"], 32, 8), bs2=_cols(inp["L1_ffn_bs2"]),
            ident=np.eye(128, dtype=np.float32)))
    return maps


def _maps_L2(inp, x, a_res, alog_l):
    import ml_dtypes
    BFH = ml_dtypes.bfloat16
    tr = lambda a: np.ascontiguousarray(a.T)
    st = lambda wT, KT, NO: np.ascontiguousarray(
        wT.reshape(KT, 128, NO, 128).transpose(2, 1, 0, 3)).astype(BFH)
    fqa = np.stack([a_res[c]["fq_out"] for c in range(NCORES)])
    b1c = np.ascontiguousarray(inp["L1_ffn_b1"].reshape(8, 32, 128).transpose(2, 0, 1))
    b2c = np.ascontiguousarray(inp["L1_ffn_b2"].reshape(8, 8, 128).transpose(2, 0, 1))
    tk = np.arange(192)[:, None]
    tq = np.arange(128)[None, :]
    dd = tk - tq - 32
    band = (np.abs(dd) <= 32) & (dd % 4 == 0)
    TEX = T + 2 * HLO
    maps = []
    for c in range(NCORES):
        b, q = c // 4, c % 4
        h1 = np.zeros(8, np.float32)
        h2 = np.zeros(8, np.float32)
        if q > 0:
            h1[4 * b: c - 1] = 1.0
            h2[c - 1] = 1.0
        cm = np.zeros((128, 40), np.float32)
        cm[:, 16:24] = h1
        cm[:, 24:32] = h2
        cm[:, 32:40] = 1.0 - h1 - h2
        def ext_of(key, P_, F_=16):
            e = np.zeros((128, F_, 64), np.float32)
            if q > 0:
                e[:, :, 0:32] = a_res[c - 1][key][:, :, T - 32:]
            if q < 3:
                e[:, :, 32:64] = a_res[c + 1][key][:, :, 0:32]
            return e
        uex = ext_of("xconv_out", None)
        dlex = ext_of("delta_out", None)
        resex = ext_of("res_out", None)
        bcex = np.zeros((32, 64), np.float32)
        if q > 0:
            bcex[:, 0:32] = a_res[c - 1]["bc_out"][:, T - 32:]
        if q < 3:
            bcex[:, 32:64] = a_res[c + 1]["bc_out"][:, 0:32]
        t0 = q * T
        xe = np.zeros((D, TEX), np.float32)
        xe[:, HLO:HLO + T] = x[b, t0:t0 + T].T
        if q > 0:
            xe[:, 0:HLO] = x[b, t0 - HLO:t0].T
        if q < 3:
            xe[:, HLO + T:] = x[b, t0 + T:t0 + T + HLO].T
        import ml_dtypes
        xne = np.zeros((128, 8, TEX), dtype=np.asarray(a_res[c]["xn_out"]).dtype)
        xne[:, :, HLO:HLO + T] = a_res[c]["xn_out"]
        if q > 0:
            xne[:, :, 0:HLO] = a_res[c - 1]["xn_out"][:, :, T - HLO:]
        if q < 3:
            xne[:, :, HLO + T:] = a_res[c + 1]["xn_out"][:, :, 0:HLO]
        m = np.zeros((16, 192, 128), np.float32)
        for qt in range(16):
            xext = qt * 128 + np.arange(192)
            valid = np.ones(192, bool)
            if q == 0:
                valid &= xext >= HLO
            if q == 3:
                valid &= xext < HLO + T
            m[qt] = band & valid[:, None]
        maps.append(dict(
            u_in=a_res[c]["xconv_out"], dl_in=a_res[c]["delta_out"],
            res_in=a_res[c]["res_out"], bc_in=a_res[c]["bc_out"],
            uex=uex, dlex=dlex, resex=resex, bcex=bcex,
            alog=alog_l, dvec=_cols(inp["L0_attn_D"]),
            wout=st(inp["L0_attn_W_out"], 16, 8),
            fqa=fqa, cmask=cm, x2fm=_fm(xe), xn_ext=xne,
            w1T=st(tr(inp["L0_ffn_w1_w"]), 8, 32), w2T=st(tr(inp["L0_ffn_w2_w"]), 8, 32),
            h1T=st(tr(inp["L0_ffn_h1_w"]), 32, 32), h2T=st(tr(inp["L0_ffn_h2_w"]), 32, 32),
            w3T=st(tr(inp["L0_ffn_w3_w"]), 32, 8),
            w1s=_cols(inp["L0_ffn_w1_s"]), w2s=_cols(inp["L0_ffn_w2_s"]),
            h1s=_cols(inp["L0_ffn_h1_s"]), h2s=_cols(inp["L0_ffn_h2_s"]),
            w3s=_cols(inp["L0_ffn_w3_s"]),
            wqT=st(tr(inp["L1_attn_Wq"]), 8, 8), wkT=st(tr(inp["L1_attn_Wk"]), 8, 8),
            wvT=st(tr(inp["L1_attn_Wv"]), 8, 8), woT=st(tr(inp["L1_attn_Wo"]), 8, 8),
            bq=_cols(inp["L1_attn_bq"]), bk=_cols(inp["L1_attn_bk"]),
            bvt=np.ascontiguousarray(np.tile(inp["L1_attn_bv"], (128, 1))),
            bo_=_cols(inp["L1_attn_bo"]),
            mask1=np.ascontiguousarray(m[:, 0:128, :]),
            mask2=np.ascontiguousarray(m[:, 128:192, :]),
            gate_w=np.ascontiguousarray(
                inp["L1_ffn_gate_w"].reshape(8, 128, NEXP).transpose(1, 0, 2)).astype(BFH),
            gbt=np.ascontiguousarray(np.tile(inp["L1_ffn_gate_b"], (128, 1))),
            w1e=np.ascontiguousarray(
                inp["L1_ffn_w1"].reshape(NEXP, 8, 128, 32, 128).transpose(0, 3, 2, 1, 4)).astype(BFH),
            b1e=b1c,
            w2e=np.ascontiguousarray(
                inp["L1_ffn_w2"].reshape(NEXP, 32, 128, 8, 128).transpose(0, 3, 2, 1, 4)).astype(BFH),
            b2e=b2c,
            ws1=st(inp["L1_ffn_ws1"], 8, 32), bs1=_cols(inp["L1_ffn_bs1"]),
            ws2=st(inp["L1_ffn_ws2"], 32, 8), bs2=_cols(inp["L1_ffn_bs2"]),
            ident=np.eye(128, dtype=np.float32)))
    return maps


def kernel(**inp):
    inp = {k: np.asarray(v) for k, v in inp.items()}
    x = inp["L0_attn_x"]
    cores = list(range(NCORES))

    maps, alog_l = _maps_A(inp)
    a_res = run_bass_kernel_spmd(build_launchA(), maps, core_ids=cores).results
    l2_res = run_bass_kernel_spmd(build_launch2(), _maps_L2(inp, x, a_res, alog_l),
                                  core_ids=cores).results
    out = np.zeros((2, 4 * T, D), np.float32)
    for c in range(NCORES):
        b, q = c // 4, c % 4
        out[b, q * T:(q + 1) * T] = _unfm(l2_res[c]["x2_out"]).T
    return out
